# version 07
# baseline (speedup 1.0000x reference)
import numpy as np
import ml_dtypes
from contextlib import ExitStack

import concourse.bass as bass
import concourse.mybir as mybir
from concourse.bass_utils import run_bass_kernel_spmd
from concourse.aot_env import allow_ici_all_to_all

F32 = mybir.dt.float32
BF16 = mybir.dt.bfloat16
I32 = mybir.dt.int32
U32 = mybir.dt.uint32
AF = mybir.ActivationFunctionType
ALU = mybir.AluOpType
AX = mybir.AxisListType

D = 2048
S = 4096
B = 2
H_LOC = 4
DK = 128
E_LOC = H_LOC * DK
TOK = 1024
NDC = D // 128
NTT = S // 512
NKC = S // 128
NQT = S // 512
NOC = D // 128
NTC = TOK // 512

EPS = 1e-5
SCL_ATTN = float(1.0 / np.sqrt(np.float32(DK)))
MAGIC = float(np.float32(3 * 2 ** 22))
QMAX_INV = float(np.float32(1.0 / 127.0))

ENGINES = ("sync", "tensor", "vector", "scalar", "gpsimd")


class Prog:
    def __init__(self):
        self.ops = {e: [] for e in ENGINES}
        self.cnt = {}

    def new_sem(self, name):
        self.cnt[name] = 0

    def raw(self, eng, fn):
        self.ops[eng].append(("raw", fn, None, 0))

    def op(self, eng, fn, inc_sem=None, inc_by=1):
        if inc_sem is not None:
            self.cnt[inc_sem] += inc_by
        self.ops[eng].append(("op", fn, inc_sem, inc_by))

    def dma(self, eng, fn, sem):
        self.op(eng, fn, inc_sem=sem, inc_by=16)

    def wait(self, eng, sem, val=None):
        v = self.cnt[sem] if val is None else val
        self.ops[eng].append(("wait", None, sem, v))

    def drain(self, eng):
        self.ops[eng].append(("drain", None, None, 0))

    def count(self, sem):
        return self.cnt[sem]

    def replay(self, eng_name, eng_handle, sems):
        for kind, fn, sem, val in self.ops[eng_name]:
            if kind == "wait":
                eng_handle.wait_ge(sems[sem], val)
            elif kind == "drain":
                eng_handle.drain()
            elif kind == "raw":
                fn(eng_handle, sems)
            else:
                inst = fn(eng_handle, sems)
                if sem is not None:
                    inst.then_inc(sems[sem], val)


def build_nc():
    nc = bass.Bass("TRN2", target_bir_lowering=False, debug=False)

    XT = nc.dram_tensor("XT", [NTT, NDC, 128, 512], F32, kind="ExternalInput").ap()
    XTM = nc.dram_tensor("XTM", [D, TOK], F32, kind="ExternalInput").ap()
    XM = nc.dram_tensor("XM", [TOK, D], F32, kind="ExternalInput").ap()
    WQ = nc.dram_tensor("WQ", [D, E_LOC], F32, kind="ExternalInput").ap()
    WK = nc.dram_tensor("WK", [D, E_LOC], F32, kind="ExternalInput").ap()
    WV = nc.dram_tensor("WV", [D, E_LOC], F32, kind="ExternalInput").ap()
    WOT = nc.dram_tensor("WOT", [NOC, 128, NDC, 128], F32, kind="ExternalInput").ap()
    FW = nc.dram_tensor("FW", [D, D], F32, kind="ExternalInput").ap()
    FWT = nc.dram_tensor("FWT", [D, D], F32, kind="ExternalInput").ap()
    FB = nc.dram_tensor("FB", [D], F32, kind="ExternalInput").ap()
    COFF = nc.dram_tensor("COFF", [1, 1], U32, kind="ExternalInput").ap()
    OUT = nc.dram_tensor("OUT", [D, TOK], F32, kind="ExternalOutput").ap()

    xc_spill = nc.dram_tensor("xc_spill", [NTT, 128, NDC * 512], BF16).ap()
    wq_spill = nc.dram_tensor("wq_spill", [D, D], BF16).ap()
    xq_spill = nc.dram_tensor("xq_spill", [NTC, 128, NDC * 512], BF16).ap()
    g_spill = nc.dram_tensor("g_spill", [D, TOK], F32).ap()
    m_row = nc.dram_tensor("m_row", [S], BF16).ap()
    rs_row = nc.dram_tensor("rs_row", [S], F32).ap()
    rws_row = nc.dram_tensor("rws_row", [D], F32).ap()
    qs_row = nc.dram_tensor("qs_row", [TOK], F32).ap()
    xs_row = nc.dram_tensor("xs_row", [TOK], F32).ap()
    mq_row = nc.dram_tensor("mq_row", [TOK], BF16).ap()
    cst_in = nc.dram_tensor("cst_in", [2, TOK], F32).ap()
    cst_out = nc.dram_tensor("cst_out", [8, TOK], F32).ap()
    cc_in = nc.dram_tensor("cc_in", [E_LOC, S], BF16).ap()
    cc_out1_t = nc.dram_tensor("cc_out1", [4 * 256, S], BF16)
    cc_out1 = cc_out1_t.ap()
    cc_out2_t = nc.dram_tensor("cc_out2", [4 * 256, S], BF16)
    cc_out2 = cc_out2_t.ap()

    es = ExitStack()
    SB = lambda name, shape, dt: es.enter_context(nc.sbuf_tensor(name, shape, dt))

    KT = SB("KT", [128, H_LOC * S], BF16)
    QT = SB("QT", [128, H_LOC * S], BF16)
    VT = SB("VT", [128, NKC * E_LOC], BF16)
    XQR = SB("XQR", [128, 2 * NDC * 512], BF16)
    WA = SB("WA", [128, NDC * E_LOC], BF16)
    WB = SB("WB", [128, NDC * E_LOC], BF16)
    F32B = SB("F32B", [128, 1024], F32)
    QS2 = SB("QS2", [128, 2 * 1024], F32)
    F32A = QS2
    XTG = SB("XTG", [128, 2 * 1024], F32)
    BRS = SB("BRS", [128, 2 * 512], F32)
    BMT = SB("BMT", [128, 2 * 512], BF16)
    BQS = SB("BQS", [128, TOK], F32)
    BMQ = SB("BMQ", [128, TOK], BF16)
    BRWS = SB("BRWS", [128, 2048], F32)
    PBUF = SB("PBUF", [128, 2048], BF16)
    ONES = SB("ONES", [128, 128], BF16)
    SUMC = SB("SUMC", [128, 8], F32)
    SQC = SB("SQC", [128, 8], F32)
    MMC = SB("MMC", [128, 8], F32)
    RSM = SB("RSM", [128, 8], F32)
    AXC = SB("AXC", [128, 8], F32)
    XSC = SB("XSC", [128, 8], F32)
    QSC = SB("QSC", [128, 8], F32)
    GSUM = SB("GSUM", [128, 32], F32)
    GSQ = SB("GSQ", [128, 32], F32)
    GM = SB("GM", [128, 32], F32)
    GMB = SB("GMB", [128, 32], BF16)
    GRS = SB("GRS", [128, 32], F32)
    GT0 = SB("GT0", [128, 32], F32)
    WMX = SB("WMX", [128, 16], F32)
    WSC = SB("WSC", [128, 16], F32)
    WRC = SB("WRC", [128, 16], F32)
    BIASC = SB("BIASC", [128, 16], F32)
    MQB = SB("MQB", [128, 8], BF16)
    PSAC = BMT
    AOB = BMQ
    SOFF = SB("SOFF", [1, 1], U32)

    scp = [es.enter_context(nc.psum_tensor(f"scp{i}", [128, 1024], F32)) for i in range(2)]
    pvb_t = [es.enter_context(nc.psum_tensor(f"pv{i}", [128, 512], F32)) for i in range(2)]
    ffb = [es.enter_context(nc.psum_tensor(f"ffn{i}", [128, 512], F32)) for i in range(2)]

    sem_names = [
        "t", "v", "s", "cc1", "cc2", "cc2b",
        "d_xm0", "d_st",
        "d_bc1", "d_bc2", "d_bc3",
        "d_xt0", "d_xt1",
        "d_sp0", "d_sp1",
        "d_xc0", "d_xc1",
        "d_fw", "d_fwt", "d_wqs", "d_xqs", "d_xtm",
        "d_q0", "d_q1",
        "d_brt0", "d_brt1",
        "d_wq0", "d_wq1", "d_wq2", "d_wq3",
        "d_xq0", "d_xq1",
        "d_ao0", "d_ao1",
        "d_gs", "d_rb",
        "d_g0", "d_g1", "d_xe0", "d_xe1",
        "d_out0", "d_out1",
    ]
    P = Prog()
    for n in sem_names:
        P.new_sem(n)
    mark = {}

    P.op("vector", lambda e, sm: e.memset(ONES[:, :], 1.0), "v")
    v_ones = P.count("v")

    P.dma("gpsimd", lambda e, sm: e.dma_start(
        out=BIASC[:, :], in_=FB.rearrange("(j p) -> p j", p=128),
        allow_slow_non_contiguous=True), "d_bc1")

    q1cnt = {}
    def _xm1_load(ih):
        i, hf = divmod(ih, 2)
        slot = ih % 2
        semn = "d_q0" if slot == 0 else "d_q1"
        pv = mark.get(("xm1v", ih - 2))
        if pv is not None:
            P.wait("scalar", "v", pv)
            P.wait("scalar", "s", mark[("xm1s", ih - 2)])
        P.dma("scalar", lambda e, sm, i=i, hf=hf, slot=slot: e.dma_start(
            out=QS2[:, 1024 * slot:1024 * (slot + 1)],
            in_=XM[128 * i:128 * (i + 1), 1024 * hf:1024 * (hf + 1)]), semn)
        q1cnt[ih] = P.count(semn)

    _xm1_load(0)
    _xm1_load(1)
    for ih in range(16):
        i, hf = divmod(ih, 2)
        slot = ih % 2
        semn = "d_q0" if slot == 0 else "d_q1"
        P.wait("vector", semn, q1cnt[ih])
        P.drain("vector")
        P.op("vector", lambda e, sm, slot=slot, hf=hf: e.tensor_reduce(
            GT0[:, 2 * hf:2 * hf + 1], QS2[:, 1024 * slot:1024 * (slot + 1)], AX.X, ALU.add), "v")
        P.wait("scalar", semn, q1cnt[ih])
        if ih > 0:
            P.wait("scalar", "v", mark[("xm1v", ih - 1)])
        P.op("scalar", lambda e, sm, slot=slot: e.square(
            PBUF[:, 0:1024], QS2[:, 1024 * slot:1024 * (slot + 1)]), "s")
        mark[("xm1s", ih)] = P.count("s")
        P.wait("vector", "s", mark[("xm1s", ih)])
        P.drain("vector")
        P.op("vector", lambda e, sm, hf=hf: e.tensor_reduce(
            GT0[:, 2 * hf + 1:2 * hf + 2], PBUF[:, 0:1024], AX.X, ALU.add), "v")
        if hf == 1:
            P.drain("vector")
            P.op("vector", lambda e, sm, i=i: e.tensor_tensor(
                SUMC[:, i:i + 1], GT0[:, 0:1], GT0[:, 2:3], ALU.add), "v")
            P.drain("vector")
            P.op("vector", lambda e, sm, i=i: e.tensor_tensor(
                SQC[:, i:i + 1], GT0[:, 1:2], GT0[:, 3:4], ALU.add), "v")
        mark[("xm1v", ih)] = P.count("v")
        if ih + 2 < 16:
            _xm1_load(ih + 2)
    mark[("xmred", 7)] = mark[("xm1v", 15)]
    mark[("xmsq", 7)] = mark[("xm1s", 15)]

    P.wait("gpsimd", "v", mark[("xmred", 7)])
    P.dma("gpsimd", lambda e, sm: e.dma_start(
        out=cst_in[0:1, :].rearrange("o (j p) -> p (o j)", p=128),
        in_=SUMC[:, :], allow_slow_non_contiguous=True), "d_st")
    P.dma("gpsimd", lambda e, sm: e.dma_start(
        out=cst_in[1:2, :].rearrange("o (j p) -> p (o j)", p=128),
        in_=SQC[:, :], allow_slow_non_contiguous=True), "d_st")
    P.wait("gpsimd", "d_st")
    P.op("gpsimd", lambda e, sm: e.collective_compute(
        "AllGather", ALU.bypass, replica_groups=[[0, 1, 2, 3], [4, 5, 6, 7]],
        ins=[cst_in.opt()], outs=[cst_out.opt()]), "cc1")
    P.wait("gpsimd", "cc1")
    for i in range(4):
        P.dma("gpsimd", lambda e, sm, i=i: e.dma_start(
            out=GSUM[:, 8 * i:8 * (i + 1)],
            in_=cst_out[2 * i:2 * i + 1, :].rearrange("o (j p) -> p (o j)", p=128),
            allow_slow_non_contiguous=True), "d_st")
        P.dma("gpsimd", lambda e, sm, i=i: e.dma_start(
            out=GSQ[:, 8 * i:8 * (i + 1)],
            in_=cst_out[2 * i + 1:2 * i + 2, :].rearrange("o (j p) -> p (o j)", p=128),
            allow_slow_non_contiguous=True), "d_st")
    P.wait("vector", "d_st")
    P.op("vector", lambda e, sm: e.tensor_scalar(GM[:, :], GSUM[:, :], 1.0 / D, None, ALU.mult), "v")
    P.drain("vector")
    P.op("vector", lambda e, sm: e.tensor_mul(GT0[:, :], GM[:, :], GM[:, :]), "v")
    P.drain("vector")
    P.op("vector", lambda e, sm: e.tensor_scalar(GRS[:, :], GSQ[:, :], 1.0 / D, None, ALU.mult), "v")
    P.drain("vector")
    P.op("vector", lambda e, sm: e.tensor_sub(GRS[:, :], GRS[:, :], GT0[:, :]), "v")
    P.drain("vector")
    P.op("vector", lambda e, sm: e.tensor_scalar(GRS[:, :], GRS[:, :], EPS, None, ALU.add), "v")
    mark["grs_pre"] = P.count("v")
    P.wait("scalar", "v", mark["grs_pre"])
    P.op("scalar", lambda e, sm: e.sqrt(GT0[:, :], GRS[:, :]), "s")
    mark["grs_sqrt"] = P.count("s")
    P.wait("vector", "s", mark["grs_sqrt"])
    P.op("vector", lambda e, sm: e.reciprocal(GRS[:, :], GT0[:, :]), "v")
    P.drain("vector")
    P.op("vector", lambda e, sm: e.tensor_copy(GMB[:, :], GM[:, :]), "v")
    mark["gstats"] = P.count("v")
    P.wait("gpsimd", "v", mark["gstats"])
    P.dma("gpsimd", lambda e, sm: e.dma_start(
        out=m_row.rearrange("(j p) -> p j", p=128), in_=GMB[:, :],
        allow_slow_non_contiguous=True), "d_st")
    P.dma("gpsimd", lambda e, sm: e.dma_start(
        out=rs_row.rearrange("(j p) -> p j", p=128), in_=GRS[:, :],
        allow_slow_non_contiguous=True), "d_st")
    mark["rows_mrs"] = P.count("d_st")
    P.wait("gpsimd", "d_st")

    P.op("vector", lambda e, sm: e.tensor_scalar(MMC[:, :], SUMC[:, :], 1.0 / D, None, ALU.mult), "v")
    P.drain("vector")
    P.op("vector", lambda e, sm: e.tensor_mul(GT0[:, 0:8], MMC[:, :], MMC[:, :]), "v")
    P.drain("vector")
    P.op("vector", lambda e, sm: e.tensor_scalar(RSM[:, :], SQC[:, :], 1.0 / D, None, ALU.mult), "v")
    P.drain("vector")
    P.op("vector", lambda e, sm: e.tensor_sub(RSM[:, :], RSM[:, :], GT0[:, 0:8]), "v")
    P.drain("vector")
    P.op("vector", lambda e, sm: e.tensor_scalar(RSM[:, :], RSM[:, :], EPS, None, ALU.add), "v")
    mark["rsm_pre"] = P.count("v")
    P.wait("scalar", "v", mark["rsm_pre"])
    P.op("scalar", lambda e, sm: e.sqrt(GT0[:, 0:8], RSM[:, :]), "s")
    mark["rsm_sqrt"] = P.count("s")
    P.wait("vector", "s", mark["rsm_sqrt"])
    P.op("vector", lambda e, sm: e.reciprocal(RSM[:, :], GT0[:, 0:8]), "v")
    P.drain("vector")
    P.op("vector", lambda e, sm: e.tensor_copy(MQB[:, :], MMC[:, :]), "v")
    mark["minestats"] = P.count("v")

    QITEMS = []
    QLI = []

    def q_add(load, comp):
        QLI.append(None if load is None else sum(1 for x in QITEMS if x[0] is not None))
        QITEMS.append((load, comp))

    qcnt = {}
    qvmark = {}
    q_li_item = {}
    qemit = [0]
    qcp = [0]

    def _q_emit_load(k):
        load = QITEMS[k][0]
        if load is None:
            return
        li = QLI[k]
        q_li_item[li] = k
        slot = li % 2
        semn = "d_q0" if slot == 0 else "d_q1"
        prev_item = q_li_item.get(li - 2)
        if prev_item is not None:
            P.wait("scalar", "v", qvmark[prev_item])
        load(slot, semn)
        qcnt[k] = P.count(semn)

    def pump(n):
        for _ in range(n):
            k = qcp[0]
            if k >= len(QITEMS):
                return
            while qemit[0] < min(k + 2, len(QITEMS)):
                _q_emit_load(qemit[0])
                qemit[0] += 1
            load, comp = QITEMS[k]
            if load is not None:
                slot = QLI[k] % 2
                semn = "d_q0" if slot == 0 else "d_q1"
                P.wait("vector", semn, qcnt[k])
                comp(slot, semn)
            else:
                comp(None, None)
            qvmark[k] = P.count("v")
            qcp[0] += 1

    def pump_all():
        pump(len(QITEMS) - qcp[0])

    def q_xm2(i, hf):
        def load(slot, semn, i=i, hf=hf):
            if i == 0 and hf == 0:
                P.wait("scalar", "v", mark["minestats"])
            P.dma("scalar", lambda e, sm, i=i, hf=hf, slot=slot: e.dma_start(
                out=QS2[:, 1024 * slot:1024 * (slot + 1)],
                in_=XM[128 * i:128 * (i + 1), 1024 * hf:1024 * (hf + 1)]), semn)

        def comp(slot, semn, i=i, hf=hf):
            P.drain("vector")
            P.op("vector", lambda e, sm, i=i, slot=slot: e.tensor_scalar(
                QS2[:, 1024 * slot:1024 * (slot + 1)], QS2[:, 1024 * slot:1024 * (slot + 1)],
                MMC[:, i:i + 1], None, ALU.subtract), "v")
            P.drain("vector")
            P.op("vector", lambda e, sm, hf=hf, slot=slot: e.tensor_reduce(
                GT0[:, 8 + hf:9 + hf], QS2[:, 1024 * slot:1024 * (slot + 1)],
                AX.X, ALU.max, apply_absolute_value=True), "v")
            if hf == 1:
                P.drain("vector")
                P.op("vector", lambda e, sm, i=i: e.tensor_tensor(
                    AXC[:, i:i + 1], GT0[:, 8:9], GT0[:, 9:10], ALU.max), "v")
                mark[("axred", i)] = P.count("v")
        return load, comp

    def q_xscales():
        def comp(slot, semn):
            P.drain("vector")
            P.op("vector", lambda e, sm: e.tensor_mul(XSC[:, :], AXC[:, :], RSM[:, :]), "v")
            P.drain("vector")
            P.op("vector", lambda e, sm: e.tensor_scalar(
                XSC[:, :], XSC[:, :], QMAX_INV, 1e-12, ALU.mult, ALU.max), "v")
            P.drain("vector")
            P.op("vector", lambda e, sm: e.reciprocal(QSC[:, :], XSC[:, :]), "v")
            P.drain("vector")
            P.op("vector", lambda e, sm: e.tensor_mul(QSC[:, :], QSC[:, :], RSM[:, :]), "v")
            mark["qscales"] = P.count("v")
            P.wait("gpsimd", "v", mark["qscales"])
            P.dma("gpsimd", lambda e, sm: e.dma_start(
                out=qs_row.rearrange("(j p) -> p j", p=128), in_=QSC[:, :],
                allow_slow_non_contiguous=True), "d_st")
            P.dma("gpsimd", lambda e, sm: e.dma_start(
                out=xs_row.rearrange("(j p) -> p j", p=128), in_=XSC[:, :],
                allow_slow_non_contiguous=True), "d_st")
            P.dma("gpsimd", lambda e, sm: e.dma_start(
                out=mq_row.rearrange("(j p) -> p j", p=128), in_=MQB[:, :],
                allow_slow_non_contiguous=True), "d_st")
            P.wait("gpsimd", "d_st")
            P.dma("gpsimd", lambda e, sm: e.dma_start(
                out=BQS[:, :], in_=qs_row.rearrange("(o a) -> o a", o=1).broadcast_to([128, TOK])), "d_bc1")
            P.dma("gpsimd", lambda e, sm: e.dma_start(
                out=BMQ[:, :], in_=mq_row.rearrange("(o a) -> o a", o=1).broadcast_to([128, TOK])), "d_bc1")
            mark["bq_ready"] = P.count("d_bc1")
        return None, comp

    def q_fw(i, hf):
        def load(slot, semn, i=i, hf=hf):
            P.dma("scalar", lambda e, sm, i=i, hf=hf, slot=slot: e.dma_start(
                out=QS2[:, 1024 * slot:1024 * (slot + 1)],
                in_=FW[128 * i:128 * (i + 1), 1024 * hf:1024 * (hf + 1)]), semn)

        def comp(slot, semn, i=i, hf=hf):
            P.drain("vector")
            P.op("vector", lambda e, sm, hf=hf, slot=slot: e.tensor_reduce(
                GT0[:, 8 + hf:9 + hf], QS2[:, 1024 * slot:1024 * (slot + 1)],
                AX.X, ALU.max, apply_absolute_value=True), "v")
            if hf == 1:
                P.drain("vector")
                P.op("vector", lambda e, sm, i=i: e.tensor_tensor(
                    WMX[:, i:i + 1], GT0[:, 8:9], GT0[:, 9:10], ALU.max), "v")
                mark[("fwred", i)] = P.count("v")
        return load, comp

    def q_wscales():
        def comp(slot, semn):
            P.drain("vector")
            P.op("vector", lambda e, sm: e.tensor_scalar(
                WSC[:, :], WMX[:, :], 1e-8, QMAX_INV, ALU.max, ALU.mult), "v")
            P.drain("vector")
            P.op("vector", lambda e, sm: e.reciprocal(WRC[:, :], WSC[:, :]), "v")
            mark["wscales"] = P.count("v")
            P.wait("gpsimd", "v", mark["wscales"])
            P.dma("gpsimd", lambda e, sm: e.dma_start(
                out=rws_row.rearrange("(j p) -> p j", p=128), in_=WRC[:, :],
                allow_slow_non_contiguous=True), "d_st")
            P.wait("gpsimd", "d_st")
            P.dma("gpsimd", lambda e, sm: e.dma_start(
                out=BRWS[:, :], in_=rws_row.rearrange("(o a) -> o a", o=1).broadcast_to([128, 2048])), "d_bc3")
            mark["brws_ready"] = P.count("d_bc3")
        return None, comp

    def q_fwt(dc, h2):
        def load(slot, semn, dc=dc, h2=h2):
            P.dma("scalar", lambda e, sm, dc=dc, h2=h2, slot=slot: e.dma_start(
                out=QS2[:, 1024 * slot:1024 * (slot + 1)],
                in_=FWT[128 * dc:128 * (dc + 1), 1024 * h2:1024 * (h2 + 1)]), semn)

        def comp(slot, semn, dc=dc, h2=h2):
            if dc == 0 and h2 == 0:
                P.wait("vector", "d_bc3", mark["brws_ready"])
            if h2 == 0:
                prevsp = mark.get(("wqsp", dc - 1))
                if prevsp is not None:
                    P.wait("vector", "d_wqs", prevsp)
            P.drain("vector")
            P.op("vector", lambda e, sm, h2=h2, slot=slot: e.tensor_tensor(
                F32B[:, :], QS2[:, 1024 * slot:1024 * (slot + 1)],
                BRWS[:, 1024 * h2:1024 * (h2 + 1)], ALU.mult), "v")
            P.drain("vector")
            P.op("vector", lambda e, sm: e.tensor_scalar(
                F32B[:, :], F32B[:, :], MAGIC, MAGIC, ALU.add, ALU.subtract), "v")
            P.drain("vector")
            P.op("vector", lambda e, sm: e.tensor_copy(
                F32B[:, :].bitcast(I32), F32B[:, :].bitcast(I32)), "v")
            P.drain("vector")
            P.op("vector", lambda e, sm, h2=h2: e.tensor_copy(
                PBUF[:, 1024 * h2:1024 * (h2 + 1)], F32B[:, :]), "v")
            if h2 == 1:
                mark[("wqq", dc)] = P.count("v")
                P.wait("gpsimd", "v", mark[("wqq", dc)])
                P.dma("gpsimd", lambda e, sm, dc=dc: e.dma_start(
                    out=wq_spill[128 * dc:128 * (dc + 1), :], in_=PBUF[:, :]), "d_wqs")
                mark[("wqsp", dc)] = P.count("d_wqs")
        return load, comp

    def q_xq(dc):
        def load(slot, semn, dc=dc):
            P.dma("scalar", lambda e, sm, dc=dc, slot=slot: e.dma_start(
                out=QS2[:, 1024 * slot:1024 * (slot + 1)],
                in_=XTM[128 * dc:128 * (dc + 1), :]), semn)

        def comp(slot, semn, dc=dc):
            if dc == 0:
                P.wait("vector", "d_bc1", mark["bq_ready"])
                P.wait("vector", "d_wqs", mark[("wqsp", NDC - 1)])
            else:
                prevsp = mark.get(("xqsp", dc - 1))
                if prevsp is not None:
                    P.wait("vector", "d_xqs", prevsp)
            P.drain("vector")
            P.op("vector", lambda e, sm, slot=slot: e.tensor_tensor(
                F32B[:, :], QS2[:, 1024 * slot:1024 * (slot + 1)], BMQ[:, :], ALU.subtract), "v")
            P.drain("vector")
            P.op("vector", lambda e, sm: e.tensor_mul(F32B[:, :], F32B[:, :], BQS[:, :]), "v")
            P.drain("vector")
            P.op("vector", lambda e, sm: e.tensor_scalar(
                F32B[:, :], F32B[:, :], MAGIC, MAGIC, ALU.add, ALU.subtract), "v")
            P.drain("vector")
            P.op("vector", lambda e, sm: e.tensor_copy(
                F32B[:, :].bitcast(I32), F32B[:, :].bitcast(I32)), "v")
            P.drain("vector")
            P.op("vector", lambda e, sm: e.tensor_copy(PBUF[:, 0:1024], F32B[:, :]), "v")
            mark[("xqq", dc)] = P.count("v")
            P.wait("gpsimd", "v", mark[("xqq", dc)])
            for tc in range(NTC):
                P.dma("gpsimd", lambda e, sm, dc=dc, tc=tc: e.dma_start(
                    out=xq_spill[tc, :, 512 * dc:512 * (dc + 1)],
                    in_=PBUF[:, 512 * tc:512 * (tc + 1)]), "d_xqs")
            mark[("xqsp", dc)] = P.count("d_xqs")
        return load, comp

    def q_xqfin():
        def comp(slot, semn):
            mark["xq_done"] = P.count("d_xqs")
            P.wait("gpsimd", "v", mark[("xqq", NDC - 1)])
            P.dma("gpsimd", lambda e, sm: e.dma_start(
                out=BQS[:, :], in_=xs_row.rearrange("(o a) -> o a", o=1).broadcast_to([128, TOK])), "d_bc2")
            mark["bxs_ready"] = P.count("d_bc2")
        return None, comp

    for i in range(8):
        for hf in range(2):
            q_add(*q_xm2(i, hf))
    q_add(*q_xscales())
    for i in range(NOC):
        for hf in range(2):
            q_add(*q_fw(i, hf))
    q_add(*q_wscales())
    for dc in range(NDC):
        for h2 in range(2):
            q_add(*q_fwt(dc, h2))
    for dc in range(NDC):
        q_add(*q_xq(dc))
    q_add(*q_xqfin())

    for wi, (srcw, dst) in enumerate([(WK, WA), (WV, WB)]):
        for g in range(8):
            k = wi * 8 + g
            slot = k % 2
            semn = "d_xt0" if slot == 0 else "d_xt1"
            prev = mark.get(("wcast", k - 2))
            if prev is not None:
                P.wait("sync", "s", prev)
            P.dma("sync", lambda e, sm, srcw=srcw, g=g, slot=slot: e.dma_start(
                out=XTG[:, 1024 * slot:1024 * (slot + 1)].rearrange("p (d c) -> p d c", d=2),
                in_=srcw[256 * g:256 * (g + 1), :].rearrange("(d p) c -> p d c", p=128)), semn)
            P.wait("scalar", semn)
            P.op("scalar", lambda e, sm, dst=dst, g=g, slot=slot: e.copy(
                dst[:, 1024 * g:1024 * (g + 1)], XTG[:, 1024 * slot:1024 * (slot + 1)]), "s")
            mark[("wcast", k)] = P.count("s")
    mark["wkv_ready_s"] = P.count("s")

    P.wait("tensor", "v", v_ones)
    for _ in range(64):
        P.op("tensor", lambda e, sm: e.matmul(
            ffb[0][:, 0:128], ONES[:, :], ONES[:, 0:128], start=True, stop=True,
            skip_group_check=True), None)

    def xcbuf(tt):
        return XQR, (tt % 2) * 8192

    def sw1_produce(tt):
        slot = tt % 2
        brt = "d_brt0" if slot == 0 else "d_brt1"
        P.wait("sync", "d_st", mark["rows_mrs"])
        prev = mark.get(("kevac_g", (tt - 2) * H_LOC + H_LOC - 1))
        if prev is not None:
            P.wait("sync", "v", prev)
        P.dma("sync", lambda e, sm, tt=tt, slot=slot: e.dma_start(
            out=BMT[:, 512 * slot:512 * (slot + 1)],
            in_=m_row[512 * tt:512 * (tt + 1)].rearrange("(o a) -> o a", o=1).broadcast_to([128, 512])), brt)
        P.dma("sync", lambda e, sm, tt=tt, slot=slot: e.dma_start(
            out=BRS[:, 512 * slot:512 * (slot + 1)],
            in_=rs_row[512 * tt:512 * (tt + 1)].rearrange("(o a) -> o a", o=1).broadcast_to([128, 512])), brt)
        mark[("brt", tt)] = P.count(brt)

        xcb, xcoff = xcbuf(tt)
        for g in range(8):
            kg = tt * 8 + g
            xslot = kg % 2
            xsem = "d_xt0" if xslot == 0 else "d_xt1"
            prev = mark.get(("xgsub", kg - 2))
            if prev is not None:
                P.wait("sync", "v", prev)
            elif kg < 2:
                P.wait("sync", "s", mark["wkv_ready_s"])
            if g == 0:
                prev_mm = mark.get(("sw1_mm", tt - 2))
                if prev_mm is not None:
                    P.wait("sync", "t", prev_mm)
                prev_sp = mark.get(("sw1_spill", tt - 2))
                if prev_sp is not None:
                    P.wait("sync", "d_sp0" if tt % 2 == 0 else "d_sp1", prev_sp)
            P.dma("sync", lambda e, sm, tt=tt, g=g, xslot=xslot: e.dma_start(
                out=XTG[:, 1024 * xslot:1024 * (xslot + 1)].rearrange("p (d c) -> p d c", d=2),
                in_=XT[tt, 2 * g:2 * (g + 1)].rearrange("d p c -> p d c")), xsem)
            P.wait("vector", xsem)
            if g == 0:
                P.wait("vector", brt, mark[("brt", tt)])
            for j in range(2):
                dc = 2 * g + j
                P.op("vector", lambda e, sm, xcb=xcb, xcoff=xcoff, dc=dc, j=j, xslot=xslot, slot=slot: e.tensor_tensor(
                    xcb[:, xcoff + 512 * dc:xcoff + 512 * (dc + 1)],
                    XTG[:, 1024 * xslot + 512 * j:1024 * xslot + 512 * (j + 1)],
                    BMT[:, 512 * slot:512 * (slot + 1)], ALU.subtract), "v")
            mark[("xgsub", kg)] = P.count("v")
            mark[("xcsub", tt * NDC + 2 * g + 1)] = P.count("v")
            if g % 2 == 1:
                P.wait("scalar", "v", mark[("xgsub", kg)])
                P.dma("scalar", lambda e, sm, xcb=xcb, xcoff=xcoff, tt=tt, g=g: e.dma_start(
                    out=xc_spill[tt, :, 2048 * (g // 2):2048 * (g // 2 + 1)],
                    in_=xcb[:, xcoff + 2048 * (g // 2):xcoff + 2048 * (g // 2 + 1)]), "d_sp0" if tt % 2 == 0 else "d_sp1")
        mark[("sw1_spill", tt)] = P.count("d_sp0" if tt % 2 == 0 else "d_sp1")

    def sw1_compute(tt):
        slot = tt % 2
        xcb, xcoff = xcbuf(tt)
        if tt == 0:
            P.wait("tensor", "s", mark["wkv_ready_s"])
        for ec in range(H_LOC):
            bank = pvb_t[ec % 2]
            gk = tt * H_LOC + ec
            prev = mark.get(("kevac_g", gk - 2))
            if prev is not None:
                P.wait("tensor", "v", prev)
            for dc in range(NDC):
                if dc == 0 and ec == 0:
                    P.wait("tensor", "v", mark[("xcsub", tt * NDC + NDC - 1)])
                P.op("tensor", lambda e, sm, bank=bank, ec=ec, dc=dc, xcb=xcb, xcoff=xcoff: e.matmul(
                    bank[:, :],
                    WA[:, 512 * dc + 128 * ec:512 * dc + 128 * (ec + 1)],
                    xcb[:, xcoff + 512 * dc:xcoff + 512 * (dc + 1)],
                    start=(dc == 0), stop=(dc == NDC - 1)),
                    "t" if dc == NDC - 1 else None)
            mark[("kmm", tt, ec)] = P.count("t")
            P.wait("vector", "t", mark[("kmm", tt, ec)])
            P.op("vector", lambda e, sm, bank=bank, ec=ec, tt=tt, slot=slot: e.tensor_tensor(
                KT[:, S * ec + 512 * tt:S * ec + 512 * (tt + 1)],
                bank[:, :], BRS[:, 512 * slot:512 * (slot + 1)], ALU.mult), "v")
            mark[("kevac_g", tt * H_LOC + ec)] = P.count("v")

        for tc in range(4):
            bank = ffb[tc % 2]
            gk = tt * 4 + tc
            prev = mark.get(("vevac_g", gk - 2))
            if prev is not None:
                P.wait("tensor", "s", prev)
            for dc in range(NDC):
                P.op("tensor", lambda e, sm, bank=bank, tc=tc, dc=dc, xcb=xcb, xcoff=xcoff: e.matmul(
                    bank[:, :],
                    xcb[:, xcoff + 512 * dc + 128 * tc:xcoff + 512 * dc + 128 * (tc + 1)],
                    WB[:, 512 * dc:512 * (dc + 1)],
                    start=(dc == 0), stop=(dc == NDC - 1)),
                    "t" if dc == NDC - 1 else None)
            mark[("vmm", tt, tc)] = P.count("t")
            ck = tt * 4 + tc
            P.wait("scalar", "t", mark[("vmm", tt, tc)])
            if ck == 0:
                P.wait("scalar", "v", mark["gstats"])
            P.op("scalar", lambda e, sm, bank=bank, ck=ck: e.activation(
                VT[:, 512 * ck:512 * (ck + 1)], bank[:, :], AF.Identity,
                bias=0.0, scale=GRS[:, ck:ck + 1]), "s")
            mark[("vevac_g", ck)] = P.count("s")
        mark[("sw1_mm", tt)] = P.count("t")
        pump(6)

    sw1_produce(0)
    for tt in range(NTT):
        if tt + 1 < NTT:
            sw1_produce(tt + 1)
        sw1_compute(tt)

    P.wait("sync", "t", mark[("sw1_mm", NTT - 1)])
    for g in range(8):
        slot = g % 2
        semn = "d_xt0" if slot == 0 else "d_xt1"
        prev = mark.get(("wqcast", g - 2))
        if prev is not None:
            P.wait("sync", "s", prev)
        elif g < 2:
            P.wait("sync", "v", mark[("xgsub", NTT * 8 - 1)])
        P.dma("sync", lambda e, sm, g=g, slot=slot: e.dma_start(
            out=XTG[:, 1024 * slot:1024 * (slot + 1)].rearrange("p (d c) -> p d c", d=2),
            in_=WQ[256 * g:256 * (g + 1), :].rearrange("(d p) c -> p d c", p=128)), semn)
        P.wait("scalar", semn)
        if g == 0:
            P.wait("scalar", "t", mark[("sw1_mm", NTT - 1)])
        P.op("scalar", lambda e, sm, g=g, slot=slot: e.copy(
            WA[:, 1024 * g:1024 * (g + 1)], XTG[:, 1024 * slot:1024 * (slot + 1)]), "s")
        mark[("wqcast", g)] = P.count("s")
    mark["wq_cast_done"] = P.count("s")

    def sw2_produce(tt):
        xcb, xcoff = xcbuf(tt)
        semn = "d_xc0" if tt % 2 == 0 else "d_xc1"
        prev = mark.get(("qmm_all", tt - 2))
        if prev is not None:
            P.wait("sync", "t", prev)
        else:
            P.wait("sync", "t", mark[("sw1_mm", NTT - 1)])
        if tt == 0:
            P.wait("sync", "d_sp0")
            P.wait("sync", "d_sp1")
        P.dma("sync", lambda e, sm, tt=tt, xcb=xcb, xcoff=xcoff: e.dma_start(
            out=xcb[:, xcoff:xcoff + 8192], in_=xc_spill[tt]), semn)
        mark[("xcld", tt)] = P.count(semn)
        slot = tt % 2
        brt = "d_brt0" if slot == 0 else "d_brt1"
        prev = mark.get(("qevac_g", (tt - 2) * H_LOC + 3))
        if prev is not None:
            P.wait("sync", "v", prev)
        else:
            P.wait("sync", "v", mark[("kevac_g", NTT * H_LOC - 1)])
        P.dma("sync", lambda e, sm, tt=tt, slot=slot: e.dma_start(
            out=BRS[:, 512 * slot:512 * (slot + 1)],
            in_=rs_row[512 * tt:512 * (tt + 1)].rearrange("(o a) -> o a", o=1).broadcast_to([128, 512])), brt)
        mark[("brt2", tt)] = P.count(brt)

    def sw2_compute(tt):
        xcb, xcoff = xcbuf(tt)
        semn = "d_xc0" if tt % 2 == 0 else "d_xc1"
        for ec in range(H_LOC):
            bank = pvb_t[ec % 2]
            gk = tt * H_LOC + ec
            prev = mark.get(("qevac_g", gk - 2))
            if prev is not None:
                P.wait("tensor", "v", prev)
            elif gk < 2:
                P.wait("tensor", "v", mark[("kevac_g", NTT * H_LOC - 1)])
            for dc in range(NDC):
                if dc == 0 and ec == 0:
                    P.wait("tensor", semn, mark[("xcld", tt)])
                    if tt == 0:
                        P.wait("tensor", "s", mark["wq_cast_done"])
                P.op("tensor", lambda e, sm, bank=bank, ec=ec, dc=dc, xcb=xcb, xcoff=xcoff: e.matmul(
                    bank[:, :],
                    WA[:, 512 * dc + 128 * ec:512 * dc + 128 * (ec + 1)],
                    xcb[:, xcoff + 512 * dc:xcoff + 512 * (dc + 1)],
                    start=(dc == 0), stop=(dc == NDC - 1)),
                    "t" if dc == NDC - 1 else None)
            mark[("qmm", tt, ec)] = P.count("t")
            slot = tt % 2
            brt = "d_brt0" if slot == 0 else "d_brt1"
            P.wait("vector", "t", mark[("qmm", tt, ec)])
            if ec == 0:
                P.wait("vector", brt, mark[("brt2", tt)])
            P.op("vector", lambda e, sm, bank=bank, ec=ec, tt=tt, slot=slot: e.tensor_tensor(
                QT[:, S * ec + 512 * tt:S * ec + 512 * (tt + 1)],
                bank[:, :], BRS[:, 512 * slot:512 * (slot + 1)], ALU.mult), "v")
            mark[("qevac_g", tt * H_LOC + ec)] = P.count("v")
        mark[("qmm_all", tt)] = P.count("t")
        pump(8)

    sw2_produce(0)
    sw2_produce(1)
    for tt in range(NTT):
        sw2_compute(tt)
        if tt + 2 < NTT:
            sw2_produce(tt + 2)
    pump_all()
    mark["sweep2_done_t"] = P.count("t")
    mark["sweep2_done_v"] = P.count("v")

    def emit_xq_load(tc, slot):
        semn = "d_xq0" if slot == 0 else "d_xq1"
        P.dma("sync", lambda e, sm, tc=tc, slot=slot: e.dma_start(
            out=XQR[:, 8192 * slot:8192 * (slot + 1)], in_=xq_spill[tc]), semn)
        return P.count(semn)

    def emit_wq_load(blk):
        slot = blk % 2
        semn = f"d_wq{slot}"
        P.dma("scalar", lambda e, sm, blk=blk, slot=slot: e.dma_start(
            out=WB[:, 4096 * slot:4096 * (slot + 1)].rearrange("p (d c) -> p d c", d=16),
            in_=wq_spill[:, 256 * blk:256 * (blk + 1)].rearrange("(d p) c -> p d c", p=128)), semn)
        return P.count(semn)

    P.wait("sync", "d_xqs", mark["xq_done"])
    P.wait("sync", "t", mark["sweep2_done_t"])
    mark[("xqld", 0)] = emit_xq_load(0, 0)
    P.wait("scalar", "d_wqs", mark[("wqsp", NDC - 1)])
    P.wait("scalar", "t", mark["sweep2_done_t"])
    wq_marks = {}
    for blk in range(2):
        wq_marks[blk] = emit_wq_load(blk)

    BDEN = [(BRWS, 0), (BRWS, 512)]
    FFT1 = (BRWS, 1024)
    GST = (BRWS, 1536)

    n_it = H_LOC * NQT
    NP = NKC // 2

    P.wait("scalar", "v", mark[("xqq", NDC - 1)])
    P.wait("scalar", "d_wqs", mark[("wqsp", NDC - 1)])
    P.wait("scalar", "d_xqs", mark["xq_done"])
    P.wait("scalar", "d_bc1", mark["bq_ready"])
    P.wait("scalar", "d_bc2", mark["bxs_ready"])
    P.wait("vector", "d_bc2", mark["bxs_ready"])
    P.wait("vector", "d_xqs", mark["xq_done"])
    P.wait("tensor", "v", v_ones)

    pend_den = [None]
    for it in range(n_it):
        h, qt = divmod(it, NQT)
        f_tc, f_oc = divmod(it, NOC)
        fbank = ffb[it % 2]
        pvb = pvb_t[it % 2]

        if it == 8:
            mark[("xqld", 1)] = emit_xq_load(1, 1)

        ffn_dc = 0
        for p in range(NP + 2):
            if p >= 2:
                pp = p - 2
                for half in range(2):
                    kcp = 2 * pp + half
                    if kcp == 0:
                        prev = mark.get(("aoevac", it - 2))
                        if prev is not None:
                            P.wait("tensor", "v", prev)
                        elif it < 2:
                            P.wait("tensor", "v", mark["sweep2_done_v"])
                    P.wait("tensor", "s", mark[("exp", it, pp)])
                    P.op("tensor", lambda e, sm, h=h, kcp=kcp, pvb=pvb, pp=pp, half=half: e.matmul(
                        pvb[:, :],
                        VT[:, 512 * kcp + 128 * h:512 * kcp + 128 * (h + 1)],
                        PBUF[:, 1024 * (pp % 2) + 512 * half:1024 * (pp % 2) + 512 * (half + 1)],
                        start=(kcp == 0), stop=(kcp == NKC - 1), skip_group_check=True),
                        "t" if kcp == NKC - 1 else None)
                    if kcp == NKC - 1:
                        mark[("pvmm", it)] = P.count("t")
                if ffn_dc < NDC:
                    dc = ffn_dc
                    ffn_dc += 1
                    gb = it // 2
                    wslot2 = gb % 2
                    if dc == 0:
                        P.wait("tensor", f"d_wq{wslot2}", wq_marks[gb])
                        P.wait("tensor", "d_xq0" if f_tc == 0 else "d_xq1", mark[("xqld", f_tc)])
                        prevf = mark.get(("recip", it - 2))
                        if prevf is not None:
                            P.wait("tensor", "v", prevf)
                        elif it < 2:
                            P.wait("tensor", "s", mark[("vevac_g", NTT * 4 - 1)])
                    P.op("tensor", lambda e, sm, f_oc=f_oc, f_tc=f_tc, dc=dc, fbank=fbank, wslot2=wslot2: e.matmul(
                        fbank[:, :],
                        WB[:, 4096 * wslot2 + 256 * dc + 128 * (f_oc % 2):4096 * wslot2 + 256 * dc + 128 * (f_oc % 2) + 128],
                        XQR[:, 8192 * f_tc + 512 * dc:8192 * f_tc + 512 * (dc + 1)],
                        start=(dc == 0), stop=(dc == NDC - 1), skip_group_check=True),
                        "t" if dc == NDC - 1 else None)
                    if dc == NDC - 1:
                        mark[("ffnmm", it)] = P.count("t")
            if p < NP:
                scb = scp[p % 2]
                prev = mark.get(("exp", it, p - 2))
                if prev is None:
                    prev = mark.get(("exp", it - 1, NP - 2 + p))
                if prev is not None:
                    P.wait("tensor", "s", prev)
                for half in range(2):
                    kc = 2 * p + half
                    P.op("tensor", lambda e, sm, h=h, qt=qt, kc=kc, scb=scb, half=half: e.matmul(
                        scb[:, 512 * half:512 * (half + 1)],
                        KT[:, S * h + 128 * kc:S * h + 128 * (kc + 1)],
                        QT[:, S * h + 512 * qt:S * h + 512 * (qt + 1)],
                        start=True, stop=True, skip_group_check=True), "t")
                mark[("sc", it, p)] = P.count("t")
                if p == 0 and pend_den[0] is not None:
                    pend_den[0]()
                    pend_den[0] = None
                    if it == 16:
                        P.wait("gpsimd", "d_ao0")
                        P.wait("gpsimd", "d_ao1")
                        P.op("gpsimd", lambda e, sm: e.collective_compute(
                            "AllGather", ALU.bypass, replica_groups=[[0, 1, 2, 3], [4, 5, 6, 7]],
                            ins=[cc_in[0:256, :].opt()], outs=[cc_out1.opt()]), "cc2")
                P.wait("scalar", "t", mark[("sc", it, p)])
                prevv = mark.get(("add", it, p - 2)) if p >= 2 else mark.get(("add", it - 1, NP - 1))
                if prevv is not None:
                    P.wait("scalar", "v", prevv)
                P.op("scalar", lambda e, sm, p=p, scb=scb: e.activation(
                    PBUF[:, 1024 * (p % 2):1024 * (p % 2 + 1)], scb[:, :],
                    AF.Exp, scale=SCL_ATTN), "s")
                mark[("exp", it, p)] = P.count("s")
                P.wait("vector", "s", mark[("exp", it, p)])
                if p == 0:
                    prevd = mark.get(("denmm", it - 1))
                    if prevd is not None:
                        P.wait("vector", "t", prevd)
                    P.op("vector", lambda e, sm, p=p: e.tensor_copy(
                        PSAC[:, :], PBUF[:, 1024 * (p % 2):1024 * (p % 2 + 1)]), "v")
                else:
                    P.drain("vector")
                    P.op("vector", lambda e, sm, p=p: e.tensor_tensor(
                        PSAC[:, :], PSAC[:, :], PBUF[:, 1024 * (p % 2):1024 * (p % 2 + 1)],
                        ALU.add), "v")
                mark[("add", it, p)] = P.count("v")

        P.drain("vector")
        P.op("vector", lambda e, sm: e.tensor_tensor(
            PSAC[:, 0:512], PSAC[:, 0:512], PSAC[:, 512:1024], ALU.add), "v")
        mark[("fold", it)] = P.count("v")
        def emit_den(it=it, fbank=fbank, pvb=pvb, h=h, qt=qt):
            P.wait("tensor", "v", mark[("fold", it)])
            P.wait("tensor", "v", mark[("ffn_deq", it)])
            P.op("tensor", lambda e, sm, fbank=fbank: e.matmul(
                fbank[:, :], ONES[:, :], PSAC[:, 0:512], start=True, stop=True,
                skip_group_check=True), "t")
            mark[("denmm", it)] = P.count("t")
            db, doff = BDEN[it % 2]
            P.wait("vector", "t", mark[("denmm", it)])
            prev = mark.get(("aoevac", it - 2))
            if prev is not None:
                P.wait("vector", "v", prev)
            elif it == 0:
                P.wait("vector", "d_bc3", mark["brws_ready"])
                P.wait("vector", "v", mark[("wqq", NDC - 1)])
            P.op("vector", lambda e, sm, db=db, doff=doff, fbank=fbank: e.reciprocal(
                db[:, doff:doff + 512], fbank[:, :]), "v")
            mark[("recip", it)] = P.count("v")
            P.drain("vector")
            P.wait("vector", "t", mark[("pvmm", it)])
            prev = mark.get(("aodma", it - 2))
            if prev is not None:
                P.wait("vector", "d_ao0" if it % 2 == 0 else "d_ao1", prev)
            P.op("vector", lambda e, sm, it=it, db=db, doff=doff, pvb=pvb: e.tensor_tensor(
                AOB[:, 512 * (it % 2):512 * (it % 2 + 1)],
                pvb[:, :], db[:, doff:doff + 512], ALU.mult), "v")
            mark[("aoevac", it)] = P.count("v")
            P.wait("gpsimd", "v", mark[("aoevac", it)])
            P.dma("gpsimd", lambda e, sm, it=it, h=h, qt=qt: e.dma_start(
                out=cc_in[128 * h:128 * (h + 1), 512 * qt:512 * (qt + 1)],
                in_=AOB[:, 512 * (it % 2):512 * (it % 2 + 1)]), "d_ao0" if it % 2 == 0 else "d_ao1")
            mark[("aodma", it)] = P.count("d_ao0" if it % 2 == 0 else "d_ao1")
        pend_den[0] = emit_den

        t1b, t1off = FFT1
        gb, goff = GST
        P.wait("vector", "t", mark[("ffnmm", it)])
        prevg = mark.get(("gdma", it - 1))
        if prevg is not None:
            P.wait("vector", "d_gs", prevg)
        P.op("vector", lambda e, sm, t1b=t1b, t1off=t1off, fbank=fbank, f_tc=f_tc: e.tensor_tensor(
            t1b[:, t1off:t1off + 512], fbank[:, :],
            BQS[:, 512 * f_tc:512 * (f_tc + 1)], ALU.mult), "v")
        mark[("ffn_deq", it)] = P.count("v")
        P.wait("scalar", "v", mark[("ffn_deq", it)])
        P.op("scalar", lambda e, sm, t1b=t1b, t1off=t1off, gb=gb, goff=goff, f_oc=f_oc: e.activation(
            gb[:, goff:goff + 512], t1b[:, t1off:t1off + 512], AF.Silu,
            bias=BIASC[:, f_oc:f_oc + 1], scale=WSC[:, f_oc:f_oc + 1]), "s")
        mark[("silu", it)] = P.count("s")
        P.wait("gpsimd", "s", mark[("silu", it)])
        P.dma("gpsimd", lambda e, sm, f_oc=f_oc, f_tc=f_tc, gb=gb, goff=goff: e.dma_start(
            out=g_spill[128 * f_oc:128 * (f_oc + 1), 512 * f_tc:512 * (f_tc + 1)],
            in_=gb[:, goff:goff + 512]), "d_gs")
        mark[("gdma", it)] = P.count("d_gs")

        if it % 2 == 1:
            gb_next = it // 2 + 2
            if gb_next < 16:
                P.wait("scalar", "v", mark[("ffn_deq", it)])
                wq_marks[gb_next] = emit_wq_load(gb_next % 8)


    wo_cnt = [0]

    def emit_wo_load(ec, gate_t):
        wslot = ec % 2
        for hf in range(2):
            idx = wo_cnt[0]
            wo_cnt[0] += 1
            xs2 = idx % 2
            wsem = "d_xt0" if xs2 == 0 else "d_xt1"
            prevx = mark.get(("wocastg", idx - 2))
            if prevx is not None:
                P.wait("sync", "s", prevx)
            else:
                P.wait("sync", "s", mark["wq_cast_done"])
                P.wait("sync", "v", mark[("xgsub", NTT * 8 - 1)])
                P.wait("sync", "t", mark["sweep2_done_t"])
            P.dma("sync", lambda e, sm, ec=ec, hf=hf, xs2=xs2: e.dma_start(
                out=XTG[:, 1024 * xs2:1024 * (xs2 + 1)],
                in_=WOT[ec][:, 8 * hf:8 * (hf + 1), :].rearrange("p d c -> p (d c)")), wsem)
            P.wait("scalar", wsem)
            if hf == 0:
                P.wait("scalar", "t", gate_t)
            P.op("scalar", lambda e, sm, wslot=wslot, hf=hf, xs2=xs2: e.copy(
                WA[:, 2048 * wslot + 1024 * hf:2048 * wslot + 1024 * (hf + 1)],
                XTG[:, 1024 * xs2:1024 * (xs2 + 1)]), "s")
            mark[("wocastg", idx)] = P.count("s")
        mark[("wocast", ec)] = P.count("s")

    if pend_den[0] is not None:
        pend_den[0]()
        pend_den[0] = None

    emit_wo_load(0, mark["sweep2_done_t"])
    emit_wo_load(1, mark["sweep2_done_t"])

    P.wait("gpsimd", "d_ao0")
    P.wait("gpsimd", "d_ao1")
    P.op("gpsimd", lambda e, sm: e.collective_compute(
        "AllGather", ALU.bypass, replica_groups=[[0, 1, 2, 3], [4, 5, 6, 7]],
        ins=[cc_in[256:512, :].opt()], outs=[cc_out2.opt()]), "cc2b")
    P.dma("gpsimd", lambda e, sm: e.dma_start(out=SOFF[:, :], in_=COFF[:, :]), "d_rb")
    P.wait("gpsimd", "d_rb")
    P.wait("gpsimd", "cc2")
    holder = {}

    def _alloc_reg(e, sm):
        holder["roff"] = e.alloc_register("roff")
        return e.nop()
    P.op("gpsimd", _alloc_reg, None)
    done_p2_wait = [False]
    ec_order = [ec for ec in range(NOC) if (ec % 4) // 2 == 0] + \
               [ec for ec in range(NOC) if (ec % 4) // 2 == 1]
    for ec in ec_order:
        i, hh = divmod(ec, 4)
        part = hh // 2
        src_t = cc_out1_t if part == 0 else cc_out2_t
        if part == 1 and not done_p2_wait[0]:
            P.wait("gpsimd", "cc2b")
            done_p2_wait[0] = True
        row = 256 * i + 128 * (hh % 2)
        P.raw("gpsimd", lambda e, sm: e.reg_load(holder["roff"], SOFF[0:1, 0:1]))
        P.raw("gpsimd", lambda e, sm, row=row: e.reg_add(holder["roff"], holder["roff"], row * S))
        P.dma("gpsimd", lambda e, sm, ec=ec, src_t=src_t: e.dma_start(
            out=KT[:, TOK * ec:TOK * (ec + 1)],
            in_=bass.AP(src_t, holder["roff"], [[S, 128], [1, TOK]])), "d_rb")
    mark["rb_done"] = P.count("d_rb")

    for gi in range(NOC * NTC):
        ec, tc = divmod(gi, NTC)
        bank = pvb_t[gi % 2]
        wslot = ec % 2

        gsem = "d_g0" if gi % 2 == 0 else "d_g1"
        xsem = "d_xe0" if gi % 2 == 0 else "d_xe1"
        prev = mark.get(("ep_done", gi - 2))
        if prev is not None:
            P.wait("scalar", "v", prev)
        else:
            P.wait("scalar", "d_gs", mark[("gdma", n_it - 1)])
        P.dma("scalar", lambda e, sm, ec=ec, tc=tc, gi=gi: e.dma_start(
            out=F32A[:, 512 * (gi % 2):512 * (gi % 2 + 1)],
            in_=g_spill[128 * ec:128 * (ec + 1), 512 * tc:512 * (tc + 1)]), gsem)
        P.dma("scalar", lambda e, sm, ec=ec, tc=tc, gi=gi: e.dma_start(
            out=BRS[:, 512 * (gi % 2):512 * (gi % 2 + 1)],
            in_=XTM[128 * ec:128 * (ec + 1), 512 * tc:512 * (tc + 1)]), xsem)
        mark[("epld", gi)] = (P.count(gsem), P.count(xsem))

        for dc in range(NOC):
            if dc == 0:
                P.wait("tensor", "d_rb", mark["rb_done"])
                P.wait("tensor", "s", mark[("wocast", ec)])
                prevb = mark.get(("ep_done", gi - 2))
                if prevb is not None:
                    P.wait("tensor", "v", prevb)
            P.op("tensor", lambda e, sm, wslot=wslot, dc=dc, tc=tc, bank=bank: e.matmul(
                bank[:, :],
                WA[:, 2048 * wslot + 128 * dc:2048 * wslot + 128 * (dc + 1)],
                KT[:, TOK * dc + 512 * tc:TOK * dc + 512 * (tc + 1)],
                start=(dc == 0), stop=(dc == NOC - 1), skip_group_check=True),
                "t" if dc == NOC - 1 else None)
        mark[("womm", gi)] = P.count("t")
        if tc == NTC - 1 and ec + 2 < NOC:
            emit_wo_load(ec + 2, mark[("womm", gi)])

        P.wait("vector", "t", mark[("womm", gi)])
        P.wait("vector", gsem, mark[("epld", gi)][0])
        P.wait("vector", xsem, mark[("epld", gi)][1])
        prevo = mark.get(("outdma", gi - 2))
        if prevo is not None:
            P.wait("vector", "d_out0" if gi % 2 == 0 else "d_out1", prevo)
        P.op("vector", lambda e, sm, gi=gi, bank=bank: e.scalar_tensor_tensor(
            F32B[:, 512 * (gi % 2):512 * (gi % 2 + 1)],
            bank[:, :], 1.0,
            F32A[:, 512 * (gi % 2):512 * (gi % 2 + 1)], ALU.mult, ALU.mult), "v")
        P.drain("vector")
        P.op("vector", lambda e, sm, gi=gi: e.tensor_tensor(
            F32B[:, 512 * (gi % 2):512 * (gi % 2 + 1)],
            F32B[:, 512 * (gi % 2):512 * (gi % 2 + 1)],
            BRS[:, 512 * (gi % 2):512 * (gi % 2 + 1)], ALU.add), "v")
        mark[("ep_done", gi)] = P.count("v")
        P.wait("gpsimd", "v", mark[("ep_done", gi)])
        P.dma("gpsimd", lambda e, sm, ec=ec, tc=tc, gi=gi: e.dma_start(
            out=OUT[128 * ec:128 * (ec + 1), 512 * tc:512 * (tc + 1)],
            in_=F32B[:, 512 * (gi % 2):512 * (gi % 2 + 1)]), "d_out0" if gi % 2 == 0 else "d_out1")
        mark[("outdma", gi)] = P.count("d_out0" if gi % 2 == 0 else "d_out1")

    P.wait("gpsimd", "d_out0")
    P.wait("gpsimd", "d_out1")

    with ExitStack() as ses:
        sems = {n: ses.enter_context(nc.semaphore(n)) for n in sem_names}
        with nc.Block() as block:
            @block.sync
            def _(sync):
                P.replay("sync", sync, sems)

            @block.tensor
            def _(tensor):
                P.replay("tensor", tensor, sems)

            @block.vector
            def _(vector):
                P.replay("vector", vector, sems)

            @block.scalar
            def _(scalar):
                P.replay("scalar", scalar, sems)

            @block.gpsimd
            def _(gpsimd):
                with allow_ici_all_to_all():
                    P.replay("gpsimd", gpsimd, sems)
    es.close()
    return nc


_NC_CACHE = None


def kernel(L0_attn_x, L0_attn_wq, L0_attn_wk, L0_attn_wv, L0_attn_wo,
           L0_ffn_weight, L0_ffn_bias):
    global _NC_CACHE
    x = np.asarray(L0_attn_x, dtype=np.float32)
    wq = np.asarray(L0_attn_wq, dtype=np.float32)
    wk = np.asarray(L0_attn_wk, dtype=np.float32)
    wv = np.asarray(L0_attn_wv, dtype=np.float32)
    wo = np.asarray(L0_attn_wo, dtype=np.float32)
    fw = np.asarray(L0_ffn_weight, dtype=np.float32)
    fb = np.asarray(L0_ffn_bias, dtype=np.float32)

    C = np.ascontiguousarray
    wot = C(wo.T.reshape(NDC, 128, NOC, 128).transpose(2, 1, 0, 3))
    fwt = C(fw.T)
    fw_c = C(fw)
    in_maps = []
    for c in range(8):
        b, g = divmod(c, 4)
        sl = slice(TOK * g, TOK * (g + 1))
        es0 = slice(E_LOC * g, E_LOC * (g + 1))
        xb = x[b]
        xtr = np.ascontiguousarray(
            xb.T.reshape(NDC, 128, NTT, 512).transpose(2, 0, 1, 3))
        in_maps.append({
            "XT": xtr,
            "XTM": C(xb[sl].T),
            "XM": C(xb[sl]),
            "WQ": C(wq[es0].T),
            "WK": C(wk[es0].T),
            "WV": C(wv[es0].T),
            "WOT": wot,
            "FW": fw_c,
            "FWT": fwt,
            "FB": fb,
            "COFF": np.array([[TOK * g]], dtype=np.uint32),
        })

    if _NC_CACHE is None:
        _NC_CACHE = build_nc()
    nc = _NC_CACHE
    res = run_bass_kernel_spmd(nc, in_maps, core_ids=list(range(8)))

    out = np.empty((B, S, D), dtype=np.float32)
    for c in range(8):
        b, g = divmod(c, 4)
        out[b, TOK * g:TOK * (g + 1), :] = np.asarray(res.results[c]["OUT"]).T
    return out


# version 08
# speedup vs baseline: 1.0357x; 1.0357x over previous
import numpy as np
import ml_dtypes
from contextlib import ExitStack

import concourse.bass as bass
import concourse.mybir as mybir
from concourse.bass_utils import run_bass_kernel_spmd
from concourse.aot_env import allow_ici_all_to_all

F32 = mybir.dt.float32
BF16 = mybir.dt.bfloat16
I32 = mybir.dt.int32
U32 = mybir.dt.uint32
AF = mybir.ActivationFunctionType
ALU = mybir.AluOpType
AX = mybir.AxisListType

D = 2048
S = 4096
B = 2
H_LOC = 4
DK = 128
E_LOC = H_LOC * DK
TOK = 1024
NDC = D // 128
NTT = S // 512
NKC = S // 128
NQT = S // 512
NOC = D // 128
NTC = TOK // 512

EPS = 1e-5
SCL_ATTN = float(1.0 / np.sqrt(np.float32(DK)))
MAGIC = float(np.float32(3 * 2 ** 22))
QMAX_INV = float(np.float32(1.0 / 127.0))

ENGINES = ("sync", "tensor", "vector", "scalar", "gpsimd")


class Prog:
    def __init__(self):
        self.ops = {e: [] for e in ENGINES}
        self.cnt = {}

    def new_sem(self, name):
        self.cnt[name] = 0

    def raw(self, eng, fn):
        self.ops[eng].append(("raw", fn, None, 0))

    def op(self, eng, fn, inc_sem=None, inc_by=1):
        if inc_sem is not None:
            self.cnt[inc_sem] += inc_by
        self.ops[eng].append(("op", fn, inc_sem, inc_by))

    def dma(self, eng, fn, sem):
        self.op(eng, fn, inc_sem=sem, inc_by=16)

    def wait(self, eng, sem, val=None):
        v = self.cnt[sem] if val is None else val
        self.ops[eng].append(("wait", None, sem, v))

    def drain(self, eng):
        self.ops[eng].append(("drain", None, None, 0))

    def count(self, sem):
        return self.cnt[sem]

    def replay(self, eng_name, eng_handle, sems):
        for kind, fn, sem, val in self.ops[eng_name]:
            if kind == "wait":
                eng_handle.wait_ge(sems[sem], val)
            elif kind == "drain":
                eng_handle.drain()
            elif kind == "raw":
                fn(eng_handle, sems)
            else:
                inst = fn(eng_handle, sems)
                if sem is not None:
                    inst.then_inc(sems[sem], val)


def build_nc():
    nc = bass.Bass("TRN2", target_bir_lowering=False, debug=False)

    XT = nc.dram_tensor("XT", [NTT, NDC, 128, 512], F32, kind="ExternalInput").ap()
    XTM = nc.dram_tensor("XTM", [D, TOK], F32, kind="ExternalInput").ap()
    XM = nc.dram_tensor("XM", [TOK, D], F32, kind="ExternalInput").ap()
    WQ = nc.dram_tensor("WQ", [D, E_LOC], F32, kind="ExternalInput").ap()
    WK = nc.dram_tensor("WK", [D, E_LOC], F32, kind="ExternalInput").ap()
    WV = nc.dram_tensor("WV", [D, E_LOC], F32, kind="ExternalInput").ap()
    WOT = nc.dram_tensor("WOT", [NOC, 128, NDC, 128], F32, kind="ExternalInput").ap()
    FW = nc.dram_tensor("FW", [D, D], F32, kind="ExternalInput").ap()
    FWT = nc.dram_tensor("FWT", [D, D], F32, kind="ExternalInput").ap()
    FB = nc.dram_tensor("FB", [D], F32, kind="ExternalInput").ap()
    COFF = nc.dram_tensor("COFF", [1, 1], U32, kind="ExternalInput").ap()
    OUT = nc.dram_tensor("OUT", [D, TOK], F32, kind="ExternalOutput").ap()

    xc_spill = nc.dram_tensor("xc_spill", [NTT, 128, NDC * 512], BF16).ap()
    wq_spill = nc.dram_tensor("wq_spill", [D, D], BF16).ap()
    xq_spill = nc.dram_tensor("xq_spill", [NTC, 128, NDC * 512], BF16).ap()
    g_spill = nc.dram_tensor("g_spill", [D, TOK], F32).ap()
    m_row = nc.dram_tensor("m_row", [S], BF16).ap()
    rs_row = nc.dram_tensor("rs_row", [S], F32).ap()
    rws_row = nc.dram_tensor("rws_row", [D], F32).ap()
    qs_row = nc.dram_tensor("qs_row", [TOK], F32).ap()
    xs_row = nc.dram_tensor("xs_row", [TOK], F32).ap()
    mq_row = nc.dram_tensor("mq_row", [TOK], BF16).ap()
    cst_in = nc.dram_tensor("cst_in", [2, TOK], F32).ap()
    cst_out = nc.dram_tensor("cst_out", [8, TOK], F32).ap()
    cc_in = nc.dram_tensor("cc_in", [E_LOC, S], BF16).ap()
    cc_out1_t = nc.dram_tensor("cc_out1", [4 * 256, S], BF16)
    cc_out1 = cc_out1_t.ap()
    cc_out2_t = nc.dram_tensor("cc_out2", [4 * 256, S], BF16)
    cc_out2 = cc_out2_t.ap()

    es = ExitStack()
    SB = lambda name, shape, dt: es.enter_context(nc.sbuf_tensor(name, shape, dt))

    KT = SB("KT", [128, H_LOC * S], BF16)
    QT = SB("QT", [128, H_LOC * S], BF16)
    VT = SB("VT", [128, NKC * E_LOC], BF16)
    XQR = SB("XQR", [128, 2 * NDC * 512], BF16)
    WA = SB("WA", [128, NDC * E_LOC], BF16)
    WB = SB("WB", [128, NDC * E_LOC], BF16)
    F32B = SB("F32B", [128, 1024], F32)
    QS2 = SB("QS2", [128, 2 * 1024], F32)
    F32A = QS2
    XTG = SB("XTG", [128, 2 * 1024], F32)
    BRS = SB("BRS", [128, 2 * 512], F32)
    BMT = SB("BMT", [128, 2 * 512], BF16)
    MROW2 = BMT
    NEGWK = SB("NEGWK", [1, E_LOC], BF16)
    NEGWV = SB("NEGWV", [1, E_LOC], BF16)
    NEGWQ = NEGWK
    BQS = SB("BQS", [128, TOK], F32)
    BMQ = SB("BMQ", [128, TOK], BF16)
    BRWS = SB("BRWS", [128, 2048], F32)
    PBUF = SB("PBUF", [128, 2048], BF16)
    ONES = SB("ONES", [128, 128], BF16)
    SUMC = SB("SUMC", [128, 8], F32)
    SQC = SB("SQC", [128, 8], F32)
    MMC = SB("MMC", [128, 8], F32)
    RSM = SB("RSM", [128, 8], F32)
    AXC = SB("AXC", [128, 8], F32)
    XSC = SB("XSC", [128, 8], F32)
    QSC = SB("QSC", [128, 8], F32)
    GSUM = SB("GSUM", [128, 32], F32)
    GSQ = SB("GSQ", [128, 32], F32)
    GM = SB("GM", [128, 32], F32)
    GMB = SB("GMB", [128, 32], BF16)
    GRS = SB("GRS", [128, 32], F32)
    GT0 = SB("GT0", [128, 32], F32)
    WMX = SB("WMX", [128, 16], F32)
    WSC = SB("WSC", [128, 16], F32)
    WRC = SB("WRC", [128, 16], F32)
    BIASC = SB("BIASC", [128, 16], F32)
    MQB = SB("MQB", [128, 8], BF16)
    PSAC = BMT
    AOB = BMQ
    SOFF = SB("SOFF", [1, 1], U32)

    scp = [es.enter_context(nc.psum_tensor(f"scp{i}", [128, 1024], F32)) for i in range(2)]
    pvb_t = [es.enter_context(nc.psum_tensor(f"pv{i}", [128, 512], F32)) for i in range(2)]
    ffb = [es.enter_context(nc.psum_tensor(f"ffn{i}", [128, 512], F32)) for i in range(2)]

    sem_names = [
        "t", "v", "s", "cc1", "cc2", "cc2b",
        "d_xm0", "d_st",
        "d_bc1", "d_bc2", "d_bc3",
        "d_xt0", "d_xt1",
        "d_sp0", "d_sp1",
        "d_xc0", "d_xc1",
        "d_fw", "d_fwt", "d_wqs", "d_xqs", "d_xtm",
        "d_q0", "d_q1",
        "d_brt0", "d_brt1",
        "d_wq0", "d_wq1", "d_wq2", "d_wq3",
        "d_xq0", "d_xq1",
        "d_ao0", "d_ao1",
        "d_gs", "d_rb",
        "d_g0", "d_g1", "d_xe0", "d_xe1",
        "d_out0", "d_out1",
    ]
    P = Prog()
    for n in sem_names:
        P.new_sem(n)
    mark = {}

    P.op("vector", lambda e, sm: e.memset(ONES[:, :], 1.0), "v")
    v_ones = P.count("v")

    P.dma("gpsimd", lambda e, sm: e.dma_start(
        out=BIASC[:, :], in_=FB.rearrange("(j p) -> p j", p=128),
        allow_slow_non_contiguous=True), "d_bc1")

    q1cnt = {}
    def _xm1_load(ih):
        i, hf = divmod(ih, 2)
        slot = ih % 2
        semn = "d_q0" if slot == 0 else "d_q1"
        pv = mark.get(("xm1v", ih - 2))
        if pv is not None:
            P.wait("scalar", "v", pv)
            P.wait("scalar", "s", mark[("xm1s", ih - 2)])
        P.dma("scalar", lambda e, sm, i=i, hf=hf, slot=slot: e.dma_start(
            out=QS2[:, 1024 * slot:1024 * (slot + 1)],
            in_=XM[128 * i:128 * (i + 1), 1024 * hf:1024 * (hf + 1)]), semn)
        q1cnt[ih] = P.count(semn)

    _xm1_load(0)
    _xm1_load(1)
    for ih in range(16):
        i, hf = divmod(ih, 2)
        slot = ih % 2
        semn = "d_q0" if slot == 0 else "d_q1"
        P.wait("vector", semn, q1cnt[ih])
        P.drain("vector")
        P.op("vector", lambda e, sm, slot=slot, hf=hf: e.tensor_reduce(
            GT0[:, 2 * hf:2 * hf + 1], QS2[:, 1024 * slot:1024 * (slot + 1)], AX.X, ALU.add), "v")
        P.wait("scalar", semn, q1cnt[ih])
        if ih > 0:
            P.wait("scalar", "v", mark[("xm1v", ih - 1)])
        P.op("scalar", lambda e, sm, slot=slot: e.square(
            PBUF[:, 0:1024], QS2[:, 1024 * slot:1024 * (slot + 1)]), "s")
        mark[("xm1s", ih)] = P.count("s")
        P.wait("vector", "s", mark[("xm1s", ih)])
        P.drain("vector")
        P.op("vector", lambda e, sm, hf=hf: e.tensor_reduce(
            GT0[:, 2 * hf + 1:2 * hf + 2], PBUF[:, 0:1024], AX.X, ALU.add), "v")
        if hf == 1:
            P.drain("vector")
            P.op("vector", lambda e, sm, i=i: e.tensor_tensor(
                SUMC[:, i:i + 1], GT0[:, 0:1], GT0[:, 2:3], ALU.add), "v")
            P.drain("vector")
            P.op("vector", lambda e, sm, i=i: e.tensor_tensor(
                SQC[:, i:i + 1], GT0[:, 1:2], GT0[:, 3:4], ALU.add), "v")
        mark[("xm1v", ih)] = P.count("v")
        if ih + 2 < 16:
            _xm1_load(ih + 2)
    mark[("xmred", 7)] = mark[("xm1v", 15)]
    mark[("xmsq", 7)] = mark[("xm1s", 15)]

    P.wait("gpsimd", "v", mark[("xmred", 7)])
    P.dma("gpsimd", lambda e, sm: e.dma_start(
        out=cst_in[0:1, :].rearrange("o (j p) -> p (o j)", p=128),
        in_=SUMC[:, :], allow_slow_non_contiguous=True), "d_st")
    P.dma("gpsimd", lambda e, sm: e.dma_start(
        out=cst_in[1:2, :].rearrange("o (j p) -> p (o j)", p=128),
        in_=SQC[:, :], allow_slow_non_contiguous=True), "d_st")
    P.wait("gpsimd", "d_st")
    P.op("gpsimd", lambda e, sm: e.collective_compute(
        "AllGather", ALU.bypass, replica_groups=[[0, 1, 2, 3], [4, 5, 6, 7]],
        ins=[cst_in.opt()], outs=[cst_out.opt()]), "cc1")
    P.wait("gpsimd", "cc1")
    for i in range(4):
        P.dma("gpsimd", lambda e, sm, i=i: e.dma_start(
            out=GSUM[:, 8 * i:8 * (i + 1)],
            in_=cst_out[2 * i:2 * i + 1, :].rearrange("o (j p) -> p (o j)", p=128),
            allow_slow_non_contiguous=True), "d_st")
        P.dma("gpsimd", lambda e, sm, i=i: e.dma_start(
            out=GSQ[:, 8 * i:8 * (i + 1)],
            in_=cst_out[2 * i + 1:2 * i + 2, :].rearrange("o (j p) -> p (o j)", p=128),
            allow_slow_non_contiguous=True), "d_st")
    P.wait("vector", "d_st")
    P.op("vector", lambda e, sm: e.tensor_scalar(GM[:, :], GSUM[:, :], 1.0 / D, None, ALU.mult), "v")
    P.drain("vector")
    P.op("vector", lambda e, sm: e.tensor_mul(GT0[:, :], GM[:, :], GM[:, :]), "v")
    P.drain("vector")
    P.op("vector", lambda e, sm: e.tensor_scalar(GRS[:, :], GSQ[:, :], 1.0 / D, None, ALU.mult), "v")
    P.drain("vector")
    P.op("vector", lambda e, sm: e.tensor_sub(GRS[:, :], GRS[:, :], GT0[:, :]), "v")
    P.drain("vector")
    P.op("vector", lambda e, sm: e.tensor_scalar(GRS[:, :], GRS[:, :], EPS, None, ALU.add), "v")
    mark["grs_pre"] = P.count("v")
    P.wait("scalar", "v", mark["grs_pre"])
    P.op("scalar", lambda e, sm: e.sqrt(GT0[:, :], GRS[:, :]), "s")
    mark["grs_sqrt"] = P.count("s")
    P.wait("vector", "s", mark["grs_sqrt"])
    P.op("vector", lambda e, sm: e.reciprocal(GRS[:, :], GT0[:, :]), "v")
    P.drain("vector")
    P.op("vector", lambda e, sm: e.tensor_copy(GMB[:, :], GM[:, :]), "v")
    mark["gstats"] = P.count("v")
    P.wait("gpsimd", "v", mark["gstats"])
    P.dma("gpsimd", lambda e, sm: e.dma_start(
        out=m_row.rearrange("(j p) -> p j", p=128), in_=GMB[:, :],
        allow_slow_non_contiguous=True), "d_st")
    P.dma("gpsimd", lambda e, sm: e.dma_start(
        out=rs_row.rearrange("(j p) -> p j", p=128), in_=GRS[:, :],
        allow_slow_non_contiguous=True), "d_st")
    mark["rows_mrs"] = P.count("d_st")
    P.wait("gpsimd", "d_st")

    P.op("vector", lambda e, sm: e.tensor_scalar(MMC[:, :], SUMC[:, :], 1.0 / D, None, ALU.mult), "v")
    P.drain("vector")
    P.op("vector", lambda e, sm: e.tensor_mul(GT0[:, 0:8], MMC[:, :], MMC[:, :]), "v")
    P.drain("vector")
    P.op("vector", lambda e, sm: e.tensor_scalar(RSM[:, :], SQC[:, :], 1.0 / D, None, ALU.mult), "v")
    P.drain("vector")
    P.op("vector", lambda e, sm: e.tensor_sub(RSM[:, :], RSM[:, :], GT0[:, 0:8]), "v")
    P.drain("vector")
    P.op("vector", lambda e, sm: e.tensor_scalar(RSM[:, :], RSM[:, :], EPS, None, ALU.add), "v")
    mark["rsm_pre"] = P.count("v")
    P.wait("scalar", "v", mark["rsm_pre"])
    P.op("scalar", lambda e, sm: e.sqrt(GT0[:, 0:8], RSM[:, :]), "s")
    mark["rsm_sqrt"] = P.count("s")
    P.wait("vector", "s", mark["rsm_sqrt"])
    P.op("vector", lambda e, sm: e.reciprocal(RSM[:, :], GT0[:, 0:8]), "v")
    P.drain("vector")
    P.op("vector", lambda e, sm: e.tensor_copy(MQB[:, :], MMC[:, :]), "v")
    mark["minestats"] = P.count("v")

    QITEMS = []
    QLI = []

    def q_add(load, comp):
        QLI.append(None if load is None else sum(1 for x in QITEMS if x[0] is not None))
        QITEMS.append((load, comp))

    qcnt = {}
    qvmark = {}
    q_li_item = {}
    qemit = [0]
    qcp = [0]

    def _q_emit_load(k):
        load = QITEMS[k][0]
        if load is None:
            return
        li = QLI[k]
        q_li_item[li] = k
        slot = li % 2
        semn = "d_q0" if slot == 0 else "d_q1"
        prev_item = q_li_item.get(li - 2)
        if prev_item is not None:
            P.wait("scalar", "v", qvmark[prev_item])
        load(slot, semn)
        qcnt[k] = P.count(semn)

    def pump(n):
        for _ in range(n):
            k = qcp[0]
            if k >= len(QITEMS):
                return
            while qemit[0] < min(k + 2, len(QITEMS)):
                _q_emit_load(qemit[0])
                qemit[0] += 1
            load, comp = QITEMS[k]
            if load is not None:
                slot = QLI[k] % 2
                semn = "d_q0" if slot == 0 else "d_q1"
                P.wait("vector", semn, qcnt[k])
                comp(slot, semn)
            else:
                comp(None, None)
            qvmark[k] = P.count("v")
            qcp[0] += 1

    def pump_all():
        pump(len(QITEMS) - qcp[0])

    def q_xm2(i, hf):
        def load(slot, semn, i=i, hf=hf):
            if i == 0 and hf == 0:
                P.wait("scalar", "v", mark["minestats"])
            P.dma("scalar", lambda e, sm, i=i, hf=hf, slot=slot: e.dma_start(
                out=QS2[:, 1024 * slot:1024 * (slot + 1)],
                in_=XM[128 * i:128 * (i + 1), 1024 * hf:1024 * (hf + 1)]), semn)

        def comp(slot, semn, i=i, hf=hf):
            P.drain("vector")
            P.op("vector", lambda e, sm, i=i, slot=slot: e.tensor_scalar(
                QS2[:, 1024 * slot:1024 * (slot + 1)], QS2[:, 1024 * slot:1024 * (slot + 1)],
                MMC[:, i:i + 1], None, ALU.subtract), "v")
            P.drain("vector")
            P.op("vector", lambda e, sm, hf=hf, slot=slot: e.tensor_reduce(
                GT0[:, 8 + hf:9 + hf], QS2[:, 1024 * slot:1024 * (slot + 1)],
                AX.X, ALU.max, apply_absolute_value=True), "v")
            if hf == 1:
                P.drain("vector")
                P.op("vector", lambda e, sm, i=i: e.tensor_tensor(
                    AXC[:, i:i + 1], GT0[:, 8:9], GT0[:, 9:10], ALU.max), "v")
                mark[("axred", i)] = P.count("v")
        return load, comp

    def q_xscales():
        def comp(slot, semn):
            P.drain("vector")
            P.op("vector", lambda e, sm: e.tensor_mul(XSC[:, :], AXC[:, :], RSM[:, :]), "v")
            P.drain("vector")
            P.op("vector", lambda e, sm: e.tensor_scalar(
                XSC[:, :], XSC[:, :], QMAX_INV, 1e-12, ALU.mult, ALU.max), "v")
            P.drain("vector")
            P.op("vector", lambda e, sm: e.reciprocal(QSC[:, :], XSC[:, :]), "v")
            P.drain("vector")
            P.op("vector", lambda e, sm: e.tensor_mul(QSC[:, :], QSC[:, :], RSM[:, :]), "v")
            mark["qscales"] = P.count("v")
            P.wait("gpsimd", "v", mark["qscales"])
            P.dma("gpsimd", lambda e, sm: e.dma_start(
                out=qs_row.rearrange("(j p) -> p j", p=128), in_=QSC[:, :],
                allow_slow_non_contiguous=True), "d_st")
            P.dma("gpsimd", lambda e, sm: e.dma_start(
                out=xs_row.rearrange("(j p) -> p j", p=128), in_=XSC[:, :],
                allow_slow_non_contiguous=True), "d_st")
            P.dma("gpsimd", lambda e, sm: e.dma_start(
                out=mq_row.rearrange("(j p) -> p j", p=128), in_=MQB[:, :],
                allow_slow_non_contiguous=True), "d_st")
            P.wait("gpsimd", "d_st")
            P.dma("gpsimd", lambda e, sm: e.dma_start(
                out=BQS[:, :], in_=qs_row.rearrange("(o a) -> o a", o=1).broadcast_to([128, TOK])), "d_bc1")
            P.dma("gpsimd", lambda e, sm: e.dma_start(
                out=BMQ[:, :], in_=mq_row.rearrange("(o a) -> o a", o=1).broadcast_to([128, TOK])), "d_bc1")
            mark["bq_ready"] = P.count("d_bc1")
        return None, comp

    def q_fw(i, hf):
        def load(slot, semn, i=i, hf=hf):
            P.dma("scalar", lambda e, sm, i=i, hf=hf, slot=slot: e.dma_start(
                out=QS2[:, 1024 * slot:1024 * (slot + 1)],
                in_=FW[128 * i:128 * (i + 1), 1024 * hf:1024 * (hf + 1)]), semn)

        def comp(slot, semn, i=i, hf=hf):
            P.drain("vector")
            P.op("vector", lambda e, sm, hf=hf, slot=slot: e.tensor_reduce(
                GT0[:, 8 + hf:9 + hf], QS2[:, 1024 * slot:1024 * (slot + 1)],
                AX.X, ALU.max, apply_absolute_value=True), "v")
            if hf == 1:
                P.drain("vector")
                P.op("vector", lambda e, sm, i=i: e.tensor_tensor(
                    WMX[:, i:i + 1], GT0[:, 8:9], GT0[:, 9:10], ALU.max), "v")
                mark[("fwred", i)] = P.count("v")
        return load, comp

    def q_wscales():
        def comp(slot, semn):
            P.drain("vector")
            P.op("vector", lambda e, sm: e.tensor_scalar(
                WSC[:, :], WMX[:, :], 1e-8, QMAX_INV, ALU.max, ALU.mult), "v")
            P.drain("vector")
            P.op("vector", lambda e, sm: e.reciprocal(WRC[:, :], WSC[:, :]), "v")
            mark["wscales"] = P.count("v")
            P.wait("gpsimd", "v", mark["wscales"])
            P.dma("gpsimd", lambda e, sm: e.dma_start(
                out=rws_row.rearrange("(j p) -> p j", p=128), in_=WRC[:, :],
                allow_slow_non_contiguous=True), "d_st")
            P.wait("gpsimd", "d_st")
            P.dma("gpsimd", lambda e, sm: e.dma_start(
                out=BRWS[:, :], in_=rws_row.rearrange("(o a) -> o a", o=1).broadcast_to([128, 2048])), "d_bc3")
            mark["brws_ready"] = P.count("d_bc3")
        return None, comp

    def q_fwt(dc, h2):
        def load(slot, semn, dc=dc, h2=h2):
            P.dma("scalar", lambda e, sm, dc=dc, h2=h2, slot=slot: e.dma_start(
                out=QS2[:, 1024 * slot:1024 * (slot + 1)],
                in_=FWT[128 * dc:128 * (dc + 1), 1024 * h2:1024 * (h2 + 1)]), semn)

        def comp(slot, semn, dc=dc, h2=h2):
            if dc == 0 and h2 == 0:
                P.wait("vector", "d_bc3", mark["brws_ready"])
            if h2 == 0:
                prevsp = mark.get(("wqsp", dc - 1))
                if prevsp is not None:
                    P.wait("vector", "d_wqs", prevsp)
            P.drain("vector")
            P.op("vector", lambda e, sm, h2=h2, slot=slot: e.tensor_tensor(
                F32B[:, :], QS2[:, 1024 * slot:1024 * (slot + 1)],
                BRWS[:, 1024 * h2:1024 * (h2 + 1)], ALU.mult), "v")
            P.drain("vector")
            P.op("vector", lambda e, sm: e.tensor_scalar(
                F32B[:, :], F32B[:, :], MAGIC, MAGIC, ALU.add, ALU.subtract), "v")
            P.drain("vector")
            P.op("vector", lambda e, sm: e.tensor_copy(
                F32B[:, :].bitcast(I32), F32B[:, :].bitcast(I32)), "v")
            P.drain("vector")
            P.op("vector", lambda e, sm, h2=h2: e.tensor_copy(
                PBUF[:, 1024 * h2:1024 * (h2 + 1)], F32B[:, :]), "v")
            if h2 == 1:
                mark[("wqq", dc)] = P.count("v")
                P.wait("gpsimd", "v", mark[("wqq", dc)])
                P.dma("gpsimd", lambda e, sm, dc=dc: e.dma_start(
                    out=wq_spill[128 * dc:128 * (dc + 1), :], in_=PBUF[:, :]), "d_wqs")
                mark[("wqsp", dc)] = P.count("d_wqs")
        return load, comp

    def q_xq(dc):
        def load(slot, semn, dc=dc):
            P.dma("scalar", lambda e, sm, dc=dc, slot=slot: e.dma_start(
                out=QS2[:, 1024 * slot:1024 * (slot + 1)],
                in_=XTM[128 * dc:128 * (dc + 1), :]), semn)

        def comp(slot, semn, dc=dc):
            if dc == 0:
                P.wait("vector", "d_bc1", mark["bq_ready"])
                P.wait("vector", "d_wqs", mark[("wqsp", NDC - 1)])
            else:
                prevsp = mark.get(("xqsp", dc - 1))
                if prevsp is not None:
                    P.wait("vector", "d_xqs", prevsp)
            P.drain("vector")
            P.op("vector", lambda e, sm, slot=slot: e.tensor_tensor(
                F32B[:, :], QS2[:, 1024 * slot:1024 * (slot + 1)], BMQ[:, :], ALU.subtract), "v")
            P.drain("vector")
            P.op("vector", lambda e, sm: e.tensor_mul(F32B[:, :], F32B[:, :], BQS[:, :]), "v")
            P.drain("vector")
            P.op("vector", lambda e, sm: e.tensor_scalar(
                F32B[:, :], F32B[:, :], MAGIC, MAGIC, ALU.add, ALU.subtract), "v")
            P.drain("vector")
            P.op("vector", lambda e, sm: e.tensor_copy(
                F32B[:, :].bitcast(I32), F32B[:, :].bitcast(I32)), "v")
            P.drain("vector")
            P.op("vector", lambda e, sm: e.tensor_copy(PBUF[:, 0:1024], F32B[:, :]), "v")
            mark[("xqq", dc)] = P.count("v")
            P.wait("gpsimd", "v", mark[("xqq", dc)])
            for tc in range(NTC):
                P.dma("gpsimd", lambda e, sm, dc=dc, tc=tc: e.dma_start(
                    out=xq_spill[tc, :, 512 * dc:512 * (dc + 1)],
                    in_=PBUF[:, 512 * tc:512 * (tc + 1)]), "d_xqs")
            mark[("xqsp", dc)] = P.count("d_xqs")
        return load, comp

    def q_xqfin():
        def comp(slot, semn):
            mark["xq_done"] = P.count("d_xqs")
            P.wait("gpsimd", "v", mark[("xqq", NDC - 1)])
            P.dma("gpsimd", lambda e, sm: e.dma_start(
                out=BQS[:, :], in_=xs_row.rearrange("(o a) -> o a", o=1).broadcast_to([128, TOK])), "d_bc2")
            mark["bxs_ready"] = P.count("d_bc2")
        return None, comp

    for i in range(8):
        for hf in range(2):
            q_add(*q_xm2(i, hf))
    q_add(*q_xscales())
    for i in range(NOC):
        for hf in range(2):
            q_add(*q_fw(i, hf))
    q_add(*q_wscales())
    for dc in range(NDC):
        for h2 in range(2):
            q_add(*q_fwt(dc, h2))
    for dc in range(NDC):
        q_add(*q_xq(dc))
    q_add(*q_xqfin())

    for wi, (srcw, dst) in enumerate([(WK, WA), (WV, WB)]):
        for g in range(8):
            k = wi * 8 + g
            slot = k % 2
            semn = "d_xt0" if slot == 0 else "d_xt1"
            prev = mark.get(("wcast", k - 2))
            if prev is not None:
                P.wait("sync", "s", prev)
            P.dma("sync", lambda e, sm, srcw=srcw, g=g, slot=slot: e.dma_start(
                out=XTG[:, 1024 * slot:1024 * (slot + 1)].rearrange("p (d c) -> p d c", d=2),
                in_=srcw[256 * g:256 * (g + 1), :].rearrange("(d p) c -> p d c", p=128)), semn)
            P.wait("scalar", semn)
            P.op("scalar", lambda e, sm, dst=dst, g=g, slot=slot: e.copy(
                dst[:, 1024 * g:1024 * (g + 1)], XTG[:, 1024 * slot:1024 * (slot + 1)]), "s")
            mark[("wcast", k)] = P.count("s")
    mark["wkv_ready_s"] = P.count("s")

    P.wait("tensor", "v", v_ones)
    for _ in range(64):
        P.op("tensor", lambda e, sm: e.matmul(
            ffb[0][:, 0:128], ONES[:, :], ONES[:, 0:128], start=True, stop=True,
            skip_group_check=True), None)

    P.wait("tensor", "s", mark["wkv_ready_s"])
    for wi, (wsrc, bank) in enumerate([(WA, pvb_t[0]), (WB, pvb_t[1])]):
        for dc in range(NDC):
            P.op("tensor", lambda e, sm, wsrc=wsrc, bank=bank, dc=dc: e.matmul(
                bank[:, :], ONES[:, :], wsrc[:, 512 * dc:512 * (dc + 1)],
                start=(dc == 0), stop=(dc == NDC - 1)), "t" if dc == NDC - 1 else None)
        mark[("wsum", wi)] = P.count("t")
    for wi, (bank, dstrow) in enumerate([(pvb_t[0], NEGWK), (pvb_t[1], NEGWV)]):
        P.wait("vector", "t", mark[("wsum", wi)])
        P.op("vector", lambda e, sm, bank=bank, dstrow=dstrow: e.tensor_scalar(
            dstrow[0:1, :], bank[0:1, :], -1.0, None, ALU.mult), "v")
    mark["negw_kv"] = P.count("v")

    def xcbuf(tt):
        return XQR, (tt % 2) * 8192

    def sw1_produce(tt):
        slot = tt % 2
        brt = "d_brt0" if slot == 0 else "d_brt1"
        P.wait("sync", "d_st", mark["rows_mrs"])
        prev = mark.get(("kevac_g", (tt - 2) * H_LOC + H_LOC - 1))
        if prev is not None:
            P.wait("sync", "v", prev)
        prev_mm = mark.get(("sw1_mm", tt - 2))
        if prev_mm is not None:
            P.wait("sync", "t", prev_mm)
        P.dma("sync", lambda e, sm, tt=tt, slot=slot: e.dma_start(
            out=BRS[:, 512 * slot:512 * (slot + 1)],
            in_=rs_row[512 * tt:512 * (tt + 1)].rearrange("(o a) -> o a", o=1).broadcast_to([128, 512])), brt)
        P.dma("sync", lambda e, sm, tt=tt, slot=slot: e.dma_start(
            out=MROW2[0:1, 512 * slot:512 * (slot + 1)],
            in_=m_row[512 * tt:512 * (tt + 1)].rearrange("(o a) -> o a", o=1)), brt)
        mark[("brt", tt)] = P.count(brt)

        xcb, xcoff = xcbuf(tt)
        for g in range(8):
            kg = tt * 8 + g
            xslot = kg % 2
            xsem = "d_xt0" if xslot == 0 else "d_xt1"
            prev = mark.get(("xcast", kg - 2))
            if prev is not None:
                P.wait("sync", "s", prev)
            elif kg < 2:
                P.wait("sync", "s", mark["wkv_ready_s"])
            if g == 0:
                prev_mm = mark.get(("sw1_mm", tt - 2))
                if prev_mm is not None:
                    P.wait("sync", "t", prev_mm)
            P.dma("sync", lambda e, sm, tt=tt, g=g, xslot=xslot: e.dma_start(
                out=XTG[:, 1024 * xslot:1024 * (xslot + 1)].rearrange("p (d c) -> p d c", d=2),
                in_=XT[tt, 2 * g:2 * (g + 1)].rearrange("d p c -> p d c")), xsem)
            P.wait("scalar", xsem)
            P.op("scalar", lambda e, sm, xcb=xcb, xcoff=xcoff, g=g, xslot=xslot: e.copy(
                xcb[:, xcoff + 1024 * g:xcoff + 1024 * (g + 1)],
                XTG[:, 1024 * xslot:1024 * (xslot + 1)]), "s")
            mark[("xcast", kg)] = P.count("s")

    def sw1_compute(tt):
        slot = tt % 2
        xcb, xcoff = xcbuf(tt)
        if tt == 0:
            P.wait("tensor", "s", mark["wkv_ready_s"])
            P.wait("tensor", "v", mark["negw_kv"])
        for ec in range(H_LOC):
            bank = pvb_t[ec % 2]
            gk = tt * H_LOC + ec
            prev = mark.get(("kevac_g", gk - 2))
            if prev is not None:
                P.wait("tensor", "v", prev)
            if ec == 0:
                P.wait("tensor", "d_brt0" if slot == 0 else "d_brt1", mark[("brt", tt)])
            for dc in range(NDC):
                if dc == 0 and ec == 0:
                    P.wait("tensor", "s", mark[("xcast", tt * 8 + 7)])
                P.op("tensor", lambda e, sm, bank=bank, ec=ec, dc=dc, xcb=xcb, xcoff=xcoff: e.matmul(
                    bank[:, :],
                    WA[:, 512 * dc + 128 * ec:512 * dc + 128 * (ec + 1)],
                    xcb[:, xcoff + 512 * dc:xcoff + 512 * (dc + 1)],
                    start=(dc == 0), stop=False), None)
            P.op("tensor", lambda e, sm, bank=bank, ec=ec, slot=slot: e.matmul(
                bank[:, :], NEGWK[0:1, 128 * ec:128 * (ec + 1)],
                MROW2[0:1, 512 * slot:512 * (slot + 1)], start=False, stop=True), "t")
            mark[("kmm", tt, ec)] = P.count("t")
            P.wait("vector", "t", mark[("kmm", tt, ec)])
            if ec == 0:
                P.wait("vector", "d_brt0" if slot == 0 else "d_brt1", mark[("brt", tt)])
            P.op("vector", lambda e, sm, bank=bank, ec=ec, tt=tt, slot=slot: e.tensor_tensor(
                KT[:, S * ec + 512 * tt:S * ec + 512 * (tt + 1)],
                bank[:, :], BRS[:, 512 * slot:512 * (slot + 1)], ALU.mult), "v")
            mark[("kevac_g", tt * H_LOC + ec)] = P.count("v")

        for tc in range(4):
            bank = ffb[tc % 2]
            gk = tt * 4 + tc
            prev = mark.get(("vevac_g", gk - 2))
            if prev is not None:
                P.wait("tensor", "s", prev)
            for dc in range(NDC):
                P.op("tensor", lambda e, sm, bank=bank, tc=tc, dc=dc, xcb=xcb, xcoff=xcoff: e.matmul(
                    bank[:, :],
                    xcb[:, xcoff + 512 * dc + 128 * tc:xcoff + 512 * dc + 128 * (tc + 1)],
                    WB[:, 512 * dc:512 * (dc + 1)],
                    start=(dc == 0), stop=False), None)
            P.op("tensor", lambda e, sm, bank=bank, slot=slot, tc=tc: e.matmul(
                bank[:, :], MROW2[0:1, 512 * slot + 128 * tc:512 * slot + 128 * (tc + 1)],
                NEGWV[0:1, 0:E_LOC], start=False, stop=True), "t")
            mark[("vmm", tt, tc)] = P.count("t")
            ck = tt * 4 + tc
            P.wait("scalar", "t", mark[("vmm", tt, tc)])
            if ck == 0:
                P.wait("scalar", "v", mark["gstats"])
            P.op("scalar", lambda e, sm, bank=bank, ck=ck: e.activation(
                VT[:, 512 * ck:512 * (ck + 1)], bank[:, :], AF.Identity,
                bias=0.0, scale=GRS[:, ck:ck + 1]), "s")
            mark[("vevac_g", ck)] = P.count("s")
        mark[("sw1_mm", tt)] = P.count("t")
        pump(9)

    sw1_produce(0)
    for tt in range(NTT):
        if tt + 1 < NTT:
            sw1_produce(tt + 1)
        sw1_compute(tt)

    P.wait("sync", "t", mark[("sw1_mm", NTT - 1)])
    for g in range(8):
        slot = g % 2
        semn = "d_xt0" if slot == 0 else "d_xt1"
        prev = mark.get(("wqcast", g - 2))
        if prev is not None:
            P.wait("sync", "s", prev)
        elif g < 2:
            P.wait("sync", "s", mark[("xcast", NTT * 8 - 1)])
        P.dma("sync", lambda e, sm, g=g, slot=slot: e.dma_start(
            out=XTG[:, 1024 * slot:1024 * (slot + 1)].rearrange("p (d c) -> p d c", d=2),
            in_=WQ[256 * g:256 * (g + 1), :].rearrange("(d p) c -> p d c", p=128)), semn)
        P.wait("scalar", semn)
        if g == 0:
            P.wait("scalar", "t", mark[("sw1_mm", NTT - 1)])
        P.op("scalar", lambda e, sm, g=g, slot=slot: e.copy(
            WA[:, 1024 * g:1024 * (g + 1)], XTG[:, 1024 * slot:1024 * (slot + 1)]), "s")
        mark[("wqcast", g)] = P.count("s")
    mark["wq_cast_done"] = P.count("s")

    P.wait("tensor", "s", mark["wq_cast_done"])
    for dc in range(NDC):
        P.op("tensor", lambda e, sm, dc=dc: e.matmul(
            ffb[0][:, :], ONES[:, :], WA[:, 512 * dc:512 * (dc + 1)],
            start=(dc == 0), stop=(dc == NDC - 1)), "t" if dc == NDC - 1 else None)
    mark["wsumq"] = P.count("t")
    P.wait("vector", "t", mark["wsumq"])
    P.op("vector", lambda e, sm: e.tensor_scalar(
        NEGWQ[0:1, :], ffb[0][0:1, :], -1.0, None, ALU.mult), "v")
    mark["negwq"] = P.count("v")

    def sw2_produce(tt):
        xcb, xcoff = xcbuf(tt)
        for g in range(8):
            kg = tt * 8 + g
            xslot = kg % 2
            xsem = "d_xt0" if xslot == 0 else "d_xt1"
            prev = mark.get(("xcast2", kg - 2))
            if prev is not None:
                P.wait("sync", "s", prev)
            elif kg < 2:
                P.wait("sync", "s", mark["wq_cast_done"])
            if g == 0:
                prev_mm = mark.get(("qmm_all", tt - 2))
                if prev_mm is not None:
                    P.wait("sync", "t", prev_mm)
                else:
                    P.wait("sync", "t", mark[("sw1_mm", NTT - 1)])
            P.dma("sync", lambda e, sm, tt=tt, g=g, xslot=xslot: e.dma_start(
                out=XTG[:, 1024 * xslot:1024 * (xslot + 1)].rearrange("p (d c) -> p d c", d=2),
                in_=XT[tt, 2 * g:2 * (g + 1)].rearrange("d p c -> p d c")), xsem)
            P.wait("scalar", xsem)
            P.op("scalar", lambda e, sm, xcb=xcb, xcoff=xcoff, g=g, xslot=xslot: e.copy(
                xcb[:, xcoff + 1024 * g:xcoff + 1024 * (g + 1)],
                XTG[:, 1024 * xslot:1024 * (xslot + 1)]), "s")
            mark[("xcast2", kg)] = P.count("s")
        mark[("xcld", tt)] = P.count("s")
        slot = tt % 2
        brt = "d_brt0" if slot == 0 else "d_brt1"
        prev = mark.get(("qevac_g", (tt - 2) * H_LOC + 3))
        if prev is not None:
            P.wait("sync", "v", prev)
        else:
            P.wait("sync", "v", mark[("kevac_g", NTT * H_LOC - 1)])
        P.dma("sync", lambda e, sm, tt=tt, slot=slot: e.dma_start(
            out=BRS[:, 512 * slot:512 * (slot + 1)],
            in_=rs_row[512 * tt:512 * (tt + 1)].rearrange("(o a) -> o a", o=1).broadcast_to([128, 512])), brt)
        P.dma("sync", lambda e, sm, tt=tt, slot=slot: e.dma_start(
            out=MROW2[0:1, 512 * slot:512 * (slot + 1)],
            in_=m_row[512 * tt:512 * (tt + 1)].rearrange("(o a) -> o a", o=1)), brt)
        mark[("brt2", tt)] = P.count(brt)

    def sw2_compute(tt):
        xcb, xcoff = xcbuf(tt)
        slot = tt % 2
        for ec in range(H_LOC):
            bank = pvb_t[ec % 2]
            gk = tt * H_LOC + ec
            prev = mark.get(("qevac_g", gk - 2))
            if prev is not None:
                P.wait("tensor", "v", prev)
            elif gk < 2:
                P.wait("tensor", "v", mark["negwq"])
            if ec == 0:
                P.wait("tensor", "d_brt0" if slot == 0 else "d_brt1", mark[("brt2", tt)])
            for dc in range(NDC):
                if dc == 0 and ec == 0:
                    P.wait("tensor", "s", mark[("xcld", tt)])
                P.op("tensor", lambda e, sm, bank=bank, ec=ec, dc=dc, xcb=xcb, xcoff=xcoff: e.matmul(
                    bank[:, :],
                    WA[:, 512 * dc + 128 * ec:512 * dc + 128 * (ec + 1)],
                    xcb[:, xcoff + 512 * dc:xcoff + 512 * (dc + 1)],
                    start=(dc == 0), stop=False), None)
            P.op("tensor", lambda e, sm, bank=bank, ec=ec, slot=slot: e.matmul(
                bank[:, :], NEGWQ[0:1, 128 * ec:128 * (ec + 1)],
                MROW2[0:1, 512 * slot:512 * (slot + 1)], start=False, stop=True), "t")
            mark[("qmm", tt, ec)] = P.count("t")
            slot = tt % 2
            brt = "d_brt0" if slot == 0 else "d_brt1"
            P.wait("vector", "t", mark[("qmm", tt, ec)])
            if ec == 0:
                P.wait("vector", brt, mark[("brt2", tt)])
            P.op("vector", lambda e, sm, bank=bank, ec=ec, tt=tt, slot=slot: e.tensor_tensor(
                QT[:, S * ec + 512 * tt:S * ec + 512 * (tt + 1)],
                bank[:, :], BRS[:, 512 * slot:512 * (slot + 1)], ALU.mult), "v")
            mark[("qevac_g", tt * H_LOC + ec)] = P.count("v")
        mark[("qmm_all", tt)] = P.count("t")
        pump(6)

    sw2_produce(0)
    sw2_produce(1)
    for tt in range(NTT):
        sw2_compute(tt)
        if tt + 2 < NTT:
            sw2_produce(tt + 2)
    pump_all()
    mark["sweep2_done_t"] = P.count("t")
    mark["sweep2_done_v"] = P.count("v")

    def emit_xq_load(tc, slot):
        semn = "d_xq0" if slot == 0 else "d_xq1"
        P.dma("sync", lambda e, sm, tc=tc, slot=slot: e.dma_start(
            out=XQR[:, 8192 * slot:8192 * (slot + 1)], in_=xq_spill[tc]), semn)
        return P.count(semn)

    def emit_wq_load(blk):
        slot = blk % 2
        semn = f"d_wq{slot}"
        P.dma("scalar", lambda e, sm, blk=blk, slot=slot: e.dma_start(
            out=WB[:, 4096 * slot:4096 * (slot + 1)].rearrange("p (d c) -> p d c", d=16),
            in_=wq_spill[:, 256 * blk:256 * (blk + 1)].rearrange("(d p) c -> p d c", p=128)), semn)
        return P.count(semn)

    P.wait("sync", "d_xqs", mark["xq_done"])
    P.wait("sync", "t", mark["sweep2_done_t"])
    mark[("xqld", 0)] = emit_xq_load(0, 0)
    P.wait("scalar", "d_wqs", mark[("wqsp", NDC - 1)])
    P.wait("scalar", "t", mark["sweep2_done_t"])
    wq_marks = {}
    for blk in range(2):
        wq_marks[blk] = emit_wq_load(blk)

    BDEN = [(BRWS, 0), (BRWS, 512)]
    FFT1 = (BRWS, 1024)
    GST = (BRWS, 1536)

    n_it = H_LOC * NQT
    NP = NKC // 2

    P.wait("scalar", "v", mark[("xqq", NDC - 1)])
    P.wait("scalar", "d_wqs", mark[("wqsp", NDC - 1)])
    P.wait("scalar", "d_xqs", mark["xq_done"])
    P.wait("scalar", "d_bc1", mark["bq_ready"])
    P.wait("scalar", "d_bc2", mark["bxs_ready"])
    P.wait("vector", "d_bc2", mark["bxs_ready"])
    P.wait("vector", "d_xqs", mark["xq_done"])
    P.wait("tensor", "v", v_ones)

    pend_den = [None]
    for it in range(n_it):
        h, qt = divmod(it, NQT)
        f_tc, f_oc = divmod(it, NOC)
        fbank = ffb[it % 2]
        pvb = pvb_t[it % 2]

        if it == 8:
            mark[("xqld", 1)] = emit_xq_load(1, 1)

        ffn_dc = 0
        for p in range(NP + 2):
            if p >= 2:
                pp = p - 2
                for half in range(2):
                    kcp = 2 * pp + half
                    if kcp == 0:
                        prev = mark.get(("aoevac", it - 2))
                        if prev is not None:
                            P.wait("tensor", "v", prev)
                        elif it < 2:
                            P.wait("tensor", "v", mark["sweep2_done_v"])
                    P.wait("tensor", "s", mark[("exp", it, pp)])
                    P.op("tensor", lambda e, sm, h=h, kcp=kcp, pvb=pvb, pp=pp, half=half: e.matmul(
                        pvb[:, :],
                        VT[:, 512 * kcp + 128 * h:512 * kcp + 128 * (h + 1)],
                        PBUF[:, 1024 * (pp % 2) + 512 * half:1024 * (pp % 2) + 512 * (half + 1)],
                        start=(kcp == 0), stop=(kcp == NKC - 1), skip_group_check=True),
                        "t" if kcp == NKC - 1 else None)
                    if kcp == NKC - 1:
                        mark[("pvmm", it)] = P.count("t")
                if ffn_dc < NDC:
                    dc = ffn_dc
                    ffn_dc += 1
                    gb = it // 2
                    wslot2 = gb % 2
                    if dc == 0:
                        P.wait("tensor", f"d_wq{wslot2}", wq_marks[gb])
                        P.wait("tensor", "d_xq0" if f_tc == 0 else "d_xq1", mark[("xqld", f_tc)])
                        prevf = mark.get(("recip", it - 2))
                        if prevf is not None:
                            P.wait("tensor", "v", prevf)
                        elif it < 2:
                            P.wait("tensor", "s", mark[("vevac_g", NTT * 4 - 1)])
                    P.op("tensor", lambda e, sm, f_oc=f_oc, f_tc=f_tc, dc=dc, fbank=fbank, wslot2=wslot2: e.matmul(
                        fbank[:, :],
                        WB[:, 4096 * wslot2 + 256 * dc + 128 * (f_oc % 2):4096 * wslot2 + 256 * dc + 128 * (f_oc % 2) + 128],
                        XQR[:, 8192 * f_tc + 512 * dc:8192 * f_tc + 512 * (dc + 1)],
                        start=(dc == 0), stop=(dc == NDC - 1), skip_group_check=True),
                        "t" if dc == NDC - 1 else None)
                    if dc == NDC - 1:
                        mark[("ffnmm", it)] = P.count("t")
            if p < NP:
                scb = scp[p % 2]
                prev = mark.get(("exp", it, p - 2))
                if prev is None:
                    prev = mark.get(("exp", it - 1, NP - 2 + p))
                if prev is not None:
                    P.wait("tensor", "s", prev)
                for half in range(2):
                    kc = 2 * p + half
                    P.op("tensor", lambda e, sm, h=h, qt=qt, kc=kc, scb=scb, half=half: e.matmul(
                        scb[:, 512 * half:512 * (half + 1)],
                        KT[:, S * h + 128 * kc:S * h + 128 * (kc + 1)],
                        QT[:, S * h + 512 * qt:S * h + 512 * (qt + 1)],
                        start=True, stop=True, skip_group_check=True), "t")
                mark[("sc", it, p)] = P.count("t")
                if p == 0 and pend_den[0] is not None:
                    pend_den[0]()
                    pend_den[0] = None
                    if it == 16:
                        P.wait("gpsimd", "d_ao0")
                        P.wait("gpsimd", "d_ao1")
                        P.op("gpsimd", lambda e, sm: e.collective_compute(
                            "AllGather", ALU.bypass, replica_groups=[[0, 1, 2, 3], [4, 5, 6, 7]],
                            ins=[cc_in[0:256, :].opt()], outs=[cc_out1.opt()]), "cc2")
                P.wait("scalar", "t", mark[("sc", it, p)])
                prevv = mark.get(("add", it, p - 2)) if p >= 2 else mark.get(("add", it - 1, NP - 1))
                if prevv is not None:
                    P.wait("scalar", "v", prevv)
                P.op("scalar", lambda e, sm, p=p, scb=scb: e.activation(
                    PBUF[:, 1024 * (p % 2):1024 * (p % 2 + 1)], scb[:, :],
                    AF.Exp, scale=SCL_ATTN), "s")
                mark[("exp", it, p)] = P.count("s")
                P.wait("vector", "s", mark[("exp", it, p)])
                if p == 0:
                    prevd = mark.get(("denmm", it - 1))
                    if prevd is not None:
                        P.wait("vector", "t", prevd)
                    P.op("vector", lambda e, sm, p=p: e.tensor_copy(
                        PSAC[:, :], PBUF[:, 1024 * (p % 2):1024 * (p % 2 + 1)]), "v")
                else:
                    P.drain("vector")
                    P.op("vector", lambda e, sm, p=p: e.tensor_tensor(
                        PSAC[:, :], PSAC[:, :], PBUF[:, 1024 * (p % 2):1024 * (p % 2 + 1)],
                        ALU.add), "v")
                mark[("add", it, p)] = P.count("v")

        P.drain("vector")
        P.op("vector", lambda e, sm: e.tensor_tensor(
            PSAC[:, 0:512], PSAC[:, 0:512], PSAC[:, 512:1024], ALU.add), "v")
        mark[("fold", it)] = P.count("v")
        def emit_den(it=it, fbank=fbank, pvb=pvb, h=h, qt=qt):
            P.wait("tensor", "v", mark[("fold", it)])
            P.wait("tensor", "v", mark[("ffn_deq", it)])
            P.op("tensor", lambda e, sm, fbank=fbank: e.matmul(
                fbank[:, :], ONES[:, :], PSAC[:, 0:512], start=True, stop=True,
                skip_group_check=True), "t")
            mark[("denmm", it)] = P.count("t")
            db, doff = BDEN[it % 2]
            P.wait("vector", "t", mark[("denmm", it)])
            prev = mark.get(("aoevac", it - 2))
            if prev is not None:
                P.wait("vector", "v", prev)
            elif it == 0:
                P.wait("vector", "d_bc3", mark["brws_ready"])
                P.wait("vector", "v", mark[("wqq", NDC - 1)])
            P.op("vector", lambda e, sm, db=db, doff=doff, fbank=fbank: e.reciprocal(
                db[:, doff:doff + 512], fbank[:, :]), "v")
            mark[("recip", it)] = P.count("v")
            P.drain("vector")
            P.wait("vector", "t", mark[("pvmm", it)])
            prev = mark.get(("aodma", it - 2))
            if prev is not None:
                P.wait("vector", "d_ao0" if it % 2 == 0 else "d_ao1", prev)
            P.op("vector", lambda e, sm, it=it, db=db, doff=doff, pvb=pvb: e.tensor_tensor(
                AOB[:, 512 * (it % 2):512 * (it % 2 + 1)],
                pvb[:, :], db[:, doff:doff + 512], ALU.mult), "v")
            mark[("aoevac", it)] = P.count("v")
            P.wait("gpsimd", "v", mark[("aoevac", it)])
            P.dma("gpsimd", lambda e, sm, it=it, h=h, qt=qt: e.dma_start(
                out=cc_in[128 * h:128 * (h + 1), 512 * qt:512 * (qt + 1)],
                in_=AOB[:, 512 * (it % 2):512 * (it % 2 + 1)]), "d_ao0" if it % 2 == 0 else "d_ao1")
            mark[("aodma", it)] = P.count("d_ao0" if it % 2 == 0 else "d_ao1")
        pend_den[0] = emit_den

        t1b, t1off = FFT1
        gb, goff = GST
        P.wait("vector", "t", mark[("ffnmm", it)])
        prevg = mark.get(("gdma", it - 1))
        if prevg is not None:
            P.wait("vector", "d_gs", prevg)
        P.op("vector", lambda e, sm, t1b=t1b, t1off=t1off, fbank=fbank, f_tc=f_tc: e.tensor_tensor(
            t1b[:, t1off:t1off + 512], fbank[:, :],
            BQS[:, 512 * f_tc:512 * (f_tc + 1)], ALU.mult), "v")
        mark[("ffn_deq", it)] = P.count("v")
        P.wait("scalar", "v", mark[("ffn_deq", it)])
        P.op("scalar", lambda e, sm, t1b=t1b, t1off=t1off, gb=gb, goff=goff, f_oc=f_oc: e.activation(
            gb[:, goff:goff + 512], t1b[:, t1off:t1off + 512], AF.Silu,
            bias=BIASC[:, f_oc:f_oc + 1], scale=WSC[:, f_oc:f_oc + 1]), "s")
        mark[("silu", it)] = P.count("s")
        P.wait("gpsimd", "s", mark[("silu", it)])
        P.dma("gpsimd", lambda e, sm, f_oc=f_oc, f_tc=f_tc, gb=gb, goff=goff: e.dma_start(
            out=g_spill[128 * f_oc:128 * (f_oc + 1), 512 * f_tc:512 * (f_tc + 1)],
            in_=gb[:, goff:goff + 512]), "d_gs")
        mark[("gdma", it)] = P.count("d_gs")

        if it % 2 == 1:
            gb_next = it // 2 + 2
            if gb_next < 16:
                P.wait("scalar", "v", mark[("ffn_deq", it)])
                wq_marks[gb_next] = emit_wq_load(gb_next % 8)


    wo_cnt = [0]

    def emit_wo_load(ec, gate_t):
        wslot = ec % 2
        for hf in range(2):
            idx = wo_cnt[0]
            wo_cnt[0] += 1
            xs2 = idx % 2
            wsem = "d_xt0" if xs2 == 0 else "d_xt1"
            prevx = mark.get(("wocastg", idx - 2))
            if prevx is not None:
                P.wait("sync", "s", prevx)
            else:
                P.wait("sync", "s", mark["wq_cast_done"])
                P.wait("sync", "s", mark[("xcast2", NTT * 8 - 1)])
                P.wait("sync", "t", mark["sweep2_done_t"])
            P.dma("sync", lambda e, sm, ec=ec, hf=hf, xs2=xs2: e.dma_start(
                out=XTG[:, 1024 * xs2:1024 * (xs2 + 1)],
                in_=WOT[ec][:, 8 * hf:8 * (hf + 1), :].rearrange("p d c -> p (d c)")), wsem)
            P.wait("scalar", wsem)
            if hf == 0:
                P.wait("scalar", "t", gate_t)
            P.op("scalar", lambda e, sm, wslot=wslot, hf=hf, xs2=xs2: e.copy(
                WA[:, 2048 * wslot + 1024 * hf:2048 * wslot + 1024 * (hf + 1)],
                XTG[:, 1024 * xs2:1024 * (xs2 + 1)]), "s")
            mark[("wocastg", idx)] = P.count("s")
        mark[("wocast", ec)] = P.count("s")

    if pend_den[0] is not None:
        pend_den[0]()
        pend_den[0] = None

    emit_wo_load(0, mark["sweep2_done_t"])
    emit_wo_load(1, mark["sweep2_done_t"])

    P.wait("gpsimd", "d_ao0")
    P.wait("gpsimd", "d_ao1")
    P.op("gpsimd", lambda e, sm: e.collective_compute(
        "AllGather", ALU.bypass, replica_groups=[[0, 1, 2, 3], [4, 5, 6, 7]],
        ins=[cc_in[256:512, :].opt()], outs=[cc_out2.opt()]), "cc2b")
    P.dma("gpsimd", lambda e, sm: e.dma_start(out=SOFF[:, :], in_=COFF[:, :]), "d_rb")
    P.wait("gpsimd", "d_rb")
    P.wait("gpsimd", "cc2")
    holder = {}

    def _alloc_reg(e, sm):
        holder["roff"] = e.alloc_register("roff")
        return e.nop()
    P.op("gpsimd", _alloc_reg, None)
    done_p2_wait = [False]
    ec_order = [ec for ec in range(NOC) if (ec % 4) // 2 == 0] + \
               [ec for ec in range(NOC) if (ec % 4) // 2 == 1]
    for ec in ec_order:
        i, hh = divmod(ec, 4)
        part = hh // 2
        src_t = cc_out1_t if part == 0 else cc_out2_t
        if part == 1 and not done_p2_wait[0]:
            P.wait("gpsimd", "cc2b")
            done_p2_wait[0] = True
        row = 256 * i + 128 * (hh % 2)
        P.raw("gpsimd", lambda e, sm: e.reg_load(holder["roff"], SOFF[0:1, 0:1]))
        P.raw("gpsimd", lambda e, sm, row=row: e.reg_add(holder["roff"], holder["roff"], row * S))
        P.dma("gpsimd", lambda e, sm, ec=ec, src_t=src_t: e.dma_start(
            out=KT[:, TOK * ec:TOK * (ec + 1)],
            in_=bass.AP(src_t, holder["roff"], [[S, 128], [1, TOK]])), "d_rb")
    mark["rb_done"] = P.count("d_rb")

    for gi in range(NOC * NTC):
        ec, tc = divmod(gi, NTC)
        bank = pvb_t[gi % 2]
        wslot = ec % 2

        gsem = "d_g0" if gi % 2 == 0 else "d_g1"
        xsem = "d_xe0" if gi % 2 == 0 else "d_xe1"
        prev = mark.get(("ep_done", gi - 2))
        if prev is not None:
            P.wait("scalar", "v", prev)
        else:
            P.wait("scalar", "d_gs", mark[("gdma", n_it - 1)])
        P.dma("scalar", lambda e, sm, ec=ec, tc=tc, gi=gi: e.dma_start(
            out=F32A[:, 512 * (gi % 2):512 * (gi % 2 + 1)],
            in_=g_spill[128 * ec:128 * (ec + 1), 512 * tc:512 * (tc + 1)]), gsem)
        P.dma("scalar", lambda e, sm, ec=ec, tc=tc, gi=gi: e.dma_start(
            out=BRS[:, 512 * (gi % 2):512 * (gi % 2 + 1)],
            in_=XTM[128 * ec:128 * (ec + 1), 512 * tc:512 * (tc + 1)]), xsem)
        mark[("epld", gi)] = (P.count(gsem), P.count(xsem))

        for dc in range(NOC):
            if dc == 0:
                P.wait("tensor", "d_rb", mark["rb_done"])
                P.wait("tensor", "s", mark[("wocast", ec)])
                prevb = mark.get(("ep_done", gi - 2))
                if prevb is not None:
                    P.wait("tensor", "v", prevb)
            P.op("tensor", lambda e, sm, wslot=wslot, dc=dc, tc=tc, bank=bank: e.matmul(
                bank[:, :],
                WA[:, 2048 * wslot + 128 * dc:2048 * wslot + 128 * (dc + 1)],
                KT[:, TOK * dc + 512 * tc:TOK * dc + 512 * (tc + 1)],
                start=(dc == 0), stop=(dc == NOC - 1), skip_group_check=True),
                "t" if dc == NOC - 1 else None)
        mark[("womm", gi)] = P.count("t")
        if tc == NTC - 1 and ec + 2 < NOC:
            emit_wo_load(ec + 2, mark[("womm", gi)])

        P.wait("vector", "t", mark[("womm", gi)])
        P.wait("vector", gsem, mark[("epld", gi)][0])
        P.wait("vector", xsem, mark[("epld", gi)][1])
        prevo = mark.get(("outdma", gi - 2))
        if prevo is not None:
            P.wait("vector", "d_out0" if gi % 2 == 0 else "d_out1", prevo)
        P.op("vector", lambda e, sm, gi=gi, bank=bank: e.scalar_tensor_tensor(
            F32B[:, 512 * (gi % 2):512 * (gi % 2 + 1)],
            bank[:, :], 1.0,
            F32A[:, 512 * (gi % 2):512 * (gi % 2 + 1)], ALU.mult, ALU.mult), "v")
        P.drain("vector")
        P.op("vector", lambda e, sm, gi=gi: e.tensor_tensor(
            F32B[:, 512 * (gi % 2):512 * (gi % 2 + 1)],
            F32B[:, 512 * (gi % 2):512 * (gi % 2 + 1)],
            BRS[:, 512 * (gi % 2):512 * (gi % 2 + 1)], ALU.add), "v")
        mark[("ep_done", gi)] = P.count("v")
        P.wait("gpsimd", "v", mark[("ep_done", gi)])
        P.dma("gpsimd", lambda e, sm, ec=ec, tc=tc, gi=gi: e.dma_start(
            out=OUT[128 * ec:128 * (ec + 1), 512 * tc:512 * (tc + 1)],
            in_=F32B[:, 512 * (gi % 2):512 * (gi % 2 + 1)]), "d_out0" if gi % 2 == 0 else "d_out1")
        mark[("outdma", gi)] = P.count("d_out0" if gi % 2 == 0 else "d_out1")

    P.wait("gpsimd", "d_out0")
    P.wait("gpsimd", "d_out1")

    with ExitStack() as ses:
        sems = {n: ses.enter_context(nc.semaphore(n)) for n in sem_names}
        with nc.Block() as block:
            @block.sync
            def _(sync):
                P.replay("sync", sync, sems)

            @block.tensor
            def _(tensor):
                P.replay("tensor", tensor, sems)

            @block.vector
            def _(vector):
                P.replay("vector", vector, sems)

            @block.scalar
            def _(scalar):
                P.replay("scalar", scalar, sems)

            @block.gpsimd
            def _(gpsimd):
                with allow_ici_all_to_all():
                    P.replay("gpsimd", gpsimd, sems)
    es.close()
    return nc


_NC_CACHE = None


def kernel(L0_attn_x, L0_attn_wq, L0_attn_wk, L0_attn_wv, L0_attn_wo,
           L0_ffn_weight, L0_ffn_bias):
    global _NC_CACHE
    x = np.asarray(L0_attn_x, dtype=np.float32)
    wq = np.asarray(L0_attn_wq, dtype=np.float32)
    wk = np.asarray(L0_attn_wk, dtype=np.float32)
    wv = np.asarray(L0_attn_wv, dtype=np.float32)
    wo = np.asarray(L0_attn_wo, dtype=np.float32)
    fw = np.asarray(L0_ffn_weight, dtype=np.float32)
    fb = np.asarray(L0_ffn_bias, dtype=np.float32)

    C = np.ascontiguousarray
    wot = C(wo.T.reshape(NDC, 128, NOC, 128).transpose(2, 1, 0, 3))
    fwt = C(fw.T)
    fw_c = C(fw)
    in_maps = []
    for c in range(8):
        b, g = divmod(c, 4)
        sl = slice(TOK * g, TOK * (g + 1))
        es0 = slice(E_LOC * g, E_LOC * (g + 1))
        xb = x[b]
        xtr = np.ascontiguousarray(
            xb.T.reshape(NDC, 128, NTT, 512).transpose(2, 0, 1, 3))
        in_maps.append({
            "XT": xtr,
            "XTM": C(xb[sl].T),
            "XM": C(xb[sl]),
            "WQ": C(wq[es0].T),
            "WK": C(wk[es0].T),
            "WV": C(wv[es0].T),
            "WOT": wot,
            "FW": fw_c,
            "FWT": fwt,
            "FB": fb,
            "COFF": np.array([[TOK * g]], dtype=np.uint32),
        })

    if _NC_CACHE is None:
        _NC_CACHE = build_nc()
    nc = _NC_CACHE
    res = run_bass_kernel_spmd(nc, in_maps, core_ids=list(range(8)))

    out = np.empty((B, S, D), dtype=np.float32)
    for c in range(8):
        b, g = divmod(c, 4)
        out[b, TOK * g:TOK * (g + 1), :] = np.asarray(res.results[c]["OUT"]).T
    return out


# version 10
# speedup vs baseline: 1.0888x; 1.0513x over previous
import numpy as np
import ml_dtypes
from contextlib import ExitStack

import concourse.bass as bass
import concourse.mybir as mybir
from concourse.bass_utils import run_bass_kernel_spmd
from concourse.aot_env import allow_ici_all_to_all

F32 = mybir.dt.float32
BF16 = mybir.dt.bfloat16
I32 = mybir.dt.int32
U32 = mybir.dt.uint32
AF = mybir.ActivationFunctionType
ALU = mybir.AluOpType
AX = mybir.AxisListType

D = 2048
S = 4096
B = 2
H_LOC = 4
DK = 128
E_LOC = H_LOC * DK
TOK = 1024
NDC = D // 128
NTT = S // 512
NKC = S // 128
NQT = S // 512
NOC = D // 128
NTC = TOK // 512

EPS = 1e-5
SCL_ATTN = float(1.0 / np.sqrt(np.float32(DK)))
MAGIC = float(np.float32(3 * 2 ** 22))
QMAX_INV = float(np.float32(1.0 / 127.0))

ENGINES = ("sync", "tensor", "vector", "scalar", "gpsimd")


class Prog:
    def __init__(self):
        self.ops = {e: [] for e in ENGINES}
        self.cnt = {}

    def new_sem(self, name):
        self.cnt[name] = 0

    def raw(self, eng, fn):
        self.ops[eng].append(("raw", fn, None, 0))

    def op(self, eng, fn, inc_sem=None, inc_by=1):
        if inc_sem is not None:
            self.cnt[inc_sem] += inc_by
        self.ops[eng].append(("op", fn, inc_sem, inc_by))

    def dma(self, eng, fn, sem):
        self.op(eng, fn, inc_sem=sem, inc_by=16)

    def wait(self, eng, sem, val=None):
        v = self.cnt[sem] if val is None else val
        self.ops[eng].append(("wait", None, sem, v))

    def drain(self, eng):
        self.ops[eng].append(("drain", None, None, 0))

    def count(self, sem):
        return self.cnt[sem]

    def replay(self, eng_name, eng_handle, sems):
        for kind, fn, sem, val in self.ops[eng_name]:
            if kind == "wait":
                eng_handle.wait_ge(sems[sem], val)
            elif kind == "drain":
                eng_handle.drain()
            elif kind == "raw":
                fn(eng_handle, sems)
            else:
                inst = fn(eng_handle, sems)
                if sem is not None:
                    inst.then_inc(sems[sem], val)


def build_nc():
    nc = bass.Bass("TRN2", target_bir_lowering=False, debug=False)

    XT = nc.dram_tensor("XT", [NTT, NDC, 128, 512], F32, kind="ExternalInput").ap()
    XTM = nc.dram_tensor("XTM", [D, TOK], F32, kind="ExternalInput").ap()
    XM = nc.dram_tensor("XM", [TOK, D], F32, kind="ExternalInput").ap()
    WQ = nc.dram_tensor("WQ", [D, E_LOC], F32, kind="ExternalInput").ap()
    WK = nc.dram_tensor("WK", [D, E_LOC], F32, kind="ExternalInput").ap()
    WV = nc.dram_tensor("WV", [D, E_LOC], F32, kind="ExternalInput").ap()
    WOT = nc.dram_tensor("WOT", [NOC, 128, NDC, 128], F32, kind="ExternalInput").ap()
    FW = nc.dram_tensor("FW", [D, D], F32, kind="ExternalInput").ap()
    FWT = nc.dram_tensor("FWT", [D, D], F32, kind="ExternalInput").ap()
    FB = nc.dram_tensor("FB", [D], F32, kind="ExternalInput").ap()
    COFF = nc.dram_tensor("COFF", [1, 1], U32, kind="ExternalInput").ap()
    OUT = nc.dram_tensor("OUT", [D, TOK], F32, kind="ExternalOutput").ap()

    xc_spill = nc.dram_tensor("xc_spill", [NTT, 128, NDC * 512], BF16).ap()
    wq_spill = nc.dram_tensor("wq_spill", [D, D], BF16).ap()
    xq_spill = nc.dram_tensor("xq_spill", [NTC, 128, NDC * 512], BF16).ap()
    g_spill = nc.dram_tensor("g_spill", [D, TOK], F32).ap()
    m_row = nc.dram_tensor("m_row", [S], BF16).ap()
    rs_row = nc.dram_tensor("rs_row", [S], F32).ap()
    rws_row = nc.dram_tensor("rws_row", [D], F32).ap()
    qs_row = nc.dram_tensor("qs_row", [TOK], F32).ap()
    xs_row = nc.dram_tensor("xs_row", [TOK], F32).ap()
    mq_row = nc.dram_tensor("mq_row", [TOK], BF16).ap()
    cst_in = nc.dram_tensor("cst_in", [2, TOK], F32).ap()
    cst_out = nc.dram_tensor("cst_out", [8, TOK], F32).ap()
    cc_in = nc.dram_tensor("cc_in", [E_LOC, S], BF16).ap()
    cc_out1_t = nc.dram_tensor("cc_out1", [4 * 256, S], BF16)
    cc_out1 = cc_out1_t.ap()
    cc_out2_t = nc.dram_tensor("cc_out2", [4 * 256, S], BF16)
    cc_out2 = cc_out2_t.ap()

    es = ExitStack()
    SB = lambda name, shape, dt: es.enter_context(nc.sbuf_tensor(name, shape, dt))

    KT = SB("KT", [128, H_LOC * S], BF16)
    QT = SB("QT", [128, H_LOC * S], BF16)
    VT = SB("VT", [128, NKC * E_LOC], BF16)
    XQR = SB("XQR", [128, 2 * NDC * 512], BF16)
    WA = SB("WA", [128, NDC * E_LOC], BF16)
    WB = SB("WB", [128, NDC * E_LOC], BF16)
    F32B = SB("F32B", [128, 1024], F32)
    QS2 = SB("QS2", [128, 2 * 1024], F32)
    F32A = QS2
    XTG = SB("XTG", [128, 2 * 1024], F32)
    BRS = SB("BRS", [128, 2 * 512], F32)
    BMT = SB("BMT", [128, 2 * 512], BF16)
    MROW2 = BMT
    NEGWK = SB("NEGWK", [1, E_LOC], BF16)
    NEGWV = SB("NEGWV", [1, E_LOC], BF16)
    NEGWQ = NEGWK
    BQS = SB("BQS", [128, TOK], F32)
    BMQ = SB("BMQ", [128, TOK], BF16)
    BRWS = SB("BRWS", [128, 2048], F32)
    PBUF = SB("PBUF", [128, 2048], BF16)
    ONES = SB("ONES", [128, 128], BF16)
    SUMC = SB("SUMC", [128, 8], F32)
    SQC = SB("SQC", [128, 8], F32)
    MMC = SB("MMC", [128, 8], F32)
    RSM = SB("RSM", [128, 8], F32)
    AXC = SB("AXC", [128, 8], F32)
    XSC = SB("XSC", [128, 8], F32)
    QSC = SB("QSC", [128, 8], F32)
    GSUM = SB("GSUM", [128, 32], F32)
    GSQ = SB("GSQ", [128, 32], F32)
    GM = SB("GM", [128, 32], F32)
    GMB = SB("GMB", [128, 32], BF16)
    GRS = SB("GRS", [128, 32], F32)
    GT0 = SB("GT0", [128, 32], F32)
    WMX = SB("WMX", [128, 16], F32)
    WSC = SB("WSC", [128, 16], F32)
    WRC = SB("WRC", [128, 16], F32)
    BIASC = SB("BIASC", [128, 16], F32)
    MQB = SB("MQB", [128, 8], BF16)
    PSAC = BMT
    AOB = BMQ
    SOFF = SB("SOFF", [1, 1], U32)

    scp = [es.enter_context(nc.psum_tensor(f"scp{i}", [128, 1024], F32)) for i in range(2)]
    pvb_t = [es.enter_context(nc.psum_tensor(f"pv{i}", [128, 512], F32)) for i in range(2)]
    ffb = [es.enter_context(nc.psum_tensor(f"ffn{i}", [128, 512], F32)) for i in range(2)]

    sem_names = [
        "t", "v", "s", "cc1", "cc2", "cc2b",
        "d_xm0", "d_st",
        "d_bc1", "d_bc2", "d_bc3",
        "d_xt0", "d_xt1", "d_xs",
        "d_sp0", "d_sp1",
        "d_xc0", "d_xc1",
        "d_fw", "d_fwt", "d_wqs", "d_xqs", "d_xtm",
        "d_q0", "d_q1", "d_qg0", "d_qg1",
        "d_brt0", "d_brt1",
        "d_wq0", "d_wq1", "d_wq2", "d_wq3",
        "d_xq0", "d_xq1",
        "d_ao0", "d_ao1",
        "d_gs", "d_rb",
        "d_g0", "d_g1", "d_xe0", "d_xe1",
        "d_out0", "d_out1",
    ]
    P = Prog()
    for n in sem_names:
        P.new_sem(n)
    mark = {}

    P.op("vector", lambda e, sm: e.memset(ONES[:, :], 1.0), "v")
    v_ones = P.count("v")

    P.dma("gpsimd", lambda e, sm: e.dma_start(
        out=BIASC[:, :], in_=FB.rearrange("(j p) -> p j", p=128),
        allow_slow_non_contiguous=True), "d_bc1")

    q1cnt = {}
    def _xm1_load(ih):
        i, hf = divmod(ih, 2)
        slot = ih % 2
        semn = "d_q0" if slot == 0 else "d_q1"
        pv = mark.get(("xm1v", ih - 2))
        if pv is not None:
            P.wait("scalar", "v", pv)
            P.wait("scalar", "s", mark[("xm1s", ih - 2)])
        P.dma("scalar", lambda e, sm, i=i, hf=hf, slot=slot: e.dma_start(
            out=QS2[:, 1024 * slot:1024 * (slot + 1)],
            in_=XM[128 * i:128 * (i + 1), 1024 * hf:1024 * (hf + 1)]), semn)
        q1cnt[ih] = P.count(semn)

    _xm1_load(0)
    _xm1_load(1)
    for ih in range(16):
        i, hf = divmod(ih, 2)
        slot = ih % 2
        semn = "d_q0" if slot == 0 else "d_q1"
        P.wait("vector", semn, q1cnt[ih])
        P.drain("vector")
        P.op("vector", lambda e, sm, slot=slot, hf=hf: e.tensor_reduce(
            GT0[:, 2 * hf:2 * hf + 1], QS2[:, 1024 * slot:1024 * (slot + 1)], AX.X, ALU.add), "v")
        P.wait("scalar", semn, q1cnt[ih])
        if ih > 0:
            P.wait("scalar", "v", mark[("xm1v", ih - 1)])
        P.op("scalar", lambda e, sm, slot=slot: e.square(
            PBUF[:, 0:1024], QS2[:, 1024 * slot:1024 * (slot + 1)]), "s")
        mark[("xm1s", ih)] = P.count("s")
        P.wait("vector", "s", mark[("xm1s", ih)])
        P.drain("vector")
        P.op("vector", lambda e, sm, hf=hf: e.tensor_reduce(
            GT0[:, 2 * hf + 1:2 * hf + 2], PBUF[:, 0:1024], AX.X, ALU.add), "v")
        if hf == 1:
            P.drain("vector")
            P.op("vector", lambda e, sm, i=i: e.tensor_tensor(
                SUMC[:, i:i + 1], GT0[:, 0:1], GT0[:, 2:3], ALU.add), "v")
            P.drain("vector")
            P.op("vector", lambda e, sm, i=i: e.tensor_tensor(
                SQC[:, i:i + 1], GT0[:, 1:2], GT0[:, 3:4], ALU.add), "v")
        mark[("xm1v", ih)] = P.count("v")
        if ih + 2 < 16:
            _xm1_load(ih + 2)
    mark[("xmred", 7)] = mark[("xm1v", 15)]
    mark[("xmsq", 7)] = mark[("xm1s", 15)]

    P.wait("gpsimd", "v", mark[("xmred", 7)])
    P.dma("gpsimd", lambda e, sm: e.dma_start(
        out=cst_in[0:1, :].rearrange("o (j p) -> p (o j)", p=128),
        in_=SUMC[:, :], allow_slow_non_contiguous=True), "d_st")
    P.dma("gpsimd", lambda e, sm: e.dma_start(
        out=cst_in[1:2, :].rearrange("o (j p) -> p (o j)", p=128),
        in_=SQC[:, :], allow_slow_non_contiguous=True), "d_st")
    P.wait("gpsimd", "d_st")
    P.op("gpsimd", lambda e, sm: e.collective_compute(
        "AllGather", ALU.bypass, replica_groups=[[0, 1, 2, 3], [4, 5, 6, 7]],
        ins=[cst_in.opt()], outs=[cst_out.opt()]), "cc1")
    P.wait("gpsimd", "cc1")
    for i in range(4):
        P.dma("gpsimd", lambda e, sm, i=i: e.dma_start(
            out=GSUM[:, 8 * i:8 * (i + 1)],
            in_=cst_out[2 * i:2 * i + 1, :].rearrange("o (j p) -> p (o j)", p=128),
            allow_slow_non_contiguous=True), "d_st")
        P.dma("gpsimd", lambda e, sm, i=i: e.dma_start(
            out=GSQ[:, 8 * i:8 * (i + 1)],
            in_=cst_out[2 * i + 1:2 * i + 2, :].rearrange("o (j p) -> p (o j)", p=128),
            allow_slow_non_contiguous=True), "d_st")
    P.wait("vector", "d_st")
    P.op("vector", lambda e, sm: e.tensor_scalar(GM[:, :], GSUM[:, :], 1.0 / D, None, ALU.mult), "v")
    P.drain("vector")
    P.op("vector", lambda e, sm: e.tensor_mul(GT0[:, :], GM[:, :], GM[:, :]), "v")
    P.drain("vector")
    P.op("vector", lambda e, sm: e.tensor_scalar(GRS[:, :], GSQ[:, :], 1.0 / D, None, ALU.mult), "v")
    P.drain("vector")
    P.op("vector", lambda e, sm: e.tensor_sub(GRS[:, :], GRS[:, :], GT0[:, :]), "v")
    P.drain("vector")
    P.op("vector", lambda e, sm: e.tensor_scalar(GRS[:, :], GRS[:, :], EPS, None, ALU.add), "v")
    mark["grs_pre"] = P.count("v")
    P.wait("scalar", "v", mark["grs_pre"])
    P.op("scalar", lambda e, sm: e.sqrt(GT0[:, :], GRS[:, :]), "s")
    mark["grs_sqrt"] = P.count("s")
    P.wait("vector", "s", mark["grs_sqrt"])
    P.op("vector", lambda e, sm: e.reciprocal(GRS[:, :], GT0[:, :]), "v")
    P.drain("vector")
    P.op("vector", lambda e, sm: e.tensor_copy(GMB[:, :], GM[:, :]), "v")
    mark["gstats"] = P.count("v")
    P.wait("gpsimd", "v", mark["gstats"])
    P.dma("gpsimd", lambda e, sm: e.dma_start(
        out=m_row.rearrange("(j p) -> p j", p=128), in_=GMB[:, :],
        allow_slow_non_contiguous=True), "d_st")
    P.dma("gpsimd", lambda e, sm: e.dma_start(
        out=rs_row.rearrange("(j p) -> p j", p=128), in_=GRS[:, :],
        allow_slow_non_contiguous=True), "d_st")
    mark["rows_mrs"] = P.count("d_st")
    P.wait("gpsimd", "d_st")

    P.op("vector", lambda e, sm: e.tensor_scalar(MMC[:, :], SUMC[:, :], 1.0 / D, None, ALU.mult), "v")
    P.drain("vector")
    P.op("vector", lambda e, sm: e.tensor_mul(GT0[:, 0:8], MMC[:, :], MMC[:, :]), "v")
    P.drain("vector")
    P.op("vector", lambda e, sm: e.tensor_scalar(RSM[:, :], SQC[:, :], 1.0 / D, None, ALU.mult), "v")
    P.drain("vector")
    P.op("vector", lambda e, sm: e.tensor_sub(RSM[:, :], RSM[:, :], GT0[:, 0:8]), "v")
    P.drain("vector")
    P.op("vector", lambda e, sm: e.tensor_scalar(RSM[:, :], RSM[:, :], EPS, None, ALU.add), "v")
    mark["rsm_pre"] = P.count("v")
    P.wait("scalar", "v", mark["rsm_pre"])
    P.op("scalar", lambda e, sm: e.sqrt(GT0[:, 0:8], RSM[:, :]), "s")
    mark["rsm_sqrt"] = P.count("s")
    P.wait("vector", "s", mark["rsm_sqrt"])
    P.op("vector", lambda e, sm: e.reciprocal(RSM[:, :], GT0[:, 0:8]), "v")
    P.drain("vector")
    P.op("vector", lambda e, sm: e.tensor_copy(MQB[:, :], MMC[:, :]), "v")
    mark["minestats"] = P.count("v")

    QITEMS = []
    QLI = []

    def q_add(load, comp):
        QLI.append(None if load is None else sum(1 for x in QITEMS if x[0] is not None))
        QITEMS.append((load, comp))

    qcnt = {}
    qvmark = {}
    q_li_item = {}
    qemit = [0]
    qcp = [0]

    def _q_emit_load(k):
        load = QITEMS[k][0]
        if load is None:
            return
        li = QLI[k]
        q_li_item[li] = k
        slot = li % 2
        semn = "d_qg0" if slot == 0 else "d_qg1"
        prev_item = q_li_item.get(li - 2)
        if prev_item is not None:
            P.wait("gpsimd", "v", qvmark[prev_item])
        load(slot, semn)
        qcnt[k] = P.count(semn)

    def pump(n):
        for _ in range(n):
            k = qcp[0]
            if k >= len(QITEMS):
                return
            while qemit[0] < min(k + 2, len(QITEMS)):
                _q_emit_load(qemit[0])
                qemit[0] += 1
            load, comp = QITEMS[k]
            if load is not None:
                slot = QLI[k] % 2
                semn = "d_qg0" if slot == 0 else "d_qg1"
                P.wait("vector", semn, qcnt[k])
                comp(slot, semn)
            else:
                comp(None, None)
            qvmark[k] = P.count("v")
            qcp[0] += 1

    def pump_all():
        pump(len(QITEMS) - qcp[0])

    def q_xm2(i, hf):
        def load(slot, semn, i=i, hf=hf):
            if i == 0 and hf == 0:
                P.wait("gpsimd", "v", mark["minestats"])
            P.dma("gpsimd", lambda e, sm, i=i, hf=hf, slot=slot: e.dma_start(
                out=QS2[:, 1024 * slot:1024 * (slot + 1)],
                in_=XM[128 * i:128 * (i + 1), 1024 * hf:1024 * (hf + 1)]), semn)

        def comp(slot, semn, i=i, hf=hf):
            P.drain("vector")
            P.op("vector", lambda e, sm, i=i, slot=slot: e.tensor_scalar(
                QS2[:, 1024 * slot:1024 * (slot + 1)], QS2[:, 1024 * slot:1024 * (slot + 1)],
                MMC[:, i:i + 1], None, ALU.subtract), "v")
            P.drain("vector")
            P.op("vector", lambda e, sm, hf=hf, slot=slot: e.tensor_reduce(
                GT0[:, 8 + hf:9 + hf], QS2[:, 1024 * slot:1024 * (slot + 1)],
                AX.X, ALU.max, apply_absolute_value=True), "v")
            if hf == 1:
                P.drain("vector")
                P.op("vector", lambda e, sm, i=i: e.tensor_tensor(
                    AXC[:, i:i + 1], GT0[:, 8:9], GT0[:, 9:10], ALU.max), "v")
                mark[("axred", i)] = P.count("v")
        return load, comp

    def q_xscales():
        def comp(slot, semn):
            P.drain("vector")
            P.op("vector", lambda e, sm: e.tensor_mul(XSC[:, :], AXC[:, :], RSM[:, :]), "v")
            P.drain("vector")
            P.op("vector", lambda e, sm: e.tensor_scalar(
                XSC[:, :], XSC[:, :], QMAX_INV, 1e-12, ALU.mult, ALU.max), "v")
            P.drain("vector")
            P.op("vector", lambda e, sm: e.reciprocal(QSC[:, :], XSC[:, :]), "v")
            P.drain("vector")
            P.op("vector", lambda e, sm: e.tensor_mul(QSC[:, :], QSC[:, :], RSM[:, :]), "v")
            mark["qscales"] = P.count("v")
            P.wait("gpsimd", "v", mark["qscales"])
            P.dma("gpsimd", lambda e, sm: e.dma_start(
                out=qs_row.rearrange("(j p) -> p j", p=128), in_=QSC[:, :],
                allow_slow_non_contiguous=True), "d_st")
            P.dma("gpsimd", lambda e, sm: e.dma_start(
                out=xs_row.rearrange("(j p) -> p j", p=128), in_=XSC[:, :],
                allow_slow_non_contiguous=True), "d_st")
            P.dma("gpsimd", lambda e, sm: e.dma_start(
                out=mq_row.rearrange("(j p) -> p j", p=128), in_=MQB[:, :],
                allow_slow_non_contiguous=True), "d_st")
            P.wait("gpsimd", "d_st")
            P.dma("gpsimd", lambda e, sm: e.dma_start(
                out=BQS[:, :], in_=qs_row.rearrange("(o a) -> o a", o=1).broadcast_to([128, TOK])), "d_bc1")
            P.dma("gpsimd", lambda e, sm: e.dma_start(
                out=BMQ[:, :], in_=mq_row.rearrange("(o a) -> o a", o=1).broadcast_to([128, TOK])), "d_bc1")
            mark["bq_ready"] = P.count("d_bc1")
        return None, comp

    def q_fw(i, hf):
        def load(slot, semn, i=i, hf=hf):
            P.dma("gpsimd", lambda e, sm, i=i, hf=hf, slot=slot: e.dma_start(
                out=QS2[:, 1024 * slot:1024 * (slot + 1)],
                in_=FW[128 * i:128 * (i + 1), 1024 * hf:1024 * (hf + 1)]), semn)

        def comp(slot, semn, i=i, hf=hf):
            P.drain("vector")
            P.op("vector", lambda e, sm, hf=hf, slot=slot: e.tensor_reduce(
                GT0[:, 8 + hf:9 + hf], QS2[:, 1024 * slot:1024 * (slot + 1)],
                AX.X, ALU.max, apply_absolute_value=True), "v")
            if hf == 1:
                P.drain("vector")
                P.op("vector", lambda e, sm, i=i: e.tensor_tensor(
                    WMX[:, i:i + 1], GT0[:, 8:9], GT0[:, 9:10], ALU.max), "v")
                mark[("fwred", i)] = P.count("v")
        return load, comp

    def q_wscales():
        def comp(slot, semn):
            P.drain("vector")
            P.op("vector", lambda e, sm: e.tensor_scalar(
                WSC[:, :], WMX[:, :], 1e-8, QMAX_INV, ALU.max, ALU.mult), "v")
            P.drain("vector")
            P.op("vector", lambda e, sm: e.reciprocal(WRC[:, :], WSC[:, :]), "v")
            mark["wscales"] = P.count("v")
            P.wait("gpsimd", "v", mark["wscales"])
            P.dma("gpsimd", lambda e, sm: e.dma_start(
                out=rws_row.rearrange("(j p) -> p j", p=128), in_=WRC[:, :],
                allow_slow_non_contiguous=True), "d_st")
            P.wait("gpsimd", "d_st")
            P.dma("gpsimd", lambda e, sm: e.dma_start(
                out=BRWS[:, :], in_=rws_row.rearrange("(o a) -> o a", o=1).broadcast_to([128, 2048])), "d_bc3")
            mark["brws_ready"] = P.count("d_bc3")
        return None, comp

    def q_fwt(dc, h2):
        def load(slot, semn, dc=dc, h2=h2):
            P.dma("gpsimd", lambda e, sm, dc=dc, h2=h2, slot=slot: e.dma_start(
                out=QS2[:, 1024 * slot:1024 * (slot + 1)],
                in_=FWT[128 * dc:128 * (dc + 1), 1024 * h2:1024 * (h2 + 1)]), semn)

        def comp(slot, semn, dc=dc, h2=h2):
            if dc == 0 and h2 == 0:
                P.wait("vector", "d_bc3", mark["brws_ready"])
            if h2 == 0:
                prevsp = mark.get(("wqsp", dc - 1))
                if prevsp is not None:
                    P.wait("vector", "d_wqs", prevsp)
            P.drain("vector")
            P.op("vector", lambda e, sm, h2=h2, slot=slot: e.tensor_tensor(
                F32B[:, :], QS2[:, 1024 * slot:1024 * (slot + 1)],
                BRWS[:, 1024 * h2:1024 * (h2 + 1)], ALU.mult), "v")
            P.drain("vector")
            P.op("vector", lambda e, sm: e.tensor_scalar(
                F32B[:, :], F32B[:, :], MAGIC, MAGIC, ALU.add, ALU.subtract), "v")
            P.drain("vector")
            P.op("vector", lambda e, sm: e.tensor_copy(
                F32B[:, :].bitcast(I32), F32B[:, :].bitcast(I32)), "v")
            P.drain("vector")
            P.op("vector", lambda e, sm, h2=h2: e.tensor_copy(
                PBUF[:, 1024 * h2:1024 * (h2 + 1)], F32B[:, :]), "v")
            if h2 == 1:
                mark[("wqq", dc)] = P.count("v")
                P.wait("gpsimd", "v", mark[("wqq", dc)])
                P.dma("gpsimd", lambda e, sm, dc=dc: e.dma_start(
                    out=wq_spill[128 * dc:128 * (dc + 1), :], in_=PBUF[:, :]), "d_wqs")
                mark[("wqsp", dc)] = P.count("d_wqs")
        return load, comp

    def q_xq(dc):
        def load(slot, semn, dc=dc):
            P.dma("gpsimd", lambda e, sm, dc=dc, slot=slot: e.dma_start(
                out=QS2[:, 1024 * slot:1024 * (slot + 1)],
                in_=XTM[128 * dc:128 * (dc + 1), :]), semn)

        def comp(slot, semn, dc=dc):
            if dc == 0:
                P.wait("vector", "d_bc1", mark["bq_ready"])
                P.wait("vector", "d_wqs", mark[("wqsp", NDC - 1)])
            else:
                prevsp = mark.get(("xqsp", dc - 1))
                if prevsp is not None:
                    P.wait("vector", "d_xqs", prevsp)
            P.drain("vector")
            P.op("vector", lambda e, sm, slot=slot: e.tensor_tensor(
                F32B[:, :], QS2[:, 1024 * slot:1024 * (slot + 1)], BMQ[:, :], ALU.subtract), "v")
            P.drain("vector")
            P.op("vector", lambda e, sm: e.tensor_mul(F32B[:, :], F32B[:, :], BQS[:, :]), "v")
            P.drain("vector")
            P.op("vector", lambda e, sm: e.tensor_scalar(
                F32B[:, :], F32B[:, :], MAGIC, MAGIC, ALU.add, ALU.subtract), "v")
            P.drain("vector")
            P.op("vector", lambda e, sm: e.tensor_copy(
                F32B[:, :].bitcast(I32), F32B[:, :].bitcast(I32)), "v")
            P.drain("vector")
            P.op("vector", lambda e, sm: e.tensor_copy(PBUF[:, 0:1024], F32B[:, :]), "v")
            mark[("xqq", dc)] = P.count("v")
            P.wait("gpsimd", "v", mark[("xqq", dc)])
            for tc in range(NTC):
                P.dma("gpsimd", lambda e, sm, dc=dc, tc=tc: e.dma_start(
                    out=xq_spill[tc, :, 512 * dc:512 * (dc + 1)],
                    in_=PBUF[:, 512 * tc:512 * (tc + 1)]), "d_xqs")
            mark[("xqsp", dc)] = P.count("d_xqs")
        return load, comp

    def q_xqfin():
        def comp(slot, semn):
            mark["xq_done"] = P.count("d_xqs")
            P.wait("gpsimd", "v", mark[("xqq", NDC - 1)])
            P.dma("gpsimd", lambda e, sm: e.dma_start(
                out=BQS[:, :], in_=xs_row.rearrange("(o a) -> o a", o=1).broadcast_to([128, TOK])), "d_bc2")
            mark["bxs_ready"] = P.count("d_bc2")
        return None, comp

    for i in range(8):
        for hf in range(2):
            q_add(*q_xm2(i, hf))
    q_add(*q_xscales())
    for i in range(NOC):
        for hf in range(2):
            q_add(*q_fw(i, hf))
    q_add(*q_wscales())
    for dc in range(NDC):
        for h2 in range(2):
            q_add(*q_fwt(dc, h2))
    for dc in range(NDC):
        q_add(*q_xq(dc))
    q_add(*q_xqfin())

    for wi, (srcw, dst) in enumerate([(WK, WA), (WV, WB)]):
        for g in range(8):
            k = wi * 8 + g
            slot = k % 2
            semn = "d_xt0" if slot == 0 else "d_xt1"
            prev = mark.get(("wcast", k - 2))
            if prev is not None:
                P.wait("sync", "s", prev)
            P.dma("sync", lambda e, sm, srcw=srcw, g=g, slot=slot: e.dma_start(
                out=XTG[:, 1024 * slot:1024 * (slot + 1)].rearrange("p (d c) -> p d c", d=2),
                in_=srcw[256 * g:256 * (g + 1), :].rearrange("(d p) c -> p d c", p=128)), semn)
            P.wait("scalar", semn)
            P.op("scalar", lambda e, sm, dst=dst, g=g, slot=slot: e.copy(
                dst[:, 1024 * g:1024 * (g + 1)], XTG[:, 1024 * slot:1024 * (slot + 1)]), "s")
            mark[("wcast", k)] = P.count("s")
    mark["wkv_ready_s"] = P.count("s")

    P.wait("tensor", "v", v_ones)
    for _ in range(64):
        P.op("tensor", lambda e, sm: e.matmul(
            ffb[0][:, 0:128], ONES[:, :], ONES[:, 0:128], start=True, stop=True,
            skip_group_check=True), None)

    P.wait("tensor", "s", mark["wkv_ready_s"])
    for wi, (wsrc, bank) in enumerate([(WA, pvb_t[0]), (WB, pvb_t[1])]):
        for dc in range(NDC):
            P.op("tensor", lambda e, sm, wsrc=wsrc, bank=bank, dc=dc: e.matmul(
                bank[:, :], ONES[:, :], wsrc[:, 512 * dc:512 * (dc + 1)],
                start=(dc == 0), stop=(dc == NDC - 1)), "t" if dc == NDC - 1 else None)
        mark[("wsum", wi)] = P.count("t")
    for wi, (bank, dstrow) in enumerate([(pvb_t[0], NEGWK), (pvb_t[1], NEGWV)]):
        P.wait("vector", "t", mark[("wsum", wi)])
        P.op("vector", lambda e, sm, bank=bank, dstrow=dstrow: e.tensor_scalar(
            dstrow[0:1, :], bank[0:1, :], -1.0, None, ALU.mult), "v")
    mark["negw_kv"] = P.count("v")

    def xcbuf(tt):
        return XQR, (tt % 2) * 8192

    def sw1_produce(tt):
        slot = tt % 2
        brt = "d_brt0" if slot == 0 else "d_brt1"
        P.wait("sync", "d_st", mark["rows_mrs"])
        prev = mark.get(("kevac_g", (tt - 2) * H_LOC + H_LOC - 1))
        if prev is not None:
            P.wait("sync", "v", prev)
        prev_mm = mark.get(("sw1_mm", tt - 2))
        if prev_mm is not None:
            P.wait("sync", "t", prev_mm)
        P.dma("sync", lambda e, sm, tt=tt, slot=slot: e.dma_start(
            out=BRS[:, 512 * slot:512 * (slot + 1)],
            in_=rs_row[512 * tt:512 * (tt + 1)].rearrange("(o a) -> o a", o=1).broadcast_to([128, 512])), brt)
        P.dma("sync", lambda e, sm, tt=tt, slot=slot: e.dma_start(
            out=MROW2[0:1, 512 * slot:512 * (slot + 1)],
            in_=m_row[512 * tt:512 * (tt + 1)].rearrange("(o a) -> o a", o=1)), brt)
        mark[("brt", tt)] = P.count(brt)

        xcb, xcoff = xcbuf(tt)
        for g in range(8):
            kg = tt * 8 + g
            xslot = kg % 2
            if xslot == 0:
                xsem = "d_xt0"
                prev = mark.get(("xcast", kg - 2))
                if prev is not None:
                    P.wait("sync", "s", prev)
                elif kg < 2:
                    P.wait("sync", "s", mark["wkv_ready_s"])
                if g == 0:
                    prev_mm = mark.get(("sw1_mm", tt - 2))
                    if prev_mm is not None:
                        P.wait("sync", "t", prev_mm)
                P.dma("sync", lambda e, sm, tt=tt, g=g, xslot=xslot: e.dma_start(
                    out=XTG[:, 1024 * xslot:1024 * (xslot + 1)].rearrange("p (d c) -> p d c", d=2),
                    in_=XT[tt, 2 * g:2 * (g + 1)].rearrange("d p c -> p d c")), xsem)
            else:
                xsem = "d_xs"
                prev = mark.get(("xcast", kg - 2))
                P.wait("scalar", "s", prev if prev is not None else mark["wkv_ready_s"])
                P.dma("scalar", lambda e, sm, tt=tt, g=g, xslot=xslot: e.dma_start(
                    out=XTG[:, 1024 * xslot:1024 * (xslot + 1)].rearrange("p (d c) -> p d c", d=2),
                    in_=XT[tt, 2 * g:2 * (g + 1)].rearrange("d p c -> p d c")), xsem)
            P.wait("scalar", xsem)
            P.op("scalar", lambda e, sm, xcb=xcb, xcoff=xcoff, g=g, xslot=xslot: e.copy(
                xcb[:, xcoff + 1024 * g:xcoff + 1024 * (g + 1)],
                XTG[:, 1024 * xslot:1024 * (xslot + 1)]), "s")
            mark[("xcast", kg)] = P.count("s")

    def sw1_compute(tt):
        slot = tt % 2
        xcb, xcoff = xcbuf(tt)
        if tt == 0:
            P.wait("tensor", "s", mark["wkv_ready_s"])
            P.wait("tensor", "v", mark["negw_kv"])
        for ec in range(H_LOC):
            bank = pvb_t[ec % 2]
            gk = tt * H_LOC + ec
            prev = mark.get(("kevac_g", gk - 2))
            if prev is not None:
                P.wait("tensor", "v", prev)
            if ec == 0:
                P.wait("tensor", "d_brt0" if slot == 0 else "d_brt1", mark[("brt", tt)])
            for dc in range(NDC):
                if dc == 0 and ec == 0:
                    P.wait("tensor", "s", mark[("xcast", tt * 8 + 7)])
                P.op("tensor", lambda e, sm, bank=bank, ec=ec, dc=dc, xcb=xcb, xcoff=xcoff: e.matmul(
                    bank[:, :],
                    WA[:, 512 * dc + 128 * ec:512 * dc + 128 * (ec + 1)],
                    xcb[:, xcoff + 512 * dc:xcoff + 512 * (dc + 1)],
                    start=(dc == 0), stop=False), None)
            P.op("tensor", lambda e, sm, bank=bank, ec=ec, slot=slot: e.matmul(
                bank[:, :], NEGWK[0:1, 128 * ec:128 * (ec + 1)],
                MROW2[0:1, 512 * slot:512 * (slot + 1)], start=False, stop=True), "t")
            mark[("kmm", tt, ec)] = P.count("t")
            P.wait("vector", "t", mark[("kmm", tt, ec)])
            if ec == 0:
                P.wait("vector", "d_brt0" if slot == 0 else "d_brt1", mark[("brt", tt)])
            P.op("vector", lambda e, sm, bank=bank, ec=ec, tt=tt, slot=slot: e.tensor_tensor(
                KT[:, S * ec + 512 * tt:S * ec + 512 * (tt + 1)],
                bank[:, :], BRS[:, 512 * slot:512 * (slot + 1)], ALU.mult), "v")
            mark[("kevac_g", tt * H_LOC + ec)] = P.count("v")

        for tc in range(4):
            bank = ffb[tc % 2]
            gk = tt * 4 + tc
            prev = mark.get(("vevac_g", gk - 2))
            if prev is not None:
                P.wait("tensor", "s", prev)
            for dc in range(NDC):
                P.op("tensor", lambda e, sm, bank=bank, tc=tc, dc=dc, xcb=xcb, xcoff=xcoff: e.matmul(
                    bank[:, :],
                    xcb[:, xcoff + 512 * dc + 128 * tc:xcoff + 512 * dc + 128 * (tc + 1)],
                    WB[:, 512 * dc:512 * (dc + 1)],
                    start=(dc == 0), stop=False), None)
            P.op("tensor", lambda e, sm, bank=bank, slot=slot, tc=tc: e.matmul(
                bank[:, :], MROW2[0:1, 512 * slot + 128 * tc:512 * slot + 128 * (tc + 1)],
                NEGWV[0:1, 0:E_LOC], start=False, stop=True), "t")
            mark[("vmm", tt, tc)] = P.count("t")
            ck = tt * 4 + tc
            P.wait("scalar", "t", mark[("vmm", tt, tc)])
            if ck == 0:
                P.wait("scalar", "v", mark["gstats"])
            P.op("scalar", lambda e, sm, bank=bank, ck=ck: e.activation(
                VT[:, 512 * ck:512 * (ck + 1)], bank[:, :], AF.Identity,
                bias=0.0, scale=GRS[:, ck:ck + 1]), "s")
            mark[("vevac_g", ck)] = P.count("s")
        mark[("sw1_mm", tt)] = P.count("t")
        pump(13)

    sw1_produce(0)
    for tt in range(NTT):
        if tt + 1 < NTT:
            sw1_produce(tt + 1)
        sw1_compute(tt)

    P.wait("sync", "t", mark[("sw1_mm", NTT - 1)])
    for g in range(8):
        slot = g % 2
        semn = "d_xt0" if slot == 0 else "d_xt1"
        prev = mark.get(("wqcast", g - 2))
        if prev is not None:
            P.wait("sync", "s", prev)
        elif g < 2:
            P.wait("sync", "s", mark[("xcast", NTT * 8 - 1)])
        P.dma("sync", lambda e, sm, g=g, slot=slot: e.dma_start(
            out=XTG[:, 1024 * slot:1024 * (slot + 1)].rearrange("p (d c) -> p d c", d=2),
            in_=WQ[256 * g:256 * (g + 1), :].rearrange("(d p) c -> p d c", p=128)), semn)
        P.wait("scalar", semn)
        if g == 0:
            P.wait("scalar", "t", mark[("sw1_mm", NTT - 1)])
        P.op("scalar", lambda e, sm, g=g, slot=slot: e.copy(
            WA[:, 1024 * g:1024 * (g + 1)], XTG[:, 1024 * slot:1024 * (slot + 1)]), "s")
        mark[("wqcast", g)] = P.count("s")
    mark["wq_cast_done"] = P.count("s")

    P.wait("tensor", "s", mark["wq_cast_done"])
    for dc in range(NDC):
        P.op("tensor", lambda e, sm, dc=dc: e.matmul(
            ffb[0][:, :], ONES[:, :], WA[:, 512 * dc:512 * (dc + 1)],
            start=(dc == 0), stop=(dc == NDC - 1)), "t" if dc == NDC - 1 else None)
    mark["wsumq"] = P.count("t")
    P.wait("vector", "t", mark["wsumq"])
    P.op("vector", lambda e, sm: e.tensor_scalar(
        NEGWQ[0:1, :], ffb[0][0:1, :], -1.0, None, ALU.mult), "v")
    mark["negwq"] = P.count("v")

    def sw2_produce(tt):
        xcb, xcoff = xcbuf(tt)
        for g in range(8):
            kg = tt * 8 + g
            xslot = kg % 2
            if xslot == 0:
                xsem = "d_xt0"
                prev = mark.get(("xcast2", kg - 2))
                if prev is not None:
                    P.wait("sync", "s", prev)
                elif kg < 2:
                    P.wait("sync", "s", mark["wq_cast_done"])
                if g == 0:
                    prev_mm = mark.get(("qmm_all", tt - 2))
                    if prev_mm is not None:
                        P.wait("sync", "t", prev_mm)
                    else:
                        P.wait("sync", "t", mark[("sw1_mm", NTT - 1)])
                P.dma("sync", lambda e, sm, tt=tt, g=g, xslot=xslot: e.dma_start(
                    out=XTG[:, 1024 * xslot:1024 * (xslot + 1)].rearrange("p (d c) -> p d c", d=2),
                    in_=XT[tt, 2 * g:2 * (g + 1)].rearrange("d p c -> p d c")), xsem)
            else:
                xsem = "d_xs"
                prev = mark.get(("xcast2", kg - 2))
                P.wait("scalar", "s", prev if prev is not None else mark["wq_cast_done"])
                P.dma("scalar", lambda e, sm, tt=tt, g=g, xslot=xslot: e.dma_start(
                    out=XTG[:, 1024 * xslot:1024 * (xslot + 1)].rearrange("p (d c) -> p d c", d=2),
                    in_=XT[tt, 2 * g:2 * (g + 1)].rearrange("d p c -> p d c")), xsem)
            P.wait("scalar", xsem)
            P.op("scalar", lambda e, sm, xcb=xcb, xcoff=xcoff, g=g, xslot=xslot: e.copy(
                xcb[:, xcoff + 1024 * g:xcoff + 1024 * (g + 1)],
                XTG[:, 1024 * xslot:1024 * (xslot + 1)]), "s")
            mark[("xcast2", kg)] = P.count("s")
        mark[("xcld", tt)] = P.count("s")
        slot = tt % 2
        brt = "d_brt0" if slot == 0 else "d_brt1"
        prev = mark.get(("qevac_g", (tt - 2) * H_LOC + 3))
        if prev is not None:
            P.wait("sync", "v", prev)
        else:
            P.wait("sync", "v", mark[("kevac_g", NTT * H_LOC - 1)])
        P.dma("sync", lambda e, sm, tt=tt, slot=slot: e.dma_start(
            out=BRS[:, 512 * slot:512 * (slot + 1)],
            in_=rs_row[512 * tt:512 * (tt + 1)].rearrange("(o a) -> o a", o=1).broadcast_to([128, 512])), brt)
        P.dma("sync", lambda e, sm, tt=tt, slot=slot: e.dma_start(
            out=MROW2[0:1, 512 * slot:512 * (slot + 1)],
            in_=m_row[512 * tt:512 * (tt + 1)].rearrange("(o a) -> o a", o=1)), brt)
        mark[("brt2", tt)] = P.count(brt)

    def sw2_compute(tt):
        xcb, xcoff = xcbuf(tt)
        slot = tt % 2
        for ec in range(H_LOC):
            bank = pvb_t[ec % 2]
            gk = tt * H_LOC + ec
            prev = mark.get(("qevac_g", gk - 2))
            if prev is not None:
                P.wait("tensor", "v", prev)
            elif gk < 2:
                P.wait("tensor", "v", mark["negwq"])
            if ec == 0:
                P.wait("tensor", "d_brt0" if slot == 0 else "d_brt1", mark[("brt2", tt)])
            for dc in range(NDC):
                if dc == 0 and ec == 0:
                    P.wait("tensor", "s", mark[("xcld", tt)])
                P.op("tensor", lambda e, sm, bank=bank, ec=ec, dc=dc, xcb=xcb, xcoff=xcoff: e.matmul(
                    bank[:, :],
                    WA[:, 512 * dc + 128 * ec:512 * dc + 128 * (ec + 1)],
                    xcb[:, xcoff + 512 * dc:xcoff + 512 * (dc + 1)],
                    start=(dc == 0), stop=False), None)
            P.op("tensor", lambda e, sm, bank=bank, ec=ec, slot=slot: e.matmul(
                bank[:, :], NEGWQ[0:1, 128 * ec:128 * (ec + 1)],
                MROW2[0:1, 512 * slot:512 * (slot + 1)], start=False, stop=True), "t")
            mark[("qmm", tt, ec)] = P.count("t")
            slot = tt % 2
            brt = "d_brt0" if slot == 0 else "d_brt1"
            P.wait("vector", "t", mark[("qmm", tt, ec)])
            if ec == 0:
                P.wait("vector", brt, mark[("brt2", tt)])
            P.op("vector", lambda e, sm, bank=bank, ec=ec, tt=tt, slot=slot: e.tensor_tensor(
                QT[:, S * ec + 512 * tt:S * ec + 512 * (tt + 1)],
                bank[:, :], BRS[:, 512 * slot:512 * (slot + 1)], ALU.mult), "v")
            mark[("qevac_g", tt * H_LOC + ec)] = P.count("v")
        mark[("qmm_all", tt)] = P.count("t")
        pump(2)

    sw2_produce(0)
    sw2_produce(1)
    for tt in range(NTT):
        sw2_compute(tt)
        if tt + 2 < NTT:
            sw2_produce(tt + 2)
    pump_all()
    mark["sweep2_done_t"] = P.count("t")
    mark["sweep2_done_v"] = P.count("v")

    def emit_xq_load(tc, slot):
        semn = "d_xq0" if slot == 0 else "d_xq1"
        P.dma("sync", lambda e, sm, tc=tc, slot=slot: e.dma_start(
            out=XQR[:, 8192 * slot:8192 * (slot + 1)], in_=xq_spill[tc]), semn)
        return P.count(semn)

    def emit_wq_load(blk):
        slot = blk % 2
        semn = f"d_wq{slot}"
        P.dma("scalar", lambda e, sm, blk=blk, slot=slot: e.dma_start(
            out=WB[:, 4096 * slot:4096 * (slot + 1)].rearrange("p (d c) -> p d c", d=16),
            in_=wq_spill[:, 256 * blk:256 * (blk + 1)].rearrange("(d p) c -> p d c", p=128)), semn)
        return P.count(semn)

    P.wait("sync", "d_xqs", mark["xq_done"])
    P.wait("sync", "t", mark["sweep2_done_t"])
    mark[("xqld", 0)] = emit_xq_load(0, 0)
    P.wait("scalar", "d_wqs", mark[("wqsp", NDC - 1)])
    P.wait("scalar", "t", mark["sweep2_done_t"])
    wq_marks = {}
    for blk in range(2):
        wq_marks[blk] = emit_wq_load(blk)

    BDEN = [(BRWS, 0), (BRWS, 512)]
    FFT1 = (BRWS, 1024)
    GST = (BRWS, 1536)

    n_it = H_LOC * NQT
    NP = NKC // 2

    P.wait("scalar", "v", mark[("xqq", NDC - 1)])
    P.wait("scalar", "d_wqs", mark[("wqsp", NDC - 1)])
    P.wait("scalar", "d_xqs", mark["xq_done"])
    P.wait("scalar", "d_bc1", mark["bq_ready"])
    P.wait("scalar", "d_bc2", mark["bxs_ready"])
    P.wait("vector", "d_bc2", mark["bxs_ready"])
    P.wait("vector", "d_xqs", mark["xq_done"])
    P.wait("tensor", "v", v_ones)

    pend_den = [None]
    for it in range(n_it):
        h, qt = divmod(it, NQT)
        f_tc, f_oc = divmod(it, NOC)
        fbank = ffb[it % 2]
        pvb = pvb_t[it % 2]

        if it == 8:
            mark[("xqld", 1)] = emit_xq_load(1, 1)

        ffn_dc = 0
        for p in range(NP + 2):
            if p >= 2:
                pp = p - 2
                for half in range(2):
                    kcp = 2 * pp + half
                    if kcp == 0:
                        prev = mark.get(("aoevac", it - 2))
                        if prev is not None:
                            P.wait("tensor", "v", prev)
                        elif it < 2:
                            P.wait("tensor", "v", mark["sweep2_done_v"])
                    P.wait("tensor", "s", mark[("exp", it, pp)])
                    P.op("tensor", lambda e, sm, h=h, kcp=kcp, pvb=pvb, pp=pp, half=half: e.matmul(
                        pvb[:, :],
                        VT[:, 512 * kcp + 128 * h:512 * kcp + 128 * (h + 1)],
                        PBUF[:, 1024 * (pp % 2) + 512 * half:1024 * (pp % 2) + 512 * (half + 1)],
                        start=(kcp == 0), stop=(kcp == NKC - 1), skip_group_check=True),
                        "t" if kcp == NKC - 1 else None)
                    if kcp == NKC - 1:
                        mark[("pvmm", it)] = P.count("t")
                if ffn_dc < NDC:
                    dc = ffn_dc
                    ffn_dc += 1
                    gb = it // 2
                    wslot2 = gb % 2
                    if dc == 0:
                        P.wait("tensor", f"d_wq{wslot2}", wq_marks[gb])
                        P.wait("tensor", "d_xq0" if f_tc == 0 else "d_xq1", mark[("xqld", f_tc)])
                        prevf = mark.get(("recip", it - 2))
                        if prevf is not None:
                            P.wait("tensor", "v", prevf)
                        elif it < 2:
                            P.wait("tensor", "s", mark[("vevac_g", NTT * 4 - 1)])
                    P.op("tensor", lambda e, sm, f_oc=f_oc, f_tc=f_tc, dc=dc, fbank=fbank, wslot2=wslot2: e.matmul(
                        fbank[:, :],
                        WB[:, 4096 * wslot2 + 256 * dc + 128 * (f_oc % 2):4096 * wslot2 + 256 * dc + 128 * (f_oc % 2) + 128],
                        XQR[:, 8192 * f_tc + 512 * dc:8192 * f_tc + 512 * (dc + 1)],
                        start=(dc == 0), stop=(dc == NDC - 1), skip_group_check=True),
                        "t" if dc == NDC - 1 else None)
                    if dc == NDC - 1:
                        mark[("ffnmm", it)] = P.count("t")
            if p < NP:
                scb = scp[p % 2]
                prev = mark.get(("exp", it, p - 2))
                if prev is None:
                    prev = mark.get(("exp", it - 1, NP - 2 + p))
                if prev is not None:
                    P.wait("tensor", "s", prev)
                for half in range(2):
                    kc = 2 * p + half
                    P.op("tensor", lambda e, sm, h=h, qt=qt, kc=kc, scb=scb, half=half: e.matmul(
                        scb[:, 512 * half:512 * (half + 1)],
                        KT[:, S * h + 128 * kc:S * h + 128 * (kc + 1)],
                        QT[:, S * h + 512 * qt:S * h + 512 * (qt + 1)],
                        start=True, stop=True, skip_group_check=True), "t")
                mark[("sc", it, p)] = P.count("t")
                if p == 0 and pend_den[0] is not None:
                    pend_den[0]()
                    pend_den[0] = None
                    if it == 16:
                        P.wait("gpsimd", "d_ao0")
                        P.wait("gpsimd", "d_ao1")
                        P.op("gpsimd", lambda e, sm: e.collective_compute(
                            "AllGather", ALU.bypass, replica_groups=[[0, 1, 2, 3], [4, 5, 6, 7]],
                            ins=[cc_in[0:256, :].opt()], outs=[cc_out1.opt()]), "cc2")
                P.wait("scalar", "t", mark[("sc", it, p)])
                prevv = mark.get(("add", it, p - 2)) if p >= 2 else mark.get(("add", it - 1, NP - 1))
                if prevv is not None:
                    P.wait("scalar", "v", prevv)
                P.op("scalar", lambda e, sm, p=p, scb=scb: e.activation(
                    PBUF[:, 1024 * (p % 2):1024 * (p % 2 + 1)], scb[:, :],
                    AF.Exp, scale=SCL_ATTN), "s")
                mark[("exp", it, p)] = P.count("s")
                P.wait("vector", "s", mark[("exp", it, p)])
                if p == 0:
                    prevd = mark.get(("denmm", it - 1))
                    if prevd is not None:
                        P.wait("vector", "t", prevd)
                    P.op("vector", lambda e, sm, p=p: e.tensor_copy(
                        PSAC[:, :], PBUF[:, 1024 * (p % 2):1024 * (p % 2 + 1)]), "v")
                else:
                    P.drain("vector")
                    P.op("vector", lambda e, sm, p=p: e.tensor_tensor(
                        PSAC[:, :], PSAC[:, :], PBUF[:, 1024 * (p % 2):1024 * (p % 2 + 1)],
                        ALU.add), "v")
                mark[("add", it, p)] = P.count("v")

        P.drain("vector")
        P.op("vector", lambda e, sm: e.tensor_tensor(
            PSAC[:, 0:512], PSAC[:, 0:512], PSAC[:, 512:1024], ALU.add), "v")
        mark[("fold", it)] = P.count("v")
        def emit_den(it=it, fbank=fbank, pvb=pvb, h=h, qt=qt):
            P.wait("tensor", "v", mark[("fold", it)])
            P.wait("tensor", "v", mark[("ffn_deq", it)])
            P.op("tensor", lambda e, sm, fbank=fbank: e.matmul(
                fbank[:, :], ONES[:, :], PSAC[:, 0:512], start=True, stop=True,
                skip_group_check=True), "t")
            mark[("denmm", it)] = P.count("t")
            db, doff = BDEN[it % 2]
            P.wait("vector", "t", mark[("denmm", it)])
            prev = mark.get(("aoevac", it - 2))
            if prev is not None:
                P.wait("vector", "v", prev)
            elif it == 0:
                P.wait("vector", "d_bc3", mark["brws_ready"])
                P.wait("vector", "v", mark[("wqq", NDC - 1)])
            P.op("vector", lambda e, sm, db=db, doff=doff, fbank=fbank: e.reciprocal(
                db[:, doff:doff + 512], fbank[:, :]), "v")
            mark[("recip", it)] = P.count("v")
            P.drain("vector")
            P.wait("vector", "t", mark[("pvmm", it)])
            prev = mark.get(("aodma", it - 2))
            if prev is not None:
                P.wait("vector", "d_ao0" if it % 2 == 0 else "d_ao1", prev)
            P.op("vector", lambda e, sm, it=it, db=db, doff=doff, pvb=pvb: e.tensor_tensor(
                AOB[:, 512 * (it % 2):512 * (it % 2 + 1)],
                pvb[:, :], db[:, doff:doff + 512], ALU.mult), "v")
            mark[("aoevac", it)] = P.count("v")
            P.wait("gpsimd", "v", mark[("aoevac", it)])
            P.dma("gpsimd", lambda e, sm, it=it, h=h, qt=qt: e.dma_start(
                out=cc_in[128 * h:128 * (h + 1), 512 * qt:512 * (qt + 1)],
                in_=AOB[:, 512 * (it % 2):512 * (it % 2 + 1)]), "d_ao0" if it % 2 == 0 else "d_ao1")
            mark[("aodma", it)] = P.count("d_ao0" if it % 2 == 0 else "d_ao1")
        pend_den[0] = emit_den

        t1b, t1off = FFT1
        gb, goff = GST
        P.wait("vector", "t", mark[("ffnmm", it)])
        prevg = mark.get(("gdma", it - 1))
        if prevg is not None:
            P.wait("vector", "d_gs", prevg)
        P.op("vector", lambda e, sm, t1b=t1b, t1off=t1off, fbank=fbank, f_tc=f_tc: e.tensor_tensor(
            t1b[:, t1off:t1off + 512], fbank[:, :],
            BQS[:, 512 * f_tc:512 * (f_tc + 1)], ALU.mult), "v")
        mark[("ffn_deq", it)] = P.count("v")
        P.wait("scalar", "v", mark[("ffn_deq", it)])
        P.op("scalar", lambda e, sm, t1b=t1b, t1off=t1off, gb=gb, goff=goff, f_oc=f_oc: e.activation(
            gb[:, goff:goff + 512], t1b[:, t1off:t1off + 512], AF.Silu,
            bias=BIASC[:, f_oc:f_oc + 1], scale=WSC[:, f_oc:f_oc + 1]), "s")
        mark[("silu", it)] = P.count("s")
        P.wait("gpsimd", "s", mark[("silu", it)])
        P.dma("gpsimd", lambda e, sm, f_oc=f_oc, f_tc=f_tc, gb=gb, goff=goff: e.dma_start(
            out=g_spill[128 * f_oc:128 * (f_oc + 1), 512 * f_tc:512 * (f_tc + 1)],
            in_=gb[:, goff:goff + 512]), "d_gs")
        mark[("gdma", it)] = P.count("d_gs")

        if it % 2 == 1:
            gb_next = it // 2 + 2
            if gb_next < 16:
                P.wait("scalar", "v", mark[("ffn_deq", it)])
                wq_marks[gb_next] = emit_wq_load(gb_next % 8)


    wo_cnt = [0]

    def emit_wo_load(ec, gate_t):
        wslot = ec % 2
        for hf in range(2):
            idx = wo_cnt[0]
            wo_cnt[0] += 1
            xs2 = idx % 2
            wsem = "d_xt0" if xs2 == 0 else "d_xt1"
            prevx = mark.get(("wocastg", idx - 2))
            if prevx is not None:
                P.wait("sync", "s", prevx)
            else:
                P.wait("sync", "s", mark["wq_cast_done"])
                P.wait("sync", "s", mark[("xcast2", NTT * 8 - 1)])
                P.wait("sync", "t", mark["sweep2_done_t"])
            P.dma("sync", lambda e, sm, ec=ec, hf=hf, xs2=xs2: e.dma_start(
                out=XTG[:, 1024 * xs2:1024 * (xs2 + 1)],
                in_=WOT[ec][:, 8 * hf:8 * (hf + 1), :].rearrange("p d c -> p (d c)")), wsem)
            P.wait("scalar", wsem)
            if hf == 0:
                P.wait("scalar", "t", gate_t)
            P.op("scalar", lambda e, sm, wslot=wslot, hf=hf, xs2=xs2: e.copy(
                WA[:, 2048 * wslot + 1024 * hf:2048 * wslot + 1024 * (hf + 1)],
                XTG[:, 1024 * xs2:1024 * (xs2 + 1)]), "s")
            mark[("wocastg", idx)] = P.count("s")
        mark[("wocast", ec)] = P.count("s")

    if pend_den[0] is not None:
        pend_den[0]()
        pend_den[0] = None

    emit_wo_load(0, mark["sweep2_done_t"])
    emit_wo_load(1, mark["sweep2_done_t"])

    P.wait("gpsimd", "d_ao0")
    P.wait("gpsimd", "d_ao1")
    P.op("gpsimd", lambda e, sm: e.collective_compute(
        "AllGather", ALU.bypass, replica_groups=[[0, 1, 2, 3], [4, 5, 6, 7]],
        ins=[cc_in[256:512, :].opt()], outs=[cc_out2.opt()]), "cc2b")
    P.dma("gpsimd", lambda e, sm: e.dma_start(out=SOFF[:, :], in_=COFF[:, :]), "d_rb")
    P.wait("gpsimd", "d_rb")
    P.wait("gpsimd", "cc2")
    holder = {}

    def _alloc_reg(e, sm):
        holder["roff"] = e.alloc_register("roff")
        return e.nop()
    P.op("gpsimd", _alloc_reg, None)
    done_p2_wait = [False]
    ec_order = [ec for ec in range(NOC) if (ec % 4) // 2 == 0] + \
               [ec for ec in range(NOC) if (ec % 4) // 2 == 1]
    for ec in ec_order:
        i, hh = divmod(ec, 4)
        part = hh // 2
        src_t = cc_out1_t if part == 0 else cc_out2_t
        if part == 1 and not done_p2_wait[0]:
            P.wait("gpsimd", "cc2b")
            done_p2_wait[0] = True
        row = 256 * i + 128 * (hh % 2)
        P.raw("gpsimd", lambda e, sm: e.reg_load(holder["roff"], SOFF[0:1, 0:1]))
        P.raw("gpsimd", lambda e, sm, row=row: e.reg_add(holder["roff"], holder["roff"], row * S))
        P.dma("gpsimd", lambda e, sm, ec=ec, src_t=src_t: e.dma_start(
            out=KT[:, TOK * ec:TOK * (ec + 1)],
            in_=bass.AP(src_t, holder["roff"], [[S, 128], [1, TOK]])), "d_rb")
    mark["rb_done"] = P.count("d_rb")

    for gi in range(NOC * NTC):
        ec, tc = divmod(gi, NTC)
        bank = pvb_t[gi % 2]
        wslot = ec % 2

        gsem = "d_g0" if gi % 2 == 0 else "d_g1"
        xsem = "d_xe0" if gi % 2 == 0 else "d_xe1"
        prev = mark.get(("ep_done", gi - 2))
        if prev is not None:
            P.wait("scalar", "v", prev)
        else:
            P.wait("scalar", "d_gs", mark[("gdma", n_it - 1)])
        P.dma("scalar", lambda e, sm, ec=ec, tc=tc, gi=gi: e.dma_start(
            out=F32A[:, 512 * (gi % 2):512 * (gi % 2 + 1)],
            in_=g_spill[128 * ec:128 * (ec + 1), 512 * tc:512 * (tc + 1)]), gsem)
        P.dma("scalar", lambda e, sm, ec=ec, tc=tc, gi=gi: e.dma_start(
            out=BRS[:, 512 * (gi % 2):512 * (gi % 2 + 1)],
            in_=XTM[128 * ec:128 * (ec + 1), 512 * tc:512 * (tc + 1)]), xsem)
        mark[("epld", gi)] = (P.count(gsem), P.count(xsem))

        for dc in range(NOC):
            if dc == 0:
                P.wait("tensor", "d_rb", mark["rb_done"])
                P.wait("tensor", "s", mark[("wocast", ec)])
                prevb = mark.get(("ep_done", gi - 2))
                if prevb is not None:
                    P.wait("tensor", "v", prevb)
            P.op("tensor", lambda e, sm, wslot=wslot, dc=dc, tc=tc, bank=bank: e.matmul(
                bank[:, :],
                WA[:, 2048 * wslot + 128 * dc:2048 * wslot + 128 * (dc + 1)],
                KT[:, TOK * dc + 512 * tc:TOK * dc + 512 * (tc + 1)],
                start=(dc == 0), stop=(dc == NOC - 1), skip_group_check=True),
                "t" if dc == NOC - 1 else None)
        mark[("womm", gi)] = P.count("t")
        if tc == NTC - 1 and ec + 2 < NOC:
            emit_wo_load(ec + 2, mark[("womm", gi)])

        P.wait("vector", "t", mark[("womm", gi)])
        P.wait("vector", gsem, mark[("epld", gi)][0])
        P.wait("vector", xsem, mark[("epld", gi)][1])
        prevo = mark.get(("outdma", gi - 2))
        if prevo is not None:
            P.wait("vector", "d_out0" if gi % 2 == 0 else "d_out1", prevo)
        P.op("vector", lambda e, sm, gi=gi, bank=bank: e.scalar_tensor_tensor(
            F32B[:, 512 * (gi % 2):512 * (gi % 2 + 1)],
            bank[:, :], 1.0,
            F32A[:, 512 * (gi % 2):512 * (gi % 2 + 1)], ALU.mult, ALU.mult), "v")
        P.drain("vector")
        P.op("vector", lambda e, sm, gi=gi: e.tensor_tensor(
            F32B[:, 512 * (gi % 2):512 * (gi % 2 + 1)],
            F32B[:, 512 * (gi % 2):512 * (gi % 2 + 1)],
            BRS[:, 512 * (gi % 2):512 * (gi % 2 + 1)], ALU.add), "v")
        mark[("ep_done", gi)] = P.count("v")
        P.wait("gpsimd", "v", mark[("ep_done", gi)])
        P.dma("gpsimd", lambda e, sm, ec=ec, tc=tc, gi=gi: e.dma_start(
            out=OUT[128 * ec:128 * (ec + 1), 512 * tc:512 * (tc + 1)],
            in_=F32B[:, 512 * (gi % 2):512 * (gi % 2 + 1)]), "d_out0" if gi % 2 == 0 else "d_out1")
        mark[("outdma", gi)] = P.count("d_out0" if gi % 2 == 0 else "d_out1")

    P.wait("gpsimd", "d_out0")
    P.wait("gpsimd", "d_out1")

    with ExitStack() as ses:
        sems = {n: ses.enter_context(nc.semaphore(n)) for n in sem_names}
        with nc.Block() as block:
            @block.sync
            def _(sync):
                P.replay("sync", sync, sems)

            @block.tensor
            def _(tensor):
                P.replay("tensor", tensor, sems)

            @block.vector
            def _(vector):
                P.replay("vector", vector, sems)

            @block.scalar
            def _(scalar):
                P.replay("scalar", scalar, sems)

            @block.gpsimd
            def _(gpsimd):
                with allow_ici_all_to_all():
                    P.replay("gpsimd", gpsimd, sems)
    es.close()
    return nc


_NC_CACHE = None


def kernel(L0_attn_x, L0_attn_wq, L0_attn_wk, L0_attn_wv, L0_attn_wo,
           L0_ffn_weight, L0_ffn_bias):
    global _NC_CACHE
    x = np.asarray(L0_attn_x, dtype=np.float32)
    wq = np.asarray(L0_attn_wq, dtype=np.float32)
    wk = np.asarray(L0_attn_wk, dtype=np.float32)
    wv = np.asarray(L0_attn_wv, dtype=np.float32)
    wo = np.asarray(L0_attn_wo, dtype=np.float32)
    fw = np.asarray(L0_ffn_weight, dtype=np.float32)
    fb = np.asarray(L0_ffn_bias, dtype=np.float32)

    C = np.ascontiguousarray
    wot = C(wo.T.reshape(NDC, 128, NOC, 128).transpose(2, 1, 0, 3))
    fwt = C(fw.T)
    fw_c = C(fw)
    in_maps = []
    for c in range(8):
        b, g = divmod(c, 4)
        sl = slice(TOK * g, TOK * (g + 1))
        es0 = slice(E_LOC * g, E_LOC * (g + 1))
        xb = x[b]
        xtr = np.ascontiguousarray(
            xb.T.reshape(NDC, 128, NTT, 512).transpose(2, 0, 1, 3))
        in_maps.append({
            "XT": xtr,
            "XTM": C(xb[sl].T),
            "XM": C(xb[sl]),
            "WQ": C(wq[es0].T),
            "WK": C(wk[es0].T),
            "WV": C(wv[es0].T),
            "WOT": wot,
            "FW": fw_c,
            "FWT": fwt,
            "FB": fb,
            "COFF": np.array([[TOK * g]], dtype=np.uint32),
        })

    if _NC_CACHE is None:
        _NC_CACHE = build_nc()
    nc = _NC_CACHE
    res = run_bass_kernel_spmd(nc, in_maps, core_ids=list(range(8)))

    out = np.empty((B, S, D), dtype=np.float32)
    for c in range(8):
        b, g = divmod(c, 4)
        out[b, TOK * g:TOK * (g + 1), :] = np.asarray(res.results[c]["OUT"]).T
    return out


# version 11
# speedup vs baseline: 1.0965x; 1.0070x over previous
import numpy as np
import ml_dtypes
from contextlib import ExitStack

import concourse.bass as bass
import concourse.mybir as mybir
from concourse.bass_utils import run_bass_kernel_spmd
from concourse.aot_env import allow_ici_all_to_all

F32 = mybir.dt.float32
BF16 = mybir.dt.bfloat16
I32 = mybir.dt.int32
U32 = mybir.dt.uint32
AF = mybir.ActivationFunctionType
ALU = mybir.AluOpType
AX = mybir.AxisListType

D = 2048
S = 4096
B = 2
H_LOC = 4
DK = 128
E_LOC = H_LOC * DK
TOK = 1024
NDC = D // 128
NTT = S // 512
NKC = S // 128
NQT = S // 512
NOC = D // 128
NTC = TOK // 512

EPS = 1e-5
SCL_ATTN = float(1.0 / np.sqrt(np.float32(DK)))
MAGIC = float(np.float32(3 * 2 ** 22))
QMAX_INV = float(np.float32(1.0 / 127.0))

ENGINES = ("sync", "tensor", "vector", "scalar", "gpsimd")


class Prog:
    def __init__(self):
        self.ops = {e: [] for e in ENGINES}
        self.cnt = {}

    def new_sem(self, name):
        self.cnt[name] = 0

    def raw(self, eng, fn):
        self.ops[eng].append(("raw", fn, None, 0))

    def op(self, eng, fn, inc_sem=None, inc_by=1):
        if inc_sem is not None:
            self.cnt[inc_sem] += inc_by
        self.ops[eng].append(("op", fn, inc_sem, inc_by))

    def dma(self, eng, fn, sem):
        self.op(eng, fn, inc_sem=sem, inc_by=16)

    def wait(self, eng, sem, val=None):
        v = self.cnt[sem] if val is None else val
        self.ops[eng].append(("wait", None, sem, v))

    def drain(self, eng):
        self.ops[eng].append(("drain", None, None, 0))

    def count(self, sem):
        return self.cnt[sem]

    def replay(self, eng_name, eng_handle, sems):
        for kind, fn, sem, val in self.ops[eng_name]:
            if kind == "wait":
                eng_handle.wait_ge(sems[sem], val)
            elif kind == "drain":
                eng_handle.drain()
            elif kind == "raw":
                fn(eng_handle, sems)
            else:
                inst = fn(eng_handle, sems)
                if sem is not None:
                    inst.then_inc(sems[sem], val)


def build_nc():
    nc = bass.Bass("TRN2", target_bir_lowering=False, debug=False)

    XT = nc.dram_tensor("XT", [NTT, NDC, 128, 512], F32, kind="ExternalInput").ap()
    XTM = nc.dram_tensor("XTM", [D, TOK], F32, kind="ExternalInput").ap()
    XM = nc.dram_tensor("XM", [TOK, D], F32, kind="ExternalInput").ap()
    WQ = nc.dram_tensor("WQ", [D, E_LOC], F32, kind="ExternalInput").ap()
    WK = nc.dram_tensor("WK", [D, E_LOC], F32, kind="ExternalInput").ap()
    WV = nc.dram_tensor("WV", [D, E_LOC], F32, kind="ExternalInput").ap()
    WOT = nc.dram_tensor("WOT", [NOC, 128, NDC, 128], F32, kind="ExternalInput").ap()
    FW = nc.dram_tensor("FW", [D, D], F32, kind="ExternalInput").ap()
    FWT = nc.dram_tensor("FWT", [D, D], F32, kind="ExternalInput").ap()
    FB = nc.dram_tensor("FB", [D], F32, kind="ExternalInput").ap()
    COFF = nc.dram_tensor("COFF", [1, 1], U32, kind="ExternalInput").ap()
    OUT = nc.dram_tensor("OUT", [D, TOK], F32, kind="ExternalOutput").ap()

    xc_spill = nc.dram_tensor("xc_spill", [NTT, 128, NDC * 512], BF16).ap()
    wq_spill = nc.dram_tensor("wq_spill", [D, D], BF16).ap()
    xq_spill = nc.dram_tensor("xq_spill", [NTC, 128, NDC * 512], BF16).ap()
    g_spill = nc.dram_tensor("g_spill", [D, TOK], F32).ap()
    m_row = nc.dram_tensor("m_row", [S], BF16).ap()
    rs_row = nc.dram_tensor("rs_row", [S], F32).ap()
    rws_row = nc.dram_tensor("rws_row", [D], F32).ap()
    qs_row = nc.dram_tensor("qs_row", [TOK], F32).ap()
    xs_row = nc.dram_tensor("xs_row", [TOK], F32).ap()
    mq_row = nc.dram_tensor("mq_row", [TOK], BF16).ap()
    cst_in = nc.dram_tensor("cst_in", [2, TOK], F32).ap()
    cst_out = nc.dram_tensor("cst_out", [8, TOK], F32).ap()
    cc_in = nc.dram_tensor("cc_in", [E_LOC, S], BF16).ap()
    cc_out1_t = nc.dram_tensor("cc_out1", [4 * 256, S], BF16)
    cc_out1 = cc_out1_t.ap()
    cc_out2_t = nc.dram_tensor("cc_out2", [4 * 256, S], BF16)
    cc_out2 = cc_out2_t.ap()

    es = ExitStack()
    SB = lambda name, shape, dt: es.enter_context(nc.sbuf_tensor(name, shape, dt))

    KT = SB("KT", [128, H_LOC * S], BF16)
    QT = SB("QT", [128, H_LOC * S], BF16)
    VT = SB("VT", [128, NKC * E_LOC], BF16)
    XQR = SB("XQR", [128, 2 * NDC * 512], BF16)
    WA = SB("WA", [128, NDC * E_LOC], BF16)
    WB = SB("WB", [128, NDC * E_LOC], BF16)
    F32B = SB("F32B", [128, 1024], F32)
    QS2 = SB("QS2", [128, 2 * 1024], F32)
    F32A = QS2
    XTG = SB("XTG", [128, 2 * 1024], F32)
    BRS = SB("BRS", [128, 2 * 512], F32)
    BMT = SB("BMT", [128, 2 * 512], BF16)
    MROW2 = BMT
    NEGWK = SB("NEGWK", [1, E_LOC], BF16)
    NEGWV = SB("NEGWV", [1, E_LOC], BF16)
    NEGWQ = NEGWK
    BQS = SB("BQS", [128, TOK], F32)
    BMQ = SB("BMQ", [128, TOK], BF16)
    BRWS = SB("BRWS", [128, 2048], F32)
    PBUF = SB("PBUF", [128, 2048], BF16)
    ONES = SB("ONES", [128, 128], BF16)
    SUMC = SB("SUMC", [128, 8], F32)
    SQC = SB("SQC", [128, 8], F32)
    MMC = SB("MMC", [128, 8], F32)
    RSM = SB("RSM", [128, 8], F32)
    AXC = SB("AXC", [128, 8], F32)
    XSC = SB("XSC", [128, 8], F32)
    QSC = SB("QSC", [128, 8], F32)
    GSUM = SB("GSUM", [128, 32], F32)
    GSQ = SB("GSQ", [128, 32], F32)
    GM = SB("GM", [128, 32], F32)
    GMB = SB("GMB", [128, 32], BF16)
    GRS = SB("GRS", [128, 32], F32)
    GT0 = SB("GT0", [128, 32], F32)
    WMX = SB("WMX", [128, 16], F32)
    WSC = SB("WSC", [128, 16], F32)
    WRC = SB("WRC", [128, 16], F32)
    BIASC = SB("BIASC", [128, 16], F32)
    MQB = SB("MQB", [128, 8], BF16)
    PSAC = BMT
    AOB = BMQ
    SOFF = SB("SOFF", [1, 1], U32)

    scp = [es.enter_context(nc.psum_tensor(f"scp{i}", [128, 1024], F32)) for i in range(2)]
    pvb_t = [es.enter_context(nc.psum_tensor(f"pv{i}", [128, 512], F32)) for i in range(2)]
    ffb = [es.enter_context(nc.psum_tensor(f"ffn{i}", [128, 512], F32)) for i in range(2)]

    sem_names = [
        "t", "v", "s", "cc1", "cc2", "cc2b",
        "d_xm0", "d_st",
        "d_bc1", "d_bc2", "d_bc3",
        "d_xt0", "d_xt1", "d_xs",
        "d_sp0", "d_sp1",
        "d_xc0", "d_xc1",
        "d_fw", "d_fwt", "d_wqs", "d_xqs", "d_xtm",
        "d_q0", "d_q1", "d_qg0", "d_qg1",
        "d_brt0", "d_brt1",
        "d_wq0", "d_wq1", "d_wq2", "d_wq3",
        "d_xq0", "d_xq1",
        "d_ao0", "d_ao1",
        "d_gs", "d_rb",
        "d_g0", "d_g1", "d_xe0", "d_xe1",
        "d_out0", "d_out1",
    ]
    P = Prog()
    for n in sem_names:
        P.new_sem(n)
    mark = {}

    P.op("vector", lambda e, sm: e.memset(ONES[:, :], 1.0), "v")
    v_ones = P.count("v")

    P.dma("gpsimd", lambda e, sm: e.dma_start(
        out=BIASC[:, :], in_=FB.rearrange("(j p) -> p j", p=128),
        allow_slow_non_contiguous=True), "d_bc1")

    q1cnt = {}
    def _xm1_load(ih):
        i, hf = divmod(ih, 2)
        slot = ih % 2
        semn = "d_q0" if slot == 0 else "d_q1"
        eng = "scalar" if slot == 0 else "sync"
        pv = mark.get(("xm1v", ih - 2))
        if pv is not None:
            P.wait(eng, "v", pv)
            P.wait(eng, "s", mark[("xm1s", ih - 2)])
        P.dma(eng, lambda e, sm, i=i, hf=hf, slot=slot: e.dma_start(
            out=QS2[:, 1024 * slot:1024 * (slot + 1)],
            in_=XM[128 * i:128 * (i + 1), 1024 * hf:1024 * (hf + 1)]), semn)
        q1cnt[ih] = P.count(semn)

    _xm1_load(0)
    _xm1_load(1)
    for ih in range(16):
        i, hf = divmod(ih, 2)
        slot = ih % 2
        semn = "d_q0" if slot == 0 else "d_q1"
        P.wait("vector", semn, q1cnt[ih])
        P.drain("vector")
        P.op("vector", lambda e, sm, slot=slot, hf=hf: e.tensor_reduce(
            GT0[:, 2 * hf:2 * hf + 1], QS2[:, 1024 * slot:1024 * (slot + 1)], AX.X, ALU.add), "v")
        P.wait("scalar", semn, q1cnt[ih])
        if ih > 0:
            P.wait("scalar", "v", mark[("xm1v", ih - 1)])
        P.op("scalar", lambda e, sm, slot=slot: e.square(
            PBUF[:, 0:1024], QS2[:, 1024 * slot:1024 * (slot + 1)]), "s")
        mark[("xm1s", ih)] = P.count("s")
        P.wait("vector", "s", mark[("xm1s", ih)])
        P.drain("vector")
        P.op("vector", lambda e, sm, hf=hf: e.tensor_reduce(
            GT0[:, 2 * hf + 1:2 * hf + 2], PBUF[:, 0:1024], AX.X, ALU.add), "v")
        if hf == 1:
            P.drain("vector")
            P.op("vector", lambda e, sm, i=i: e.tensor_tensor(
                SUMC[:, i:i + 1], GT0[:, 0:1], GT0[:, 2:3], ALU.add), "v")
            P.drain("vector")
            P.op("vector", lambda e, sm, i=i: e.tensor_tensor(
                SQC[:, i:i + 1], GT0[:, 1:2], GT0[:, 3:4], ALU.add), "v")
        mark[("xm1v", ih)] = P.count("v")
        if ih + 2 < 16:
            _xm1_load(ih + 2)
    mark[("xmred", 7)] = mark[("xm1v", 15)]
    mark[("xmsq", 7)] = mark[("xm1s", 15)]

    P.wait("gpsimd", "v", mark[("xmred", 7)])
    P.dma("gpsimd", lambda e, sm: e.dma_start(
        out=cst_in[0:1, :].rearrange("o (j p) -> p (o j)", p=128),
        in_=SUMC[:, :], allow_slow_non_contiguous=True), "d_st")
    P.dma("gpsimd", lambda e, sm: e.dma_start(
        out=cst_in[1:2, :].rearrange("o (j p) -> p (o j)", p=128),
        in_=SQC[:, :], allow_slow_non_contiguous=True), "d_st")
    P.wait("gpsimd", "d_st")
    P.op("gpsimd", lambda e, sm: e.collective_compute(
        "AllGather", ALU.bypass, replica_groups=[[0, 1, 2, 3], [4, 5, 6, 7]],
        ins=[cst_in.opt()], outs=[cst_out.opt()]), "cc1")
    P.wait("gpsimd", "cc1")
    for i in range(4):
        P.dma("gpsimd", lambda e, sm, i=i: e.dma_start(
            out=GSUM[:, 8 * i:8 * (i + 1)],
            in_=cst_out[2 * i:2 * i + 1, :].rearrange("o (j p) -> p (o j)", p=128),
            allow_slow_non_contiguous=True), "d_st")
        P.dma("gpsimd", lambda e, sm, i=i: e.dma_start(
            out=GSQ[:, 8 * i:8 * (i + 1)],
            in_=cst_out[2 * i + 1:2 * i + 2, :].rearrange("o (j p) -> p (o j)", p=128),
            allow_slow_non_contiguous=True), "d_st")
    P.wait("vector", "d_st")
    P.op("vector", lambda e, sm: e.tensor_scalar(GM[:, :], GSUM[:, :], 1.0 / D, None, ALU.mult), "v")
    P.drain("vector")
    P.op("vector", lambda e, sm: e.tensor_mul(GT0[:, :], GM[:, :], GM[:, :]), "v")
    P.drain("vector")
    P.op("vector", lambda e, sm: e.tensor_scalar(GRS[:, :], GSQ[:, :], 1.0 / D, None, ALU.mult), "v")
    P.drain("vector")
    P.op("vector", lambda e, sm: e.tensor_sub(GRS[:, :], GRS[:, :], GT0[:, :]), "v")
    P.drain("vector")
    P.op("vector", lambda e, sm: e.tensor_scalar(GRS[:, :], GRS[:, :], EPS, None, ALU.add), "v")
    mark["grs_pre"] = P.count("v")
    P.wait("scalar", "v", mark["grs_pre"])
    P.op("scalar", lambda e, sm: e.sqrt(GT0[:, :], GRS[:, :]), "s")
    mark["grs_sqrt"] = P.count("s")
    P.wait("vector", "s", mark["grs_sqrt"])
    P.op("vector", lambda e, sm: e.reciprocal(GRS[:, :], GT0[:, :]), "v")
    P.drain("vector")
    P.op("vector", lambda e, sm: e.tensor_copy(GMB[:, :], GM[:, :]), "v")
    mark["gstats"] = P.count("v")
    P.wait("gpsimd", "v", mark["gstats"])
    P.dma("gpsimd", lambda e, sm: e.dma_start(
        out=m_row.rearrange("(j p) -> p j", p=128), in_=GMB[:, :],
        allow_slow_non_contiguous=True), "d_st")
    P.dma("gpsimd", lambda e, sm: e.dma_start(
        out=rs_row.rearrange("(j p) -> p j", p=128), in_=GRS[:, :],
        allow_slow_non_contiguous=True), "d_st")
    mark["rows_mrs"] = P.count("d_st")
    P.wait("gpsimd", "d_st")

    P.op("vector", lambda e, sm: e.tensor_scalar(MMC[:, :], SUMC[:, :], 1.0 / D, None, ALU.mult), "v")
    P.drain("vector")
    P.op("vector", lambda e, sm: e.tensor_mul(GT0[:, 0:8], MMC[:, :], MMC[:, :]), "v")
    P.drain("vector")
    P.op("vector", lambda e, sm: e.tensor_scalar(RSM[:, :], SQC[:, :], 1.0 / D, None, ALU.mult), "v")
    P.drain("vector")
    P.op("vector", lambda e, sm: e.tensor_sub(RSM[:, :], RSM[:, :], GT0[:, 0:8]), "v")
    P.drain("vector")
    P.op("vector", lambda e, sm: e.tensor_scalar(RSM[:, :], RSM[:, :], EPS, None, ALU.add), "v")
    mark["rsm_pre"] = P.count("v")
    P.wait("scalar", "v", mark["rsm_pre"])
    P.op("scalar", lambda e, sm: e.sqrt(GT0[:, 0:8], RSM[:, :]), "s")
    mark["rsm_sqrt"] = P.count("s")
    P.wait("vector", "s", mark["rsm_sqrt"])
    P.op("vector", lambda e, sm: e.reciprocal(RSM[:, :], GT0[:, 0:8]), "v")
    P.drain("vector")
    P.op("vector", lambda e, sm: e.tensor_copy(MQB[:, :], MMC[:, :]), "v")
    mark["minestats"] = P.count("v")

    QITEMS = []
    QLI = []

    def q_add(load, comp):
        QLI.append(None if load is None else sum(1 for x in QITEMS if x[0] is not None))
        QITEMS.append((load, comp))

    qcnt = {}
    qvmark = {}
    q_li_item = {}
    qemit = [0]
    qcp = [0]

    def _q_emit_load(k):
        load = QITEMS[k][0]
        if load is None:
            return
        li = QLI[k]
        q_li_item[li] = k
        slot = li % 2
        semn = "d_qg0" if slot == 0 else "d_qg1"
        prev_item = q_li_item.get(li - 2)
        if prev_item is not None:
            P.wait("gpsimd" if slot == 0 else "sync", "v", qvmark[prev_item])
        load(slot, semn)
        qcnt[k] = P.count(semn)

    def pump(n):
        for _ in range(n):
            k = qcp[0]
            if k >= len(QITEMS):
                return
            while qemit[0] < min(k + 2, len(QITEMS)):
                _q_emit_load(qemit[0])
                qemit[0] += 1
            load, comp = QITEMS[k]
            if load is not None:
                slot = QLI[k] % 2
                semn = "d_qg0" if slot == 0 else "d_qg1"
                P.wait("vector", semn, qcnt[k])
                comp(slot, semn)
            else:
                comp(None, None)
            qvmark[k] = P.count("v")
            qcp[0] += 1

    def pump_all():
        pump(len(QITEMS) - qcp[0])

    def q_xm2(i, hf):
        def load(slot, semn, i=i, hf=hf):
            if i == 0 and hf == 0:
                P.wait("gpsimd", "v", mark["minestats"])
            P.dma("gpsimd" if slot == 0 else "sync", lambda e, sm, i=i, hf=hf, slot=slot: e.dma_start(
                out=QS2[:, 1024 * slot:1024 * (slot + 1)],
                in_=XM[128 * i:128 * (i + 1), 1024 * hf:1024 * (hf + 1)]), semn)

        def comp(slot, semn, i=i, hf=hf):
            P.drain("vector")
            P.op("vector", lambda e, sm, i=i, slot=slot: e.tensor_scalar(
                QS2[:, 1024 * slot:1024 * (slot + 1)], QS2[:, 1024 * slot:1024 * (slot + 1)],
                MMC[:, i:i + 1], None, ALU.subtract), "v")
            P.drain("vector")
            P.op("vector", lambda e, sm, hf=hf, slot=slot: e.tensor_reduce(
                GT0[:, 8 + hf:9 + hf], QS2[:, 1024 * slot:1024 * (slot + 1)],
                AX.X, ALU.max, apply_absolute_value=True), "v")
            if hf == 1:
                P.drain("vector")
                P.op("vector", lambda e, sm, i=i: e.tensor_tensor(
                    AXC[:, i:i + 1], GT0[:, 8:9], GT0[:, 9:10], ALU.max), "v")
                mark[("axred", i)] = P.count("v")
        return load, comp

    def q_xscales():
        def comp(slot, semn):
            P.drain("vector")
            P.op("vector", lambda e, sm: e.tensor_mul(XSC[:, :], AXC[:, :], RSM[:, :]), "v")
            P.drain("vector")
            P.op("vector", lambda e, sm: e.tensor_scalar(
                XSC[:, :], XSC[:, :], QMAX_INV, 1e-12, ALU.mult, ALU.max), "v")
            P.drain("vector")
            P.op("vector", lambda e, sm: e.reciprocal(QSC[:, :], XSC[:, :]), "v")
            P.drain("vector")
            P.op("vector", lambda e, sm: e.tensor_mul(QSC[:, :], QSC[:, :], RSM[:, :]), "v")
            mark["qscales"] = P.count("v")
            P.wait("gpsimd", "v", mark["qscales"])
            P.dma("gpsimd", lambda e, sm: e.dma_start(
                out=qs_row.rearrange("(j p) -> p j", p=128), in_=QSC[:, :],
                allow_slow_non_contiguous=True), "d_st")
            P.dma("gpsimd", lambda e, sm: e.dma_start(
                out=xs_row.rearrange("(j p) -> p j", p=128), in_=XSC[:, :],
                allow_slow_non_contiguous=True), "d_st")
            P.dma("gpsimd", lambda e, sm: e.dma_start(
                out=mq_row.rearrange("(j p) -> p j", p=128), in_=MQB[:, :],
                allow_slow_non_contiguous=True), "d_st")
            P.wait("gpsimd", "d_st")
            P.dma("gpsimd", lambda e, sm: e.dma_start(
                out=BQS[:, :], in_=qs_row.rearrange("(o a) -> o a", o=1).broadcast_to([128, TOK])), "d_bc1")
            P.dma("gpsimd", lambda e, sm: e.dma_start(
                out=BMQ[:, :], in_=mq_row.rearrange("(o a) -> o a", o=1).broadcast_to([128, TOK])), "d_bc1")
            mark["bq_ready"] = P.count("d_bc1")
        return None, comp

    def q_fw(i, hf):
        def load(slot, semn, i=i, hf=hf):
            P.dma("gpsimd" if slot == 0 else "sync", lambda e, sm, i=i, hf=hf, slot=slot: e.dma_start(
                out=QS2[:, 1024 * slot:1024 * (slot + 1)],
                in_=FW[128 * i:128 * (i + 1), 1024 * hf:1024 * (hf + 1)]), semn)

        def comp(slot, semn, i=i, hf=hf):
            P.drain("vector")
            P.op("vector", lambda e, sm, hf=hf, slot=slot: e.tensor_reduce(
                GT0[:, 8 + hf:9 + hf], QS2[:, 1024 * slot:1024 * (slot + 1)],
                AX.X, ALU.max, apply_absolute_value=True), "v")
            if hf == 1:
                P.drain("vector")
                P.op("vector", lambda e, sm, i=i: e.tensor_tensor(
                    WMX[:, i:i + 1], GT0[:, 8:9], GT0[:, 9:10], ALU.max), "v")
                mark[("fwred", i)] = P.count("v")
        return load, comp

    def q_wscales():
        def comp(slot, semn):
            P.drain("vector")
            P.op("vector", lambda e, sm: e.tensor_scalar(
                WSC[:, :], WMX[:, :], 1e-8, QMAX_INV, ALU.max, ALU.mult), "v")
            P.drain("vector")
            P.op("vector", lambda e, sm: e.reciprocal(WRC[:, :], WSC[:, :]), "v")
            mark["wscales"] = P.count("v")
            P.wait("gpsimd", "v", mark["wscales"])
            P.dma("gpsimd", lambda e, sm: e.dma_start(
                out=rws_row.rearrange("(j p) -> p j", p=128), in_=WRC[:, :],
                allow_slow_non_contiguous=True), "d_st")
            P.wait("gpsimd", "d_st")
            P.dma("gpsimd", lambda e, sm: e.dma_start(
                out=BRWS[:, :], in_=rws_row.rearrange("(o a) -> o a", o=1).broadcast_to([128, 2048])), "d_bc3")
            mark["brws_ready"] = P.count("d_bc3")
        return None, comp

    def q_fwt(dc, h2):
        def load(slot, semn, dc=dc, h2=h2):
            P.dma("gpsimd" if slot == 0 else "sync", lambda e, sm, dc=dc, h2=h2, slot=slot: e.dma_start(
                out=QS2[:, 1024 * slot:1024 * (slot + 1)],
                in_=FWT[128 * dc:128 * (dc + 1), 1024 * h2:1024 * (h2 + 1)]), semn)

        def comp(slot, semn, dc=dc, h2=h2):
            if dc == 0 and h2 == 0:
                P.wait("vector", "d_bc3", mark["brws_ready"])
            if h2 == 0:
                prevsp = mark.get(("wqsp", dc - 1))
                if prevsp is not None:
                    P.wait("vector", "d_wqs", prevsp)
            P.drain("vector")
            P.op("vector", lambda e, sm, h2=h2, slot=slot: e.tensor_tensor(
                F32B[:, :], QS2[:, 1024 * slot:1024 * (slot + 1)],
                BRWS[:, 1024 * h2:1024 * (h2 + 1)], ALU.mult), "v")
            P.drain("vector")
            P.op("vector", lambda e, sm: e.tensor_scalar(
                F32B[:, :], F32B[:, :], MAGIC, MAGIC, ALU.add, ALU.subtract), "v")
            P.drain("vector")
            P.op("vector", lambda e, sm: e.tensor_copy(
                F32B[:, :].bitcast(I32), F32B[:, :].bitcast(I32)), "v")
            P.drain("vector")
            P.op("vector", lambda e, sm, h2=h2: e.tensor_copy(
                PBUF[:, 1024 * h2:1024 * (h2 + 1)], F32B[:, :]), "v")
            if h2 == 1:
                mark[("wqq", dc)] = P.count("v")
                P.wait("gpsimd", "v", mark[("wqq", dc)])
                P.dma("gpsimd", lambda e, sm, dc=dc: e.dma_start(
                    out=wq_spill[128 * dc:128 * (dc + 1), :], in_=PBUF[:, :]), "d_wqs")
                mark[("wqsp", dc)] = P.count("d_wqs")
        return load, comp

    def q_xq(dc):
        def load(slot, semn, dc=dc):
            P.dma("gpsimd" if slot == 0 else "sync", lambda e, sm, dc=dc, slot=slot: e.dma_start(
                out=QS2[:, 1024 * slot:1024 * (slot + 1)],
                in_=XTM[128 * dc:128 * (dc + 1), :]), semn)

        def comp(slot, semn, dc=dc):
            if dc == 0:
                P.wait("vector", "d_bc1", mark["bq_ready"])
                P.wait("vector", "d_wqs", mark[("wqsp", NDC - 1)])
            else:
                prevsp = mark.get(("xqsp", dc - 1))
                if prevsp is not None:
                    P.wait("vector", "d_xqs", prevsp)
            P.drain("vector")
            P.op("vector", lambda e, sm, slot=slot: e.tensor_tensor(
                F32B[:, :], QS2[:, 1024 * slot:1024 * (slot + 1)], BMQ[:, :], ALU.subtract), "v")
            P.drain("vector")
            P.op("vector", lambda e, sm: e.tensor_mul(F32B[:, :], F32B[:, :], BQS[:, :]), "v")
            P.drain("vector")
            P.op("vector", lambda e, sm: e.tensor_scalar(
                F32B[:, :], F32B[:, :], MAGIC, MAGIC, ALU.add, ALU.subtract), "v")
            P.drain("vector")
            P.op("vector", lambda e, sm: e.tensor_copy(
                F32B[:, :].bitcast(I32), F32B[:, :].bitcast(I32)), "v")
            P.drain("vector")
            P.op("vector", lambda e, sm: e.tensor_copy(PBUF[:, 0:1024], F32B[:, :]), "v")
            mark[("xqq", dc)] = P.count("v")
            P.wait("gpsimd", "v", mark[("xqq", dc)])
            for tc in range(NTC):
                P.dma("gpsimd", lambda e, sm, dc=dc, tc=tc: e.dma_start(
                    out=xq_spill[tc, :, 512 * dc:512 * (dc + 1)],
                    in_=PBUF[:, 512 * tc:512 * (tc + 1)]), "d_xqs")
            mark[("xqsp", dc)] = P.count("d_xqs")
        return load, comp

    def q_xqfin():
        def comp(slot, semn):
            mark["xq_done"] = P.count("d_xqs")
            P.wait("gpsimd", "v", mark[("xqq", NDC - 1)])
            P.dma("gpsimd", lambda e, sm: e.dma_start(
                out=BQS[:, :], in_=xs_row.rearrange("(o a) -> o a", o=1).broadcast_to([128, TOK])), "d_bc2")
            mark["bxs_ready"] = P.count("d_bc2")
        return None, comp

    for i in range(8):
        for hf in range(2):
            q_add(*q_xm2(i, hf))
    q_add(*q_xscales())
    for i in range(NOC):
        for hf in range(2):
            q_add(*q_fw(i, hf))
    q_add(*q_wscales())
    for dc in range(NDC):
        for h2 in range(2):
            q_add(*q_fwt(dc, h2))
    for dc in range(NDC):
        q_add(*q_xq(dc))
    q_add(*q_xqfin())

    for wi, (srcw, dst) in enumerate([(WK, WA), (WV, WB)]):
        for g in range(8):
            k = wi * 8 + g
            slot = k % 2
            semn = "d_xt0" if slot == 0 else "d_xt1"
            prev = mark.get(("wcast", k - 2))
            if prev is not None:
                P.wait("sync", "s", prev)
            P.dma("sync", lambda e, sm, srcw=srcw, g=g, slot=slot: e.dma_start(
                out=XTG[:, 1024 * slot:1024 * (slot + 1)].rearrange("p (d c) -> p d c", d=2),
                in_=srcw[256 * g:256 * (g + 1), :].rearrange("(d p) c -> p d c", p=128)), semn)
            P.wait("scalar", semn)
            P.op("scalar", lambda e, sm, dst=dst, g=g, slot=slot: e.copy(
                dst[:, 1024 * g:1024 * (g + 1)], XTG[:, 1024 * slot:1024 * (slot + 1)]), "s")
            mark[("wcast", k)] = P.count("s")
    mark["wkv_ready_s"] = P.count("s")

    P.wait("tensor", "v", v_ones)
    for _ in range(64):
        P.op("tensor", lambda e, sm: e.matmul(
            ffb[0][:, 0:128], ONES[:, :], ONES[:, 0:128], start=True, stop=True,
            skip_group_check=True), None)

    P.wait("tensor", "s", mark["wkv_ready_s"])
    for wi, (wsrc, bank) in enumerate([(WA, pvb_t[0]), (WB, pvb_t[1])]):
        for dc in range(NDC):
            P.op("tensor", lambda e, sm, wsrc=wsrc, bank=bank, dc=dc: e.matmul(
                bank[:, :], ONES[:, :], wsrc[:, 512 * dc:512 * (dc + 1)],
                start=(dc == 0), stop=(dc == NDC - 1)), "t" if dc == NDC - 1 else None)
        mark[("wsum", wi)] = P.count("t")
    for wi, (bank, dstrow) in enumerate([(pvb_t[0], NEGWK), (pvb_t[1], NEGWV)]):
        P.wait("vector", "t", mark[("wsum", wi)])
        P.op("vector", lambda e, sm, bank=bank, dstrow=dstrow: e.tensor_scalar(
            dstrow[0:1, :], bank[0:1, :], -1.0, None, ALU.mult), "v")
    mark["negw_kv"] = P.count("v")

    def xcbuf(tt):
        return XQR, (tt % 2) * 8192

    def sw1_produce(tt):
        slot = tt % 2
        brt = "d_brt0" if slot == 0 else "d_brt1"
        P.wait("sync", "d_st", mark["rows_mrs"])
        prev = mark.get(("kevac_g", (tt - 2) * H_LOC + H_LOC - 1))
        if prev is not None:
            P.wait("sync", "v", prev)
        prev_mm = mark.get(("sw1_mm", tt - 2))
        if prev_mm is not None:
            P.wait("sync", "t", prev_mm)
        P.dma("sync", lambda e, sm, tt=tt, slot=slot: e.dma_start(
            out=BRS[:, 512 * slot:512 * (slot + 1)],
            in_=rs_row[512 * tt:512 * (tt + 1)].rearrange("(o a) -> o a", o=1).broadcast_to([128, 512])), brt)
        P.dma("sync", lambda e, sm, tt=tt, slot=slot: e.dma_start(
            out=MROW2[0:1, 512 * slot:512 * (slot + 1)],
            in_=m_row[512 * tt:512 * (tt + 1)].rearrange("(o a) -> o a", o=1)), brt)
        mark[("brt", tt)] = P.count(brt)

        xcb, xcoff = xcbuf(tt)
        for g in range(8):
            kg = tt * 8 + g
            xslot = kg % 2
            if xslot == 0:
                xsem = "d_xt0"
                prev = mark.get(("xcast", kg - 2))
                if prev is not None:
                    P.wait("sync", "s", prev)
                elif kg < 2:
                    P.wait("sync", "s", mark["wkv_ready_s"])
                if g == 0:
                    prev_mm = mark.get(("sw1_mm", tt - 2))
                    if prev_mm is not None:
                        P.wait("sync", "t", prev_mm)
                P.dma("sync", lambda e, sm, tt=tt, g=g, xslot=xslot: e.dma_start(
                    out=XTG[:, 1024 * xslot:1024 * (xslot + 1)].rearrange("p (d c) -> p d c", d=2),
                    in_=XT[tt, 2 * g:2 * (g + 1)].rearrange("d p c -> p d c")), xsem)
            else:
                xsem = "d_xs"
                prev = mark.get(("xcast", kg - 2))
                P.wait("scalar", "s", prev if prev is not None else mark["wkv_ready_s"])
                P.dma("scalar", lambda e, sm, tt=tt, g=g, xslot=xslot: e.dma_start(
                    out=XTG[:, 1024 * xslot:1024 * (xslot + 1)].rearrange("p (d c) -> p d c", d=2),
                    in_=XT[tt, 2 * g:2 * (g + 1)].rearrange("d p c -> p d c")), xsem)
            P.wait("scalar", xsem)
            P.op("scalar", lambda e, sm, xcb=xcb, xcoff=xcoff, g=g, xslot=xslot: e.copy(
                xcb[:, xcoff + 1024 * g:xcoff + 1024 * (g + 1)],
                XTG[:, 1024 * xslot:1024 * (xslot + 1)]), "s")
            mark[("xcast", kg)] = P.count("s")

    def sw1_compute(tt):
        slot = tt % 2
        xcb, xcoff = xcbuf(tt)
        if tt == 0:
            P.wait("tensor", "s", mark["wkv_ready_s"])
            P.wait("tensor", "v", mark["negw_kv"])
        for ec in range(H_LOC):
            bank = pvb_t[ec % 2]
            gk = tt * H_LOC + ec
            prev = mark.get(("kevac_g", gk - 2))
            if prev is not None:
                P.wait("tensor", "v", prev)
            if ec == 0:
                P.wait("tensor", "d_brt0" if slot == 0 else "d_brt1", mark[("brt", tt)])
            for dc in range(NDC):
                if dc == 0 and ec == 0:
                    P.wait("tensor", "s", mark[("xcast", tt * 8 + 7)])
                P.op("tensor", lambda e, sm, bank=bank, ec=ec, dc=dc, xcb=xcb, xcoff=xcoff: e.matmul(
                    bank[:, :],
                    WA[:, 512 * dc + 128 * ec:512 * dc + 128 * (ec + 1)],
                    xcb[:, xcoff + 512 * dc:xcoff + 512 * (dc + 1)],
                    start=(dc == 0), stop=False), None)
            P.op("tensor", lambda e, sm, bank=bank, ec=ec, slot=slot: e.matmul(
                bank[:, :], NEGWK[0:1, 128 * ec:128 * (ec + 1)],
                MROW2[0:1, 512 * slot:512 * (slot + 1)], start=False, stop=True), "t")
            mark[("kmm", tt, ec)] = P.count("t")
            P.wait("vector", "t", mark[("kmm", tt, ec)])
            if ec == 0:
                P.wait("vector", "d_brt0" if slot == 0 else "d_brt1", mark[("brt", tt)])
            P.op("vector", lambda e, sm, bank=bank, ec=ec, tt=tt, slot=slot: e.tensor_tensor(
                KT[:, S * ec + 512 * tt:S * ec + 512 * (tt + 1)],
                bank[:, :], BRS[:, 512 * slot:512 * (slot + 1)], ALU.mult), "v")
            mark[("kevac_g", tt * H_LOC + ec)] = P.count("v")

        for tc in range(4):
            bank = ffb[tc % 2]
            gk = tt * 4 + tc
            prev = mark.get(("vevac_g", gk - 2))
            if prev is not None:
                P.wait("tensor", "s", prev)
            for dc in range(NDC):
                P.op("tensor", lambda e, sm, bank=bank, tc=tc, dc=dc, xcb=xcb, xcoff=xcoff: e.matmul(
                    bank[:, :],
                    xcb[:, xcoff + 512 * dc + 128 * tc:xcoff + 512 * dc + 128 * (tc + 1)],
                    WB[:, 512 * dc:512 * (dc + 1)],
                    start=(dc == 0), stop=False), None)
            P.op("tensor", lambda e, sm, bank=bank, slot=slot, tc=tc: e.matmul(
                bank[:, :], MROW2[0:1, 512 * slot + 128 * tc:512 * slot + 128 * (tc + 1)],
                NEGWV[0:1, 0:E_LOC], start=False, stop=True), "t")
            mark[("vmm", tt, tc)] = P.count("t")
            ck = tt * 4 + tc
            P.wait("scalar", "t", mark[("vmm", tt, tc)])
            if ck == 0:
                P.wait("scalar", "v", mark["gstats"])
            P.op("scalar", lambda e, sm, bank=bank, ck=ck: e.activation(
                VT[:, 512 * ck:512 * (ck + 1)], bank[:, :], AF.Identity,
                bias=0.0, scale=GRS[:, ck:ck + 1]), "s")
            mark[("vevac_g", ck)] = P.count("s")
        mark[("sw1_mm", tt)] = P.count("t")
        pump(10)

    sw1_produce(0)
    for tt in range(NTT):
        if tt + 1 < NTT:
            sw1_produce(tt + 1)
        sw1_compute(tt)

    P.wait("sync", "t", mark[("sw1_mm", NTT - 1)])
    for g in range(8):
        slot = g % 2
        semn = "d_xt0" if slot == 0 else "d_xt1"
        prev = mark.get(("wqcast", g - 2))
        if prev is not None:
            P.wait("sync", "s", prev)
        elif g < 2:
            P.wait("sync", "s", mark[("xcast", NTT * 8 - 1)])
        P.dma("sync", lambda e, sm, g=g, slot=slot: e.dma_start(
            out=XTG[:, 1024 * slot:1024 * (slot + 1)].rearrange("p (d c) -> p d c", d=2),
            in_=WQ[256 * g:256 * (g + 1), :].rearrange("(d p) c -> p d c", p=128)), semn)
        P.wait("scalar", semn)
        if g == 0:
            P.wait("scalar", "t", mark[("sw1_mm", NTT - 1)])
        P.op("scalar", lambda e, sm, g=g, slot=slot: e.copy(
            WA[:, 1024 * g:1024 * (g + 1)], XTG[:, 1024 * slot:1024 * (slot + 1)]), "s")
        mark[("wqcast", g)] = P.count("s")
    mark["wq_cast_done"] = P.count("s")

    P.wait("tensor", "s", mark["wq_cast_done"])
    for dc in range(NDC):
        P.op("tensor", lambda e, sm, dc=dc: e.matmul(
            ffb[0][:, :], ONES[:, :], WA[:, 512 * dc:512 * (dc + 1)],
            start=(dc == 0), stop=(dc == NDC - 1)), "t" if dc == NDC - 1 else None)
    mark["wsumq"] = P.count("t")
    P.wait("vector", "t", mark["wsumq"])
    P.op("vector", lambda e, sm: e.tensor_scalar(
        NEGWQ[0:1, :], ffb[0][0:1, :], -1.0, None, ALU.mult), "v")
    mark["negwq"] = P.count("v")

    def sw2_produce(tt):
        xcb, xcoff = xcbuf(tt)
        for g in range(8):
            kg = tt * 8 + g
            xslot = kg % 2
            if xslot == 0:
                xsem = "d_xt0"
                prev = mark.get(("xcast2", kg - 2))
                if prev is not None:
                    P.wait("sync", "s", prev)
                elif kg < 2:
                    P.wait("sync", "s", mark["wq_cast_done"])
                if g == 0:
                    prev_mm = mark.get(("qmm_all", tt - 2))
                    if prev_mm is not None:
                        P.wait("sync", "t", prev_mm)
                    else:
                        P.wait("sync", "t", mark[("sw1_mm", NTT - 1)])
                P.dma("sync", lambda e, sm, tt=tt, g=g, xslot=xslot: e.dma_start(
                    out=XTG[:, 1024 * xslot:1024 * (xslot + 1)].rearrange("p (d c) -> p d c", d=2),
                    in_=XT[tt, 2 * g:2 * (g + 1)].rearrange("d p c -> p d c")), xsem)
            else:
                xsem = "d_xs"
                prev = mark.get(("xcast2", kg - 2))
                P.wait("scalar", "s", prev if prev is not None else mark["wq_cast_done"])
                P.dma("scalar", lambda e, sm, tt=tt, g=g, xslot=xslot: e.dma_start(
                    out=XTG[:, 1024 * xslot:1024 * (xslot + 1)].rearrange("p (d c) -> p d c", d=2),
                    in_=XT[tt, 2 * g:2 * (g + 1)].rearrange("d p c -> p d c")), xsem)
            P.wait("scalar", xsem)
            P.op("scalar", lambda e, sm, xcb=xcb, xcoff=xcoff, g=g, xslot=xslot: e.copy(
                xcb[:, xcoff + 1024 * g:xcoff + 1024 * (g + 1)],
                XTG[:, 1024 * xslot:1024 * (xslot + 1)]), "s")
            mark[("xcast2", kg)] = P.count("s")
        mark[("xcld", tt)] = P.count("s")
        slot = tt % 2
        brt = "d_brt0" if slot == 0 else "d_brt1"
        prev = mark.get(("qevac_g", (tt - 2) * H_LOC + 3))
        if prev is not None:
            P.wait("sync", "v", prev)
        else:
            P.wait("sync", "v", mark[("kevac_g", NTT * H_LOC - 1)])
        P.dma("sync", lambda e, sm, tt=tt, slot=slot: e.dma_start(
            out=BRS[:, 512 * slot:512 * (slot + 1)],
            in_=rs_row[512 * tt:512 * (tt + 1)].rearrange("(o a) -> o a", o=1).broadcast_to([128, 512])), brt)
        P.dma("sync", lambda e, sm, tt=tt, slot=slot: e.dma_start(
            out=MROW2[0:1, 512 * slot:512 * (slot + 1)],
            in_=m_row[512 * tt:512 * (tt + 1)].rearrange("(o a) -> o a", o=1)), brt)
        mark[("brt2", tt)] = P.count(brt)

    def sw2_compute(tt):
        xcb, xcoff = xcbuf(tt)
        slot = tt % 2
        for ec in range(H_LOC):
            bank = pvb_t[ec % 2]
            gk = tt * H_LOC + ec
            prev = mark.get(("qevac_g", gk - 2))
            if prev is not None:
                P.wait("tensor", "v", prev)
            elif gk < 2:
                P.wait("tensor", "v", mark["negwq"])
            if ec == 0:
                P.wait("tensor", "d_brt0" if slot == 0 else "d_brt1", mark[("brt2", tt)])
            for dc in range(NDC):
                if dc == 0 and ec == 0:
                    P.wait("tensor", "s", mark[("xcld", tt)])
                P.op("tensor", lambda e, sm, bank=bank, ec=ec, dc=dc, xcb=xcb, xcoff=xcoff: e.matmul(
                    bank[:, :],
                    WA[:, 512 * dc + 128 * ec:512 * dc + 128 * (ec + 1)],
                    xcb[:, xcoff + 512 * dc:xcoff + 512 * (dc + 1)],
                    start=(dc == 0), stop=False), None)
            P.op("tensor", lambda e, sm, bank=bank, ec=ec, slot=slot: e.matmul(
                bank[:, :], NEGWQ[0:1, 128 * ec:128 * (ec + 1)],
                MROW2[0:1, 512 * slot:512 * (slot + 1)], start=False, stop=True), "t")
            mark[("qmm", tt, ec)] = P.count("t")
            slot = tt % 2
            brt = "d_brt0" if slot == 0 else "d_brt1"
            P.wait("vector", "t", mark[("qmm", tt, ec)])
            if ec == 0:
                P.wait("vector", brt, mark[("brt2", tt)])
            P.op("vector", lambda e, sm, bank=bank, ec=ec, tt=tt, slot=slot: e.tensor_tensor(
                QT[:, S * ec + 512 * tt:S * ec + 512 * (tt + 1)],
                bank[:, :], BRS[:, 512 * slot:512 * (slot + 1)], ALU.mult), "v")
            mark[("qevac_g", tt * H_LOC + ec)] = P.count("v")
        mark[("qmm_all", tt)] = P.count("t")
        pump(4)

    sw2_produce(0)
    sw2_produce(1)
    for tt in range(NTT):
        sw2_compute(tt)
        if tt + 2 < NTT:
            sw2_produce(tt + 2)
    pump_all()
    mark["sweep2_done_t"] = P.count("t")
    mark["sweep2_done_v"] = P.count("v")

    def emit_xq_load(tc, slot):
        semn = "d_xq0" if slot == 0 else "d_xq1"
        P.dma("sync", lambda e, sm, tc=tc, slot=slot: e.dma_start(
            out=XQR[:, 8192 * slot:8192 * (slot + 1)], in_=xq_spill[tc]), semn)
        return P.count(semn)

    def emit_wq_load(blk):
        slot = blk % 2
        semn = f"d_wq{slot}"
        P.dma("scalar", lambda e, sm, blk=blk, slot=slot: e.dma_start(
            out=WB[:, 4096 * slot:4096 * (slot + 1)].rearrange("p (d c) -> p d c", d=16),
            in_=wq_spill[:, 256 * blk:256 * (blk + 1)].rearrange("(d p) c -> p d c", p=128)), semn)
        return P.count(semn)

    P.wait("sync", "d_xqs", mark["xq_done"])
    P.wait("sync", "t", mark["sweep2_done_t"])
    mark[("xqld", 0)] = emit_xq_load(0, 0)
    P.wait("scalar", "d_wqs", mark[("wqsp", NDC - 1)])
    P.wait("scalar", "t", mark["sweep2_done_t"])
    wq_marks = {}
    for blk in range(2):
        wq_marks[blk] = emit_wq_load(blk)

    BDEN = [(BRWS, 0), (BRWS, 512)]
    FFT1 = (BRWS, 1024)
    GST = (BRWS, 1536)

    n_it = H_LOC * NQT
    NP = NKC // 2

    P.wait("scalar", "v", mark[("xqq", NDC - 1)])
    P.wait("scalar", "d_wqs", mark[("wqsp", NDC - 1)])
    P.wait("scalar", "d_xqs", mark["xq_done"])
    P.wait("scalar", "d_bc1", mark["bq_ready"])
    P.wait("scalar", "d_bc2", mark["bxs_ready"])
    P.wait("vector", "d_bc2", mark["bxs_ready"])
    P.wait("vector", "d_xqs", mark["xq_done"])
    P.wait("tensor", "v", v_ones)

    pend_den = [None]
    for it in range(n_it):
        h, qt = divmod(it, NQT)
        f_tc, f_oc = divmod(it, NOC)
        fbank = ffb[it % 2]
        pvb = pvb_t[it % 2]

        if it == 8:
            mark[("xqld", 1)] = emit_xq_load(1, 1)

        ffn_dc = 0
        for p in range(NP + 2):
            if p >= 2:
                pp = p - 2
                for half in range(2):
                    kcp = 2 * pp + half
                    if kcp == 0:
                        prev = mark.get(("aoevac", it - 2))
                        if prev is not None:
                            P.wait("tensor", "v", prev)
                        elif it < 2:
                            P.wait("tensor", "v", mark["sweep2_done_v"])
                    P.wait("tensor", "s", mark[("exp", it, pp)])
                    P.op("tensor", lambda e, sm, h=h, kcp=kcp, pvb=pvb, pp=pp, half=half: e.matmul(
                        pvb[:, :],
                        VT[:, 512 * kcp + 128 * h:512 * kcp + 128 * (h + 1)],
                        PBUF[:, 1024 * (pp % 2) + 512 * half:1024 * (pp % 2) + 512 * (half + 1)],
                        start=(kcp == 0), stop=(kcp == NKC - 1), skip_group_check=True),
                        "t" if kcp == NKC - 1 else None)
                    if kcp == NKC - 1:
                        mark[("pvmm", it)] = P.count("t")
                if ffn_dc < NDC:
                    dc = ffn_dc
                    ffn_dc += 1
                    gb = it // 2
                    wslot2 = gb % 2
                    if dc == 0:
                        P.wait("tensor", f"d_wq{wslot2}", wq_marks[gb])
                        P.wait("tensor", "d_xq0" if f_tc == 0 else "d_xq1", mark[("xqld", f_tc)])
                        prevf = mark.get(("recip", it - 2))
                        if prevf is not None:
                            P.wait("tensor", "v", prevf)
                        elif it < 2:
                            P.wait("tensor", "s", mark[("vevac_g", NTT * 4 - 1)])
                    P.op("tensor", lambda e, sm, f_oc=f_oc, f_tc=f_tc, dc=dc, fbank=fbank, wslot2=wslot2: e.matmul(
                        fbank[:, :],
                        WB[:, 4096 * wslot2 + 256 * dc + 128 * (f_oc % 2):4096 * wslot2 + 256 * dc + 128 * (f_oc % 2) + 128],
                        XQR[:, 8192 * f_tc + 512 * dc:8192 * f_tc + 512 * (dc + 1)],
                        start=(dc == 0), stop=(dc == NDC - 1), skip_group_check=True),
                        "t" if dc == NDC - 1 else None)
                    if dc == NDC - 1:
                        mark[("ffnmm", it)] = P.count("t")
            if p < NP:
                scb = scp[p % 2]
                prev = mark.get(("exp", it, p - 2))
                if prev is None:
                    prev = mark.get(("exp", it - 1, NP - 2 + p))
                if prev is not None:
                    P.wait("tensor", "s", prev)
                for half in range(2):
                    kc = 2 * p + half
                    P.op("tensor", lambda e, sm, h=h, qt=qt, kc=kc, scb=scb, half=half: e.matmul(
                        scb[:, 512 * half:512 * (half + 1)],
                        KT[:, S * h + 128 * kc:S * h + 128 * (kc + 1)],
                        QT[:, S * h + 512 * qt:S * h + 512 * (qt + 1)],
                        start=True, stop=True, skip_group_check=True), "t")
                mark[("sc", it, p)] = P.count("t")
                if p == 0 and pend_den[0] is not None:
                    pend_den[0]()
                    pend_den[0] = None
                    if it == 16:
                        P.wait("gpsimd", "d_ao0")
                        P.wait("gpsimd", "d_ao1")
                        P.op("gpsimd", lambda e, sm: e.collective_compute(
                            "AllGather", ALU.bypass, replica_groups=[[0, 1, 2, 3], [4, 5, 6, 7]],
                            ins=[cc_in[0:256, :].opt()], outs=[cc_out1.opt()]), "cc2")
                P.wait("scalar", "t", mark[("sc", it, p)])
                prevv = mark.get(("add", it, p - 2)) if p >= 2 else mark.get(("add", it - 1, NP - 1))
                if prevv is not None:
                    P.wait("scalar", "v", prevv)
                P.op("scalar", lambda e, sm, p=p, scb=scb: e.activation(
                    PBUF[:, 1024 * (p % 2):1024 * (p % 2 + 1)], scb[:, :],
                    AF.Exp, scale=SCL_ATTN), "s")
                mark[("exp", it, p)] = P.count("s")
                P.wait("vector", "s", mark[("exp", it, p)])
                if p == 0:
                    prevd = mark.get(("denmm", it - 1))
                    if prevd is not None:
                        P.wait("vector", "t", prevd)
                    P.op("vector", lambda e, sm, p=p: e.tensor_copy(
                        PSAC[:, :], PBUF[:, 1024 * (p % 2):1024 * (p % 2 + 1)]), "v")
                else:
                    P.drain("vector")
                    P.op("vector", lambda e, sm, p=p: e.tensor_tensor(
                        PSAC[:, :], PSAC[:, :], PBUF[:, 1024 * (p % 2):1024 * (p % 2 + 1)],
                        ALU.add), "v")
                mark[("add", it, p)] = P.count("v")

        P.drain("vector")
        P.op("vector", lambda e, sm: e.tensor_tensor(
            PSAC[:, 0:512], PSAC[:, 0:512], PSAC[:, 512:1024], ALU.add), "v")
        mark[("fold", it)] = P.count("v")
        def emit_den(it=it, fbank=fbank, pvb=pvb, h=h, qt=qt):
            P.wait("tensor", "v", mark[("fold", it)])
            P.wait("tensor", "v", mark[("ffn_deq", it)])
            P.op("tensor", lambda e, sm, fbank=fbank: e.matmul(
                fbank[:, :], ONES[:, :], PSAC[:, 0:512], start=True, stop=True,
                skip_group_check=True), "t")
            mark[("denmm", it)] = P.count("t")
            db, doff = BDEN[it % 2]
            P.wait("vector", "t", mark[("denmm", it)])
            prev = mark.get(("aoevac", it - 2))
            if prev is not None:
                P.wait("vector", "v", prev)
            elif it == 0:
                P.wait("vector", "d_bc3", mark["brws_ready"])
                P.wait("vector", "v", mark[("wqq", NDC - 1)])
            P.op("vector", lambda e, sm, db=db, doff=doff, fbank=fbank: e.reciprocal(
                db[:, doff:doff + 512], fbank[:, :]), "v")
            mark[("recip", it)] = P.count("v")
            P.drain("vector")
            P.wait("vector", "t", mark[("pvmm", it)])
            prev = mark.get(("aodma", it - 2))
            if prev is not None:
                P.wait("vector", "d_ao0" if it % 2 == 0 else "d_ao1", prev)
            P.op("vector", lambda e, sm, it=it, db=db, doff=doff, pvb=pvb: e.tensor_tensor(
                AOB[:, 512 * (it % 2):512 * (it % 2 + 1)],
                pvb[:, :], db[:, doff:doff + 512], ALU.mult), "v")
            mark[("aoevac", it)] = P.count("v")
            P.wait("gpsimd", "v", mark[("aoevac", it)])
            P.dma("gpsimd", lambda e, sm, it=it, h=h, qt=qt: e.dma_start(
                out=cc_in[128 * h:128 * (h + 1), 512 * qt:512 * (qt + 1)],
                in_=AOB[:, 512 * (it % 2):512 * (it % 2 + 1)]), "d_ao0" if it % 2 == 0 else "d_ao1")
            mark[("aodma", it)] = P.count("d_ao0" if it % 2 == 0 else "d_ao1")
        pend_den[0] = emit_den

        t1b, t1off = FFT1
        gb, goff = GST
        P.wait("vector", "t", mark[("ffnmm", it)])
        prevg = mark.get(("gdma", it - 1))
        if prevg is not None:
            P.wait("vector", "d_gs", prevg)
        P.op("vector", lambda e, sm, t1b=t1b, t1off=t1off, fbank=fbank, f_tc=f_tc: e.tensor_tensor(
            t1b[:, t1off:t1off + 512], fbank[:, :],
            BQS[:, 512 * f_tc:512 * (f_tc + 1)], ALU.mult), "v")
        mark[("ffn_deq", it)] = P.count("v")
        P.wait("scalar", "v", mark[("ffn_deq", it)])
        P.op("scalar", lambda e, sm, t1b=t1b, t1off=t1off, gb=gb, goff=goff, f_oc=f_oc: e.activation(
            gb[:, goff:goff + 512], t1b[:, t1off:t1off + 512], AF.Silu,
            bias=BIASC[:, f_oc:f_oc + 1], scale=WSC[:, f_oc:f_oc + 1]), "s")
        mark[("silu", it)] = P.count("s")
        P.wait("gpsimd", "s", mark[("silu", it)])
        P.dma("gpsimd", lambda e, sm, f_oc=f_oc, f_tc=f_tc, gb=gb, goff=goff: e.dma_start(
            out=g_spill[128 * f_oc:128 * (f_oc + 1), 512 * f_tc:512 * (f_tc + 1)],
            in_=gb[:, goff:goff + 512]), "d_gs")
        mark[("gdma", it)] = P.count("d_gs")

        if it % 2 == 1:
            gb_next = it // 2 + 2
            if gb_next < 16:
                P.wait("scalar", "v", mark[("ffn_deq", it)])
                wq_marks[gb_next] = emit_wq_load(gb_next % 8)


    wo_cnt = [0]

    def emit_wo_load(ec, gate_t):
        wslot = ec % 2
        for hf in range(2):
            idx = wo_cnt[0]
            wo_cnt[0] += 1
            xs2 = idx % 2
            wsem = "d_xt0" if xs2 == 0 else "d_xt1"
            prevx = mark.get(("wocastg", idx - 2))
            if prevx is not None:
                P.wait("sync", "s", prevx)
            else:
                P.wait("sync", "s", mark["wq_cast_done"])
                P.wait("sync", "s", mark[("xcast2", NTT * 8 - 1)])
                P.wait("sync", "t", mark["sweep2_done_t"])
            P.dma("sync", lambda e, sm, ec=ec, hf=hf, xs2=xs2: e.dma_start(
                out=XTG[:, 1024 * xs2:1024 * (xs2 + 1)],
                in_=WOT[ec][:, 8 * hf:8 * (hf + 1), :].rearrange("p d c -> p (d c)")), wsem)
            P.wait("scalar", wsem)
            if hf == 0:
                P.wait("scalar", "t", gate_t)
            P.op("scalar", lambda e, sm, wslot=wslot, hf=hf, xs2=xs2: e.copy(
                WA[:, 2048 * wslot + 1024 * hf:2048 * wslot + 1024 * (hf + 1)],
                XTG[:, 1024 * xs2:1024 * (xs2 + 1)]), "s")
            mark[("wocastg", idx)] = P.count("s")
        mark[("wocast", ec)] = P.count("s")

    if pend_den[0] is not None:
        pend_den[0]()
        pend_den[0] = None

    emit_wo_load(0, mark["sweep2_done_t"])
    emit_wo_load(1, mark["sweep2_done_t"])

    P.wait("gpsimd", "d_ao0")
    P.wait("gpsimd", "d_ao1")
    P.op("gpsimd", lambda e, sm: e.collective_compute(
        "AllGather", ALU.bypass, replica_groups=[[0, 1, 2, 3], [4, 5, 6, 7]],
        ins=[cc_in[256:512, :].opt()], outs=[cc_out2.opt()]), "cc2b")
    P.dma("gpsimd", lambda e, sm: e.dma_start(out=SOFF[:, :], in_=COFF[:, :]), "d_rb")
    P.wait("gpsimd", "d_rb")
    P.wait("gpsimd", "cc2")
    holder = {}

    def _alloc_reg(e, sm):
        holder["roff"] = e.alloc_register("roff")
        return e.nop()
    P.op("gpsimd", _alloc_reg, None)
    done_p2_wait = [False]
    ec_order = [ec for ec in range(NOC) if (ec % 4) // 2 == 0] + \
               [ec for ec in range(NOC) if (ec % 4) // 2 == 1]
    for ec in ec_order:
        i, hh = divmod(ec, 4)
        part = hh // 2
        src_t = cc_out1_t if part == 0 else cc_out2_t
        if part == 1 and not done_p2_wait[0]:
            P.wait("gpsimd", "cc2b")
            done_p2_wait[0] = True
        row = 256 * i + 128 * (hh % 2)
        P.raw("gpsimd", lambda e, sm: e.reg_load(holder["roff"], SOFF[0:1, 0:1]))
        P.raw("gpsimd", lambda e, sm, row=row: e.reg_add(holder["roff"], holder["roff"], row * S))
        P.dma("gpsimd", lambda e, sm, ec=ec, src_t=src_t: e.dma_start(
            out=KT[:, TOK * ec:TOK * (ec + 1)],
            in_=bass.AP(src_t, holder["roff"], [[S, 128], [1, TOK]])), "d_rb")
    mark["rb_done"] = P.count("d_rb")

    for gi in range(NOC * NTC):
        ec, tc = divmod(gi, NTC)
        bank = pvb_t[gi % 2]
        wslot = ec % 2

        gsem = "d_g0" if gi % 2 == 0 else "d_g1"
        xsem = "d_xe0" if gi % 2 == 0 else "d_xe1"
        prev = mark.get(("ep_done", gi - 2))
        if prev is not None:
            P.wait("scalar", "v", prev)
        else:
            P.wait("scalar", "d_gs", mark[("gdma", n_it - 1)])
        P.dma("scalar", lambda e, sm, ec=ec, tc=tc, gi=gi: e.dma_start(
            out=F32A[:, 512 * (gi % 2):512 * (gi % 2 + 1)],
            in_=g_spill[128 * ec:128 * (ec + 1), 512 * tc:512 * (tc + 1)]), gsem)
        P.dma("scalar", lambda e, sm, ec=ec, tc=tc, gi=gi: e.dma_start(
            out=BRS[:, 512 * (gi % 2):512 * (gi % 2 + 1)],
            in_=XTM[128 * ec:128 * (ec + 1), 512 * tc:512 * (tc + 1)]), xsem)
        mark[("epld", gi)] = (P.count(gsem), P.count(xsem))

        for dc in range(NOC):
            if dc == 0:
                P.wait("tensor", "d_rb", mark["rb_done"])
                P.wait("tensor", "s", mark[("wocast", ec)])
                prevb = mark.get(("ep_done", gi - 2))
                if prevb is not None:
                    P.wait("tensor", "v", prevb)
            P.op("tensor", lambda e, sm, wslot=wslot, dc=dc, tc=tc, bank=bank: e.matmul(
                bank[:, :],
                WA[:, 2048 * wslot + 128 * dc:2048 * wslot + 128 * (dc + 1)],
                KT[:, TOK * dc + 512 * tc:TOK * dc + 512 * (tc + 1)],
                start=(dc == 0), stop=(dc == NOC - 1), skip_group_check=True),
                "t" if dc == NOC - 1 else None)
        mark[("womm", gi)] = P.count("t")
        if tc == NTC - 1 and ec + 2 < NOC:
            emit_wo_load(ec + 2, mark[("womm", gi)])

        P.wait("vector", "t", mark[("womm", gi)])
        P.wait("vector", gsem, mark[("epld", gi)][0])
        P.wait("vector", xsem, mark[("epld", gi)][1])
        prevo = mark.get(("outdma", gi - 2))
        if prevo is not None:
            P.wait("vector", "d_out0" if gi % 2 == 0 else "d_out1", prevo)
        P.op("vector", lambda e, sm, gi=gi, bank=bank: e.scalar_tensor_tensor(
            F32B[:, 512 * (gi % 2):512 * (gi % 2 + 1)],
            bank[:, :], 1.0,
            F32A[:, 512 * (gi % 2):512 * (gi % 2 + 1)], ALU.mult, ALU.mult), "v")
        P.drain("vector")
        P.op("vector", lambda e, sm, gi=gi: e.tensor_tensor(
            F32B[:, 512 * (gi % 2):512 * (gi % 2 + 1)],
            F32B[:, 512 * (gi % 2):512 * (gi % 2 + 1)],
            BRS[:, 512 * (gi % 2):512 * (gi % 2 + 1)], ALU.add), "v")
        mark[("ep_done", gi)] = P.count("v")
        P.wait("gpsimd", "v", mark[("ep_done", gi)])
        P.dma("gpsimd", lambda e, sm, ec=ec, tc=tc, gi=gi: e.dma_start(
            out=OUT[128 * ec:128 * (ec + 1), 512 * tc:512 * (tc + 1)],
            in_=F32B[:, 512 * (gi % 2):512 * (gi % 2 + 1)]), "d_out0" if gi % 2 == 0 else "d_out1")
        mark[("outdma", gi)] = P.count("d_out0" if gi % 2 == 0 else "d_out1")

    P.wait("gpsimd", "d_out0")
    P.wait("gpsimd", "d_out1")

    with ExitStack() as ses:
        sems = {n: ses.enter_context(nc.semaphore(n)) for n in sem_names}
        with nc.Block() as block:
            @block.sync
            def _(sync):
                P.replay("sync", sync, sems)

            @block.tensor
            def _(tensor):
                P.replay("tensor", tensor, sems)

            @block.vector
            def _(vector):
                P.replay("vector", vector, sems)

            @block.scalar
            def _(scalar):
                P.replay("scalar", scalar, sems)

            @block.gpsimd
            def _(gpsimd):
                with allow_ici_all_to_all():
                    P.replay("gpsimd", gpsimd, sems)
    es.close()
    return nc


_NC_CACHE = None


def kernel(L0_attn_x, L0_attn_wq, L0_attn_wk, L0_attn_wv, L0_attn_wo,
           L0_ffn_weight, L0_ffn_bias):
    global _NC_CACHE
    x = np.asarray(L0_attn_x, dtype=np.float32)
    wq = np.asarray(L0_attn_wq, dtype=np.float32)
    wk = np.asarray(L0_attn_wk, dtype=np.float32)
    wv = np.asarray(L0_attn_wv, dtype=np.float32)
    wo = np.asarray(L0_attn_wo, dtype=np.float32)
    fw = np.asarray(L0_ffn_weight, dtype=np.float32)
    fb = np.asarray(L0_ffn_bias, dtype=np.float32)

    C = np.ascontiguousarray
    wot = C(wo.T.reshape(NDC, 128, NOC, 128).transpose(2, 1, 0, 3))
    fwt = C(fw.T)
    fw_c = C(fw)
    in_maps = []
    for c in range(8):
        b, g = divmod(c, 4)
        sl = slice(TOK * g, TOK * (g + 1))
        es0 = slice(E_LOC * g, E_LOC * (g + 1))
        xb = x[b]
        xtr = np.ascontiguousarray(
            xb.T.reshape(NDC, 128, NTT, 512).transpose(2, 0, 1, 3))
        in_maps.append({
            "XT": xtr,
            "XTM": C(xb[sl].T),
            "XM": C(xb[sl]),
            "WQ": C(wq[es0].T),
            "WK": C(wk[es0].T),
            "WV": C(wv[es0].T),
            "WOT": wot,
            "FW": fw_c,
            "FWT": fwt,
            "FB": fb,
            "COFF": np.array([[TOK * g]], dtype=np.uint32),
        })

    if _NC_CACHE is None:
        _NC_CACHE = build_nc()
    nc = _NC_CACHE
    res = run_bass_kernel_spmd(nc, in_maps, core_ids=list(range(8)))

    out = np.empty((B, S, D), dtype=np.float32)
    for c in range(8):
        b, g = divmod(c, 4)
        out[b, TOK * g:TOK * (g + 1), :] = np.asarray(res.results[c]["OUT"]).T
    return out


# version 12 (final)
# speedup vs baseline: 1.1339x; 1.0341x over previous
import numpy as np
import ml_dtypes
from contextlib import ExitStack

import concourse.bass as bass
import concourse.mybir as mybir
from concourse.bass_utils import run_bass_kernel_spmd
from concourse.aot_env import allow_ici_all_to_all

F32 = mybir.dt.float32
BF16 = mybir.dt.bfloat16
I32 = mybir.dt.int32
U32 = mybir.dt.uint32
AF = mybir.ActivationFunctionType
ALU = mybir.AluOpType
AX = mybir.AxisListType

D = 2048
S = 4096
B = 2
H_LOC = 4
DK = 128
E_LOC = H_LOC * DK
TOK = 1024
NDC = D // 128
NTT = S // 512
NKC = S // 128
NQT = S // 512
NOC = D // 128
NTC = TOK // 512

EPS = 1e-5
SCL_ATTN = float(1.0 / np.sqrt(np.float32(DK)))
MAGIC = float(np.float32(3 * 2 ** 22))
QMAX_INV = float(np.float32(1.0 / 127.0))

ENGINES = ("sync", "tensor", "vector", "scalar", "gpsimd")


class Prog:
    def __init__(self):
        self.ops = {e: [] for e in ENGINES}
        self.cnt = {}

    def new_sem(self, name):
        self.cnt[name] = 0

    def raw(self, eng, fn):
        self.ops[eng].append(("raw", fn, None, 0))

    def op(self, eng, fn, inc_sem=None, inc_by=1):
        if inc_sem is not None:
            self.cnt[inc_sem] += inc_by
        self.ops[eng].append(("op", fn, inc_sem, inc_by))

    def dma(self, eng, fn, sem):
        self.op(eng, fn, inc_sem=sem, inc_by=16)

    def wait(self, eng, sem, val=None):
        v = self.cnt[sem] if val is None else val
        self.ops[eng].append(("wait", None, sem, v))

    def drain(self, eng):
        self.ops[eng].append(("drain", None, None, 0))

    def count(self, sem):
        return self.cnt[sem]

    def replay(self, eng_name, eng_handle, sems):
        for kind, fn, sem, val in self.ops[eng_name]:
            if kind == "wait":
                eng_handle.wait_ge(sems[sem], val)
            elif kind == "drain":
                eng_handle.drain()
            elif kind == "raw":
                fn(eng_handle, sems)
            else:
                inst = fn(eng_handle, sems)
                if sem is not None:
                    inst.then_inc(sems[sem], val)


def build_nc():
    nc = bass.Bass("TRN2", target_bir_lowering=False, debug=False)

    XT = nc.dram_tensor("XT", [NTT, NDC, 128, 512], F32, kind="ExternalInput").ap()
    XTM = nc.dram_tensor("XTM", [D, TOK], F32, kind="ExternalInput").ap()
    XM = nc.dram_tensor("XM", [TOK, D], F32, kind="ExternalInput").ap()
    WQ = nc.dram_tensor("WQ", [D, E_LOC], F32, kind="ExternalInput").ap()
    WK = nc.dram_tensor("WK", [D, E_LOC], F32, kind="ExternalInput").ap()
    WV = nc.dram_tensor("WV", [D, E_LOC], F32, kind="ExternalInput").ap()
    WOT = nc.dram_tensor("WOT", [NOC, 128, NDC, 128], F32, kind="ExternalInput").ap()
    FW = nc.dram_tensor("FW", [D, D], F32, kind="ExternalInput").ap()
    FWT = nc.dram_tensor("FWT", [D, D], F32, kind="ExternalInput").ap()
    FB = nc.dram_tensor("FB", [D], F32, kind="ExternalInput").ap()
    COFF = nc.dram_tensor("COFF", [1, 1], U32, kind="ExternalInput").ap()
    OUT = nc.dram_tensor("OUT", [D, TOK], F32, kind="ExternalOutput").ap()

    xc_spill = nc.dram_tensor("xc_spill", [NTT, 128, NDC * 512], BF16).ap()
    wq_spill = nc.dram_tensor("wq_spill", [D, D], BF16).ap()
    xq_spill = nc.dram_tensor("xq_spill", [NTC, 128, NDC * 512], BF16).ap()
    g_spill = nc.dram_tensor("g_spill", [D, TOK], F32).ap()
    m_row = nc.dram_tensor("m_row", [S], BF16).ap()
    rs_row = nc.dram_tensor("rs_row", [S], F32).ap()
    rws_row = nc.dram_tensor("rws_row", [D], F32).ap()
    qs_row = nc.dram_tensor("qs_row", [TOK], F32).ap()
    xs_row = nc.dram_tensor("xs_row", [TOK], F32).ap()
    mq_row = nc.dram_tensor("mq_row", [TOK], BF16).ap()
    cst_in = nc.dram_tensor("cst_in", [2, TOK], F32).ap()
    cst_out = nc.dram_tensor("cst_out", [8, TOK], F32).ap()
    cc_in = nc.dram_tensor("cc_in", [E_LOC, S], BF16).ap()
    cc_outs_t = [nc.dram_tensor(f"cc_outh{h}", [4 * 128, S], BF16) for h in range(H_LOC)]

    es = ExitStack()
    SB = lambda name, shape, dt: es.enter_context(nc.sbuf_tensor(name, shape, dt))

    KT = SB("KT", [128, H_LOC * S], BF16)
    QT = SB("QT", [128, H_LOC * S], BF16)
    VT = SB("VT", [128, NKC * E_LOC], BF16)
    XQR = SB("XQR", [128, 2 * NDC * 512], BF16)
    WA = SB("WA", [128, NDC * E_LOC], BF16)
    WB = SB("WB", [128, NDC * E_LOC], BF16)
    F32B = SB("F32B", [128, 1024], F32)
    QS2 = SB("QS2", [128, 2 * 1024], F32)
    F32A = QS2
    XTG = SB("XTG", [128, 2 * 1024], F32)
    BRS = SB("BRS", [128, 2 * 512], F32)
    BMT = SB("BMT", [128, 2 * 512], BF16)
    MROW2 = BMT
    NEGWK = SB("NEGWK", [1, E_LOC], BF16)
    NEGWV = SB("NEGWV", [1, E_LOC], BF16)
    NEGWQ = NEGWK
    BQS = SB("BQS", [128, TOK], F32)
    BMQ = SB("BMQ", [128, TOK], BF16)
    BRWS = SB("BRWS", [128, 2048], F32)
    PBUF = SB("PBUF", [128, 2048], BF16)
    ONES = SB("ONES", [128, 128], BF16)
    SUMC = SB("SUMC", [128, 8], F32)
    SQC = SB("SQC", [128, 8], F32)
    MMC = SB("MMC", [128, 8], F32)
    RSM = SB("RSM", [128, 8], F32)
    AXC = SB("AXC", [128, 8], F32)
    XSC = SB("XSC", [128, 8], F32)
    QSC = SB("QSC", [128, 8], F32)
    GSUM = SB("GSUM", [128, 32], F32)
    GSQ = SB("GSQ", [128, 32], F32)
    GM = SB("GM", [128, 32], F32)
    GMB = SB("GMB", [128, 32], BF16)
    GRS = SB("GRS", [128, 32], F32)
    GT0 = SB("GT0", [128, 32], F32)
    WMX = SB("WMX", [128, 16], F32)
    WSC = SB("WSC", [128, 16], F32)
    WRC = SB("WRC", [128, 16], F32)
    BIASC = SB("BIASC", [128, 16], F32)
    MQB = SB("MQB", [128, 8], BF16)
    PSAC = BMT
    AOB = BMQ
    SOFF = SB("SOFF", [1, 1], U32)

    scp = [es.enter_context(nc.psum_tensor(f"scp{i}", [128, 1024], F32)) for i in range(2)]
    pvb_t = [es.enter_context(nc.psum_tensor(f"pv{i}", [128, 512], F32)) for i in range(2)]
    ffb = [es.enter_context(nc.psum_tensor(f"ffn{i}", [128, 512], F32)) for i in range(2)]

    sem_names = [
        "t", "v", "s", "cc1", "cc2", "cc2b", "cc2c", "cc2d",
        "d_xm0", "d_st",
        "d_bc1", "d_bc2", "d_bc3",
        "d_xt0", "d_xt1", "d_xs",
        "d_sp0", "d_sp1",
        "d_xc0", "d_xc1",
        "d_fw", "d_fwt", "d_wqs", "d_xqs", "d_xtm",
        "d_q0", "d_q1", "d_qg0", "d_qg1",
        "d_brt0", "d_brt1",
        "d_wq0", "d_wq1", "d_wq2", "d_wq3",
        "d_xq0", "d_xq1",
        "d_ao0", "d_ao1",
        "d_gs", "d_rb",
        "d_g0", "d_g1", "d_xe0", "d_xe1",
        "d_out0", "d_out1",
    ]
    P = Prog()
    for n in sem_names:
        P.new_sem(n)
    mark = {}

    P.op("vector", lambda e, sm: e.memset(ONES[:, :], 1.0), "v")
    v_ones = P.count("v")

    P.dma("gpsimd", lambda e, sm: e.dma_start(
        out=BIASC[:, :], in_=FB.rearrange("(j p) -> p j", p=128),
        allow_slow_non_contiguous=True), "d_bc1")
    P.dma("gpsimd", lambda e, sm: e.dma_start(out=SOFF[:, :], in_=COFF[:, :]), "d_rb")
    P.wait("gpsimd", "d_rb")
    holder = {}

    def _alloc_reg(e, sm):
        holder["roff"] = e.alloc_register("roff")
        return e.nop()
    P.op("gpsimd", _alloc_reg, None)

    q1cnt = {}
    def _xm1_load(ih):
        i, hf = divmod(ih, 2)
        slot = ih % 2
        semn = "d_q0" if slot == 0 else "d_q1"
        eng = "scalar" if slot == 0 else "sync"
        pv = mark.get(("xm1v", ih - 2))
        if pv is not None:
            P.wait(eng, "v", pv)
            P.wait(eng, "s", mark[("xm1s", ih - 2)])
        P.dma(eng, lambda e, sm, i=i, hf=hf, slot=slot: e.dma_start(
            out=QS2[:, 1024 * slot:1024 * (slot + 1)],
            in_=XM[128 * i:128 * (i + 1), 1024 * hf:1024 * (hf + 1)]), semn)
        q1cnt[ih] = P.count(semn)

    _xm1_load(0)
    _xm1_load(1)
    for ih in range(16):
        i, hf = divmod(ih, 2)
        slot = ih % 2
        semn = "d_q0" if slot == 0 else "d_q1"
        P.wait("vector", semn, q1cnt[ih])
        P.drain("vector")
        P.op("vector", lambda e, sm, slot=slot, hf=hf: e.tensor_reduce(
            GT0[:, 2 * hf:2 * hf + 1], QS2[:, 1024 * slot:1024 * (slot + 1)], AX.X, ALU.add), "v")
        P.wait("scalar", semn, q1cnt[ih])
        if ih > 0:
            P.wait("scalar", "v", mark[("xm1v", ih - 1)])
        P.op("scalar", lambda e, sm, slot=slot: e.square(
            PBUF[:, 0:1024], QS2[:, 1024 * slot:1024 * (slot + 1)]), "s")
        mark[("xm1s", ih)] = P.count("s")
        P.wait("vector", "s", mark[("xm1s", ih)])
        P.drain("vector")
        P.op("vector", lambda e, sm, hf=hf: e.tensor_reduce(
            GT0[:, 2 * hf + 1:2 * hf + 2], PBUF[:, 0:1024], AX.X, ALU.add), "v")
        if hf == 1:
            P.drain("vector")
            P.op("vector", lambda e, sm, i=i: e.tensor_tensor(
                SUMC[:, i:i + 1], GT0[:, 0:1], GT0[:, 2:3], ALU.add), "v")
            P.drain("vector")
            P.op("vector", lambda e, sm, i=i: e.tensor_tensor(
                SQC[:, i:i + 1], GT0[:, 1:2], GT0[:, 3:4], ALU.add), "v")
        mark[("xm1v", ih)] = P.count("v")
        if ih + 2 < 16:
            _xm1_load(ih + 2)
    mark[("xmred", 7)] = mark[("xm1v", 15)]
    mark[("xmsq", 7)] = mark[("xm1s", 15)]

    P.wait("gpsimd", "v", mark[("xmred", 7)])
    P.dma("gpsimd", lambda e, sm: e.dma_start(
        out=cst_in[0:1, :].rearrange("o (j p) -> p (o j)", p=128),
        in_=SUMC[:, :], allow_slow_non_contiguous=True), "d_st")
    P.dma("gpsimd", lambda e, sm: e.dma_start(
        out=cst_in[1:2, :].rearrange("o (j p) -> p (o j)", p=128),
        in_=SQC[:, :], allow_slow_non_contiguous=True), "d_st")
    P.wait("gpsimd", "d_st")
    P.op("gpsimd", lambda e, sm: e.collective_compute(
        "AllGather", ALU.bypass, replica_groups=[[0, 1, 2, 3], [4, 5, 6, 7]],
        ins=[cst_in.opt()], outs=[cst_out.opt()]), "cc1")
    P.wait("gpsimd", "cc1")
    for i in range(4):
        P.dma("gpsimd", lambda e, sm, i=i: e.dma_start(
            out=GSUM[:, 8 * i:8 * (i + 1)],
            in_=cst_out[2 * i:2 * i + 1, :].rearrange("o (j p) -> p (o j)", p=128),
            allow_slow_non_contiguous=True), "d_st")
        P.dma("gpsimd", lambda e, sm, i=i: e.dma_start(
            out=GSQ[:, 8 * i:8 * (i + 1)],
            in_=cst_out[2 * i + 1:2 * i + 2, :].rearrange("o (j p) -> p (o j)", p=128),
            allow_slow_non_contiguous=True), "d_st")
    P.wait("vector", "d_st")
    P.op("vector", lambda e, sm: e.tensor_scalar(GM[:, :], GSUM[:, :], 1.0 / D, None, ALU.mult), "v")
    P.drain("vector")
    P.op("vector", lambda e, sm: e.tensor_mul(GT0[:, :], GM[:, :], GM[:, :]), "v")
    P.drain("vector")
    P.op("vector", lambda e, sm: e.tensor_scalar(GRS[:, :], GSQ[:, :], 1.0 / D, None, ALU.mult), "v")
    P.drain("vector")
    P.op("vector", lambda e, sm: e.tensor_sub(GRS[:, :], GRS[:, :], GT0[:, :]), "v")
    P.drain("vector")
    P.op("vector", lambda e, sm: e.tensor_scalar(GRS[:, :], GRS[:, :], EPS, None, ALU.add), "v")
    mark["grs_pre"] = P.count("v")
    P.wait("scalar", "v", mark["grs_pre"])
    P.op("scalar", lambda e, sm: e.sqrt(GT0[:, :], GRS[:, :]), "s")
    mark["grs_sqrt"] = P.count("s")
    P.wait("vector", "s", mark["grs_sqrt"])
    P.op("vector", lambda e, sm: e.reciprocal(GRS[:, :], GT0[:, :]), "v")
    P.drain("vector")
    P.op("vector", lambda e, sm: e.tensor_copy(GMB[:, :], GM[:, :]), "v")
    mark["gstats"] = P.count("v")
    P.wait("gpsimd", "v", mark["gstats"])
    P.dma("gpsimd", lambda e, sm: e.dma_start(
        out=m_row.rearrange("(j p) -> p j", p=128), in_=GMB[:, :],
        allow_slow_non_contiguous=True), "d_st")
    P.dma("gpsimd", lambda e, sm: e.dma_start(
        out=rs_row.rearrange("(j p) -> p j", p=128), in_=GRS[:, :],
        allow_slow_non_contiguous=True), "d_st")
    mark["rows_mrs"] = P.count("d_st")
    P.wait("gpsimd", "d_st")

    P.op("vector", lambda e, sm: e.tensor_scalar(MMC[:, :], SUMC[:, :], 1.0 / D, None, ALU.mult), "v")
    P.drain("vector")
    P.op("vector", lambda e, sm: e.tensor_mul(GT0[:, 0:8], MMC[:, :], MMC[:, :]), "v")
    P.drain("vector")
    P.op("vector", lambda e, sm: e.tensor_scalar(RSM[:, :], SQC[:, :], 1.0 / D, None, ALU.mult), "v")
    P.drain("vector")
    P.op("vector", lambda e, sm: e.tensor_sub(RSM[:, :], RSM[:, :], GT0[:, 0:8]), "v")
    P.drain("vector")
    P.op("vector", lambda e, sm: e.tensor_scalar(RSM[:, :], RSM[:, :], EPS, None, ALU.add), "v")
    mark["rsm_pre"] = P.count("v")
    P.wait("scalar", "v", mark["rsm_pre"])
    P.op("scalar", lambda e, sm: e.sqrt(GT0[:, 0:8], RSM[:, :]), "s")
    mark["rsm_sqrt"] = P.count("s")
    P.wait("vector", "s", mark["rsm_sqrt"])
    P.op("vector", lambda e, sm: e.reciprocal(RSM[:, :], GT0[:, 0:8]), "v")
    P.drain("vector")
    P.op("vector", lambda e, sm: e.tensor_copy(MQB[:, :], MMC[:, :]), "v")
    mark["minestats"] = P.count("v")

    QITEMS = []
    QLI = []

    def q_add(load, comp):
        QLI.append(None if load is None else sum(1 for x in QITEMS if x[0] is not None))
        QITEMS.append((load, comp))

    qcnt = {}
    qvmark = {}
    q_li_item = {}
    qemit = [0]
    qcp = [0]

    def _q_emit_load(k):
        load = QITEMS[k][0]
        if load is None:
            return
        li = QLI[k]
        q_li_item[li] = k
        slot = li % 2
        semn = "d_qg0" if slot == 0 else "d_qg1"
        prev_item = q_li_item.get(li - 2)
        if prev_item is not None:
            P.wait("gpsimd", "v", qvmark[prev_item])
        load(slot, semn)
        qcnt[k] = P.count(semn)

    def pump(n):
        for _ in range(n):
            k = qcp[0]
            if k >= len(QITEMS):
                return
            while qemit[0] < min(k + 2, len(QITEMS)):
                _q_emit_load(qemit[0])
                qemit[0] += 1
            load, comp = QITEMS[k]
            if load is not None:
                slot = QLI[k] % 2
                semn = "d_qg0" if slot == 0 else "d_qg1"
                P.wait("vector", semn, qcnt[k])
                comp(slot, semn)
            else:
                comp(None, None)
            qvmark[k] = P.count("v")
            qcp[0] += 1

    def pump_all():
        pump(len(QITEMS) - qcp[0])

    def q_xm2(i, hf):
        def load(slot, semn, i=i, hf=hf):
            if i == 0 and hf == 0:
                P.wait("gpsimd", "v", mark["minestats"])
            P.dma("gpsimd", lambda e, sm, i=i, hf=hf, slot=slot: e.dma_start(
                out=QS2[:, 1024 * slot:1024 * (slot + 1)],
                in_=XM[128 * i:128 * (i + 1), 1024 * hf:1024 * (hf + 1)]), semn)

        def comp(slot, semn, i=i, hf=hf):
            P.drain("vector")
            P.op("vector", lambda e, sm, i=i, slot=slot: e.tensor_scalar(
                QS2[:, 1024 * slot:1024 * (slot + 1)], QS2[:, 1024 * slot:1024 * (slot + 1)],
                MMC[:, i:i + 1], None, ALU.subtract), "v")
            P.drain("vector")
            P.op("vector", lambda e, sm, hf=hf, slot=slot: e.tensor_reduce(
                GT0[:, 8 + hf:9 + hf], QS2[:, 1024 * slot:1024 * (slot + 1)],
                AX.X, ALU.max, apply_absolute_value=True), "v")
            if hf == 1:
                P.drain("vector")
                P.op("vector", lambda e, sm, i=i: e.tensor_tensor(
                    AXC[:, i:i + 1], GT0[:, 8:9], GT0[:, 9:10], ALU.max), "v")
                mark[("axred", i)] = P.count("v")
        return load, comp

    def q_xscales():
        def comp(slot, semn):
            P.drain("vector")
            P.op("vector", lambda e, sm: e.tensor_mul(XSC[:, :], AXC[:, :], RSM[:, :]), "v")
            P.drain("vector")
            P.op("vector", lambda e, sm: e.tensor_scalar(
                XSC[:, :], XSC[:, :], QMAX_INV, 1e-12, ALU.mult, ALU.max), "v")
            P.drain("vector")
            P.op("vector", lambda e, sm: e.reciprocal(QSC[:, :], XSC[:, :]), "v")
            P.drain("vector")
            P.op("vector", lambda e, sm: e.tensor_mul(QSC[:, :], QSC[:, :], RSM[:, :]), "v")
            mark["qscales"] = P.count("v")
            P.wait("gpsimd", "v", mark["qscales"])
            P.dma("gpsimd", lambda e, sm: e.dma_start(
                out=qs_row.rearrange("(j p) -> p j", p=128), in_=QSC[:, :],
                allow_slow_non_contiguous=True), "d_st")
            P.dma("gpsimd", lambda e, sm: e.dma_start(
                out=xs_row.rearrange("(j p) -> p j", p=128), in_=XSC[:, :],
                allow_slow_non_contiguous=True), "d_st")
            P.dma("gpsimd", lambda e, sm: e.dma_start(
                out=mq_row.rearrange("(j p) -> p j", p=128), in_=MQB[:, :],
                allow_slow_non_contiguous=True), "d_st")
            P.wait("gpsimd", "d_st")
            P.dma("gpsimd", lambda e, sm: e.dma_start(
                out=BQS[:, :], in_=qs_row.rearrange("(o a) -> o a", o=1).broadcast_to([128, TOK])), "d_bc1")
            P.dma("gpsimd", lambda e, sm: e.dma_start(
                out=BMQ[:, :], in_=mq_row.rearrange("(o a) -> o a", o=1).broadcast_to([128, TOK])), "d_bc1")
            mark["bq_ready"] = P.count("d_bc1")
        return None, comp

    def q_fw(i, hf):
        def load(slot, semn, i=i, hf=hf):
            P.dma("gpsimd", lambda e, sm, i=i, hf=hf, slot=slot: e.dma_start(
                out=QS2[:, 1024 * slot:1024 * (slot + 1)],
                in_=FW[128 * i:128 * (i + 1), 1024 * hf:1024 * (hf + 1)]), semn)

        def comp(slot, semn, i=i, hf=hf):
            P.drain("vector")
            P.op("vector", lambda e, sm, hf=hf, slot=slot: e.tensor_reduce(
                GT0[:, 8 + hf:9 + hf], QS2[:, 1024 * slot:1024 * (slot + 1)],
                AX.X, ALU.max, apply_absolute_value=True), "v")
            if hf == 1:
                P.drain("vector")
                P.op("vector", lambda e, sm, i=i: e.tensor_tensor(
                    WMX[:, i:i + 1], GT0[:, 8:9], GT0[:, 9:10], ALU.max), "v")
                mark[("fwred", i)] = P.count("v")
        return load, comp

    def q_wscales():
        def comp(slot, semn):
            P.drain("vector")
            P.op("vector", lambda e, sm: e.tensor_scalar(
                WSC[:, :], WMX[:, :], 1e-8, QMAX_INV, ALU.max, ALU.mult), "v")
            P.drain("vector")
            P.op("vector", lambda e, sm: e.reciprocal(WRC[:, :], WSC[:, :]), "v")
            mark["wscales"] = P.count("v")
            P.wait("gpsimd", "v", mark["wscales"])
            P.dma("gpsimd", lambda e, sm: e.dma_start(
                out=rws_row.rearrange("(j p) -> p j", p=128), in_=WRC[:, :],
                allow_slow_non_contiguous=True), "d_st")
            P.wait("gpsimd", "d_st")
            P.dma("gpsimd", lambda e, sm: e.dma_start(
                out=BRWS[:, :], in_=rws_row.rearrange("(o a) -> o a", o=1).broadcast_to([128, 2048])), "d_bc3")
            mark["brws_ready"] = P.count("d_bc3")
        return None, comp

    def q_fwt(dc, h2):
        def load(slot, semn, dc=dc, h2=h2):
            P.dma("gpsimd", lambda e, sm, dc=dc, h2=h2, slot=slot: e.dma_start(
                out=QS2[:, 1024 * slot:1024 * (slot + 1)],
                in_=FWT[128 * dc:128 * (dc + 1), 1024 * h2:1024 * (h2 + 1)]), semn)

        def comp(slot, semn, dc=dc, h2=h2):
            if dc == 0 and h2 == 0:
                P.wait("vector", "d_bc3", mark["brws_ready"])
            if h2 == 0:
                prevsp = mark.get(("wqsp", dc - 1))
                if prevsp is not None:
                    P.wait("vector", "d_wqs", prevsp)
            P.drain("vector")
            P.op("vector", lambda e, sm, h2=h2, slot=slot: e.tensor_tensor(
                F32B[:, :], QS2[:, 1024 * slot:1024 * (slot + 1)],
                BRWS[:, 1024 * h2:1024 * (h2 + 1)], ALU.mult), "v")
            P.drain("vector")
            P.op("vector", lambda e, sm: e.tensor_scalar(
                F32B[:, :], F32B[:, :], MAGIC, MAGIC, ALU.add, ALU.subtract), "v")
            P.drain("vector")
            P.op("vector", lambda e, sm: e.tensor_copy(
                F32B[:, :].bitcast(I32), F32B[:, :].bitcast(I32)), "v")
            P.drain("vector")
            P.op("vector", lambda e, sm, h2=h2: e.tensor_copy(
                PBUF[:, 1024 * h2:1024 * (h2 + 1)], F32B[:, :]), "v")
            if h2 == 1:
                mark[("wqq", dc)] = P.count("v")
                P.wait("gpsimd", "v", mark[("wqq", dc)])
                P.dma("gpsimd", lambda e, sm, dc=dc: e.dma_start(
                    out=wq_spill[128 * dc:128 * (dc + 1), :], in_=PBUF[:, :]), "d_wqs")
                mark[("wqsp", dc)] = P.count("d_wqs")
        return load, comp

    def q_xq(dc):
        def load(slot, semn, dc=dc):
            P.dma("gpsimd", lambda e, sm, dc=dc, slot=slot: e.dma_start(
                out=QS2[:, 1024 * slot:1024 * (slot + 1)],
                in_=XTM[128 * dc:128 * (dc + 1), :]), semn)

        def comp(slot, semn, dc=dc):
            if dc == 0:
                P.wait("vector", "d_bc1", mark["bq_ready"])
                P.wait("vector", "d_wqs", mark[("wqsp", NDC - 1)])
            else:
                prevsp = mark.get(("xqsp", dc - 1))
                if prevsp is not None:
                    P.wait("vector", "d_xqs", prevsp)
            P.drain("vector")
            P.op("vector", lambda e, sm, slot=slot: e.tensor_tensor(
                F32B[:, :], QS2[:, 1024 * slot:1024 * (slot + 1)], BMQ[:, :], ALU.subtract), "v")
            P.drain("vector")
            P.op("vector", lambda e, sm: e.tensor_mul(F32B[:, :], F32B[:, :], BQS[:, :]), "v")
            P.drain("vector")
            P.op("vector", lambda e, sm: e.tensor_scalar(
                F32B[:, :], F32B[:, :], MAGIC, MAGIC, ALU.add, ALU.subtract), "v")
            P.drain("vector")
            P.op("vector", lambda e, sm: e.tensor_copy(
                F32B[:, :].bitcast(I32), F32B[:, :].bitcast(I32)), "v")
            P.drain("vector")
            P.op("vector", lambda e, sm: e.tensor_copy(PBUF[:, 0:1024], F32B[:, :]), "v")
            mark[("xqq", dc)] = P.count("v")
            P.wait("gpsimd", "v", mark[("xqq", dc)])
            for tc in range(NTC):
                P.dma("gpsimd", lambda e, sm, dc=dc, tc=tc: e.dma_start(
                    out=xq_spill[tc, :, 512 * dc:512 * (dc + 1)],
                    in_=PBUF[:, 512 * tc:512 * (tc + 1)]), "d_xqs")
            mark[("xqsp", dc)] = P.count("d_xqs")
        return load, comp

    def q_xqfin():
        def comp(slot, semn):
            mark["xq_done"] = P.count("d_xqs")
            P.wait("gpsimd", "v", mark[("xqq", NDC - 1)])
            P.dma("gpsimd", lambda e, sm: e.dma_start(
                out=BQS[:, :], in_=xs_row.rearrange("(o a) -> o a", o=1).broadcast_to([128, TOK])), "d_bc2")
            mark["bxs_ready"] = P.count("d_bc2")
        return None, comp

    for i in range(8):
        for hf in range(2):
            q_add(*q_xm2(i, hf))
    q_add(*q_xscales())
    for i in range(NOC):
        for hf in range(2):
            q_add(*q_fw(i, hf))
    q_add(*q_wscales())
    for dc in range(NDC):
        for h2 in range(2):
            q_add(*q_fwt(dc, h2))
    for dc in range(NDC):
        q_add(*q_xq(dc))
    q_add(*q_xqfin())

    for wi, (srcw, dst) in enumerate([(WK, WA), (WV, WB)]):
        for g in range(8):
            k = wi * 8 + g
            slot = k % 2
            semn = "d_xt0" if slot == 0 else "d_xt1"
            prev = mark.get(("wcast", k - 2))
            if prev is not None:
                P.wait("sync", "s", prev)
            P.dma("sync", lambda e, sm, srcw=srcw, g=g, slot=slot: e.dma_start(
                out=XTG[:, 1024 * slot:1024 * (slot + 1)].rearrange("p (d c) -> p d c", d=2),
                in_=srcw[256 * g:256 * (g + 1), :].rearrange("(d p) c -> p d c", p=128)), semn)
            P.wait("scalar", semn)
            P.op("scalar", lambda e, sm, dst=dst, g=g, slot=slot: e.copy(
                dst[:, 1024 * g:1024 * (g + 1)], XTG[:, 1024 * slot:1024 * (slot + 1)]), "s")
            mark[("wcast", k)] = P.count("s")
    mark["wkv_ready_s"] = P.count("s")

    P.wait("tensor", "v", v_ones)
    for _ in range(64):
        P.op("tensor", lambda e, sm: e.matmul(
            ffb[0][:, 0:128], ONES[:, :], ONES[:, 0:128], start=True, stop=True,
            skip_group_check=True), None)

    P.wait("tensor", "s", mark["wkv_ready_s"])
    for wi, (wsrc, bank) in enumerate([(WA, pvb_t[0]), (WB, pvb_t[1])]):
        for dc in range(NDC):
            P.op("tensor", lambda e, sm, wsrc=wsrc, bank=bank, dc=dc: e.matmul(
                bank[:, :], ONES[:, :], wsrc[:, 512 * dc:512 * (dc + 1)],
                start=(dc == 0), stop=(dc == NDC - 1)), "t" if dc == NDC - 1 else None)
        mark[("wsum", wi)] = P.count("t")
    for wi, (bank, dstrow) in enumerate([(pvb_t[0], NEGWK), (pvb_t[1], NEGWV)]):
        P.wait("vector", "t", mark[("wsum", wi)])
        P.op("vector", lambda e, sm, bank=bank, dstrow=dstrow: e.tensor_scalar(
            dstrow[0:1, :], bank[0:1, :], -1.0, None, ALU.mult), "v")
    mark["negw_kv"] = P.count("v")

    def xcbuf(tt):
        return XQR, (tt % 2) * 8192

    def sw1_produce(tt):
        slot = tt % 2
        brt = "d_brt0" if slot == 0 else "d_brt1"
        P.wait("sync", "d_st", mark["rows_mrs"])
        prev = mark.get(("kevac_g", (tt - 2) * H_LOC + H_LOC - 1))
        if prev is not None:
            P.wait("sync", "v", prev)
        prev_mm = mark.get(("sw1_mm", tt - 2))
        if prev_mm is not None:
            P.wait("sync", "t", prev_mm)
        P.dma("sync", lambda e, sm, tt=tt, slot=slot: e.dma_start(
            out=BRS[:, 512 * slot:512 * (slot + 1)],
            in_=rs_row[512 * tt:512 * (tt + 1)].rearrange("(o a) -> o a", o=1).broadcast_to([128, 512])), brt)
        P.dma("sync", lambda e, sm, tt=tt, slot=slot: e.dma_start(
            out=MROW2[0:1, 512 * slot:512 * (slot + 1)],
            in_=m_row[512 * tt:512 * (tt + 1)].rearrange("(o a) -> o a", o=1)), brt)
        mark[("brt", tt)] = P.count(brt)

        xcb, xcoff = xcbuf(tt)
        for g in range(8):
            kg = tt * 8 + g
            xslot = kg % 2
            if xslot == 0:
                xsem = "d_xt0"
                prev = mark.get(("xcast", kg - 2))
                if prev is not None:
                    P.wait("sync", "s", prev)
                elif kg < 2:
                    P.wait("sync", "s", mark["wkv_ready_s"])
                if g == 0:
                    prev_mm = mark.get(("sw1_mm", tt - 2))
                    if prev_mm is not None:
                        P.wait("sync", "t", prev_mm)
                P.dma("sync", lambda e, sm, tt=tt, g=g, xslot=xslot: e.dma_start(
                    out=XTG[:, 1024 * xslot:1024 * (xslot + 1)].rearrange("p (d c) -> p d c", d=2),
                    in_=XT[tt, 2 * g:2 * (g + 1)].rearrange("d p c -> p d c")), xsem)
            else:
                xsem = "d_xs"
                prev = mark.get(("xcast", kg - 2))
                P.wait("scalar", "s", prev if prev is not None else mark["wkv_ready_s"])
                P.dma("scalar", lambda e, sm, tt=tt, g=g, xslot=xslot: e.dma_start(
                    out=XTG[:, 1024 * xslot:1024 * (xslot + 1)].rearrange("p (d c) -> p d c", d=2),
                    in_=XT[tt, 2 * g:2 * (g + 1)].rearrange("d p c -> p d c")), xsem)
            P.wait("scalar", xsem)
            P.op("scalar", lambda e, sm, xcb=xcb, xcoff=xcoff, g=g, xslot=xslot: e.copy(
                xcb[:, xcoff + 1024 * g:xcoff + 1024 * (g + 1)],
                XTG[:, 1024 * xslot:1024 * (xslot + 1)]), "s")
            mark[("xcast", kg)] = P.count("s")

    def sw1_compute(tt):
        slot = tt % 2
        xcb, xcoff = xcbuf(tt)
        if tt == 0:
            P.wait("tensor", "s", mark["wkv_ready_s"])
            P.wait("tensor", "v", mark["negw_kv"])
        for ec in range(H_LOC):
            bank = pvb_t[ec % 2]
            gk = tt * H_LOC + ec
            prev = mark.get(("kevac_g", gk - 2))
            if prev is not None:
                P.wait("tensor", "v", prev)
            if ec == 0:
                P.wait("tensor", "d_brt0" if slot == 0 else "d_brt1", mark[("brt", tt)])
            for dc in range(NDC):
                if dc == 0 and ec == 0:
                    P.wait("tensor", "s", mark[("xcast", tt * 8 + 7)])
                P.op("tensor", lambda e, sm, bank=bank, ec=ec, dc=dc, xcb=xcb, xcoff=xcoff: e.matmul(
                    bank[:, :],
                    WA[:, 512 * dc + 128 * ec:512 * dc + 128 * (ec + 1)],
                    xcb[:, xcoff + 512 * dc:xcoff + 512 * (dc + 1)],
                    start=(dc == 0), stop=False), None)
            P.op("tensor", lambda e, sm, bank=bank, ec=ec, slot=slot: e.matmul(
                bank[:, :], NEGWK[0:1, 128 * ec:128 * (ec + 1)],
                MROW2[0:1, 512 * slot:512 * (slot + 1)], start=False, stop=True), "t")
            mark[("kmm", tt, ec)] = P.count("t")
            P.wait("vector", "t", mark[("kmm", tt, ec)])
            if ec == 0:
                P.wait("vector", "d_brt0" if slot == 0 else "d_brt1", mark[("brt", tt)])
            P.op("vector", lambda e, sm, bank=bank, ec=ec, tt=tt, slot=slot: e.tensor_tensor(
                KT[:, S * ec + 512 * tt:S * ec + 512 * (tt + 1)],
                bank[:, :], BRS[:, 512 * slot:512 * (slot + 1)], ALU.mult), "v")
            mark[("kevac_g", tt * H_LOC + ec)] = P.count("v")

        for tc in range(4):
            bank = ffb[tc % 2]
            gk = tt * 4 + tc
            prev = mark.get(("vevac_g", gk - 2))
            if prev is not None:
                P.wait("tensor", "s", prev)
            for dc in range(NDC):
                P.op("tensor", lambda e, sm, bank=bank, tc=tc, dc=dc, xcb=xcb, xcoff=xcoff: e.matmul(
                    bank[:, :],
                    xcb[:, xcoff + 512 * dc + 128 * tc:xcoff + 512 * dc + 128 * (tc + 1)],
                    WB[:, 512 * dc:512 * (dc + 1)],
                    start=(dc == 0), stop=False), None)
            P.op("tensor", lambda e, sm, bank=bank, slot=slot, tc=tc: e.matmul(
                bank[:, :], MROW2[0:1, 512 * slot + 128 * tc:512 * slot + 128 * (tc + 1)],
                NEGWV[0:1, 0:E_LOC], start=False, stop=True), "t")
            mark[("vmm", tt, tc)] = P.count("t")
            ck = tt * 4 + tc
            P.wait("scalar", "t", mark[("vmm", tt, tc)])
            if ck == 0:
                P.wait("scalar", "v", mark["gstats"])
            P.op("scalar", lambda e, sm, bank=bank, ck=ck: e.activation(
                VT[:, 512 * ck:512 * (ck + 1)], bank[:, :], AF.Identity,
                bias=0.0, scale=GRS[:, ck:ck + 1]), "s")
            mark[("vevac_g", ck)] = P.count("s")
        mark[("sw1_mm", tt)] = P.count("t")
        pump(9)

    sw1_produce(0)
    for tt in range(NTT):
        if tt + 1 < NTT:
            sw1_produce(tt + 1)
        sw1_compute(tt)

    P.wait("sync", "t", mark[("sw1_mm", NTT - 1)])
    for g in range(8):
        slot = g % 2
        semn = "d_xt0" if slot == 0 else "d_xt1"
        prev = mark.get(("wqcast", g - 2))
        if prev is not None:
            P.wait("sync", "s", prev)
        elif g < 2:
            P.wait("sync", "s", mark[("xcast", NTT * 8 - 1)])
        P.dma("sync", lambda e, sm, g=g, slot=slot: e.dma_start(
            out=XTG[:, 1024 * slot:1024 * (slot + 1)].rearrange("p (d c) -> p d c", d=2),
            in_=WQ[256 * g:256 * (g + 1), :].rearrange("(d p) c -> p d c", p=128)), semn)
        P.wait("scalar", semn)
        if g == 0:
            P.wait("scalar", "t", mark[("sw1_mm", NTT - 1)])
        P.op("scalar", lambda e, sm, g=g, slot=slot: e.copy(
            WA[:, 1024 * g:1024 * (g + 1)], XTG[:, 1024 * slot:1024 * (slot + 1)]), "s")
        mark[("wqcast", g)] = P.count("s")
    mark["wq_cast_done"] = P.count("s")

    P.wait("tensor", "s", mark["wq_cast_done"])
    for dc in range(NDC):
        P.op("tensor", lambda e, sm, dc=dc: e.matmul(
            ffb[0][:, :], ONES[:, :], WA[:, 512 * dc:512 * (dc + 1)],
            start=(dc == 0), stop=(dc == NDC - 1)), "t" if dc == NDC - 1 else None)
    mark["wsumq"] = P.count("t")
    P.wait("vector", "t", mark["wsumq"])
    P.op("vector", lambda e, sm: e.tensor_scalar(
        NEGWQ[0:1, :], ffb[0][0:1, :], -1.0, None, ALU.mult), "v")
    mark["negwq"] = P.count("v")

    def sw2_produce(tt):
        xcb, xcoff = xcbuf(tt)
        for g in range(8):
            kg = tt * 8 + g
            xslot = kg % 2
            if xslot == 0:
                xsem = "d_xt0"
                prev = mark.get(("xcast2", kg - 2))
                if prev is not None:
                    P.wait("sync", "s", prev)
                elif kg < 2:
                    P.wait("sync", "s", mark["wq_cast_done"])
                if g == 0:
                    prev_mm = mark.get(("qmm_all", tt - 2))
                    if prev_mm is not None:
                        P.wait("sync", "t", prev_mm)
                    else:
                        P.wait("sync", "t", mark[("sw1_mm", NTT - 1)])
                P.dma("sync", lambda e, sm, tt=tt, g=g, xslot=xslot: e.dma_start(
                    out=XTG[:, 1024 * xslot:1024 * (xslot + 1)].rearrange("p (d c) -> p d c", d=2),
                    in_=XT[tt, 2 * g:2 * (g + 1)].rearrange("d p c -> p d c")), xsem)
            else:
                xsem = "d_xs"
                prev = mark.get(("xcast2", kg - 2))
                P.wait("scalar", "s", prev if prev is not None else mark["wq_cast_done"])
                P.dma("scalar", lambda e, sm, tt=tt, g=g, xslot=xslot: e.dma_start(
                    out=XTG[:, 1024 * xslot:1024 * (xslot + 1)].rearrange("p (d c) -> p d c", d=2),
                    in_=XT[tt, 2 * g:2 * (g + 1)].rearrange("d p c -> p d c")), xsem)
            P.wait("scalar", xsem)
            P.op("scalar", lambda e, sm, xcb=xcb, xcoff=xcoff, g=g, xslot=xslot: e.copy(
                xcb[:, xcoff + 1024 * g:xcoff + 1024 * (g + 1)],
                XTG[:, 1024 * xslot:1024 * (xslot + 1)]), "s")
            mark[("xcast2", kg)] = P.count("s")
        mark[("xcld", tt)] = P.count("s")
        slot = tt % 2
        brt = "d_brt0" if slot == 0 else "d_brt1"
        prev = mark.get(("qevac_g", (tt - 2) * H_LOC + 3))
        if prev is not None:
            P.wait("sync", "v", prev)
        else:
            P.wait("sync", "v", mark[("kevac_g", NTT * H_LOC - 1)])
        P.dma("sync", lambda e, sm, tt=tt, slot=slot: e.dma_start(
            out=BRS[:, 512 * slot:512 * (slot + 1)],
            in_=rs_row[512 * tt:512 * (tt + 1)].rearrange("(o a) -> o a", o=1).broadcast_to([128, 512])), brt)
        P.dma("sync", lambda e, sm, tt=tt, slot=slot: e.dma_start(
            out=MROW2[0:1, 512 * slot:512 * (slot + 1)],
            in_=m_row[512 * tt:512 * (tt + 1)].rearrange("(o a) -> o a", o=1)), brt)
        mark[("brt2", tt)] = P.count(brt)

    def sw2_compute(tt):
        xcb, xcoff = xcbuf(tt)
        slot = tt % 2
        for ec in range(H_LOC):
            bank = pvb_t[ec % 2]
            gk = tt * H_LOC + ec
            prev = mark.get(("qevac_g", gk - 2))
            if prev is not None:
                P.wait("tensor", "v", prev)
            elif gk < 2:
                P.wait("tensor", "v", mark["negwq"])
            if ec == 0:
                P.wait("tensor", "d_brt0" if slot == 0 else "d_brt1", mark[("brt2", tt)])
            for dc in range(NDC):
                if dc == 0 and ec == 0:
                    P.wait("tensor", "s", mark[("xcld", tt)])
                P.op("tensor", lambda e, sm, bank=bank, ec=ec, dc=dc, xcb=xcb, xcoff=xcoff: e.matmul(
                    bank[:, :],
                    WA[:, 512 * dc + 128 * ec:512 * dc + 128 * (ec + 1)],
                    xcb[:, xcoff + 512 * dc:xcoff + 512 * (dc + 1)],
                    start=(dc == 0), stop=False), None)
            P.op("tensor", lambda e, sm, bank=bank, ec=ec, slot=slot: e.matmul(
                bank[:, :], NEGWQ[0:1, 128 * ec:128 * (ec + 1)],
                MROW2[0:1, 512 * slot:512 * (slot + 1)], start=False, stop=True), "t")
            mark[("qmm", tt, ec)] = P.count("t")
            slot = tt % 2
            brt = "d_brt0" if slot == 0 else "d_brt1"
            P.wait("vector", "t", mark[("qmm", tt, ec)])
            if ec == 0:
                P.wait("vector", brt, mark[("brt2", tt)])
            P.op("vector", lambda e, sm, bank=bank, ec=ec, tt=tt, slot=slot: e.tensor_tensor(
                QT[:, S * ec + 512 * tt:S * ec + 512 * (tt + 1)],
                bank[:, :], BRS[:, 512 * slot:512 * (slot + 1)], ALU.mult), "v")
            mark[("qevac_g", tt * H_LOC + ec)] = P.count("v")
        mark[("qmm_all", tt)] = P.count("t")
        pump(6)

    sw2_produce(0)
    sw2_produce(1)
    for tt in range(NTT):
        sw2_compute(tt)
        if tt + 2 < NTT:
            sw2_produce(tt + 2)
    pump_all()
    mark["sweep2_done_t"] = P.count("t")
    mark["sweep2_done_v"] = P.count("v")

    def emit_xq_load(tc, slot):
        semn = "d_xq0" if slot == 0 else "d_xq1"
        P.dma("sync", lambda e, sm, tc=tc, slot=slot: e.dma_start(
            out=XQR[:, 8192 * slot:8192 * (slot + 1)], in_=xq_spill[tc]), semn)
        return P.count(semn)

    def emit_wq_load(blk):
        slot = blk % 2
        semn = f"d_wq{slot}"
        P.dma("scalar", lambda e, sm, blk=blk, slot=slot: e.dma_start(
            out=WB[:, 4096 * slot:4096 * (slot + 1)].rearrange("p (d c) -> p d c", d=16),
            in_=wq_spill[:, 256 * blk:256 * (blk + 1)].rearrange("(d p) c -> p d c", p=128)), semn)
        return P.count(semn)

    P.wait("sync", "d_xqs", mark["xq_done"])
    P.wait("sync", "t", mark["sweep2_done_t"])
    mark[("xqld", 0)] = emit_xq_load(0, 0)
    P.wait("scalar", "d_wqs", mark[("wqsp", NDC - 1)])
    P.wait("scalar", "t", mark["sweep2_done_t"])
    wq_marks = {}
    for blk in range(2):
        wq_marks[blk] = emit_wq_load(blk)

    BDEN = [(BRWS, 0), (BRWS, 512)]
    FFT1 = (BRWS, 1024)
    GST = (BRWS, 1536)

    n_it = H_LOC * NQT
    NP = NKC // 2

    P.wait("scalar", "v", mark[("xqq", NDC - 1)])
    P.wait("scalar", "d_wqs", mark[("wqsp", NDC - 1)])
    P.wait("scalar", "d_xqs", mark["xq_done"])
    P.wait("scalar", "d_bc1", mark["bq_ready"])
    P.wait("scalar", "d_bc2", mark["bxs_ready"])
    P.wait("vector", "d_bc2", mark["bxs_ready"])
    P.wait("vector", "d_xqs", mark["xq_done"])
    P.wait("tensor", "v", v_ones)

    AG_SEMS = ["cc2", "cc2b", "cc2c", "cc2d"]

    def emit_ag(hh):
        P.wait("gpsimd", "d_ao0")
        P.wait("gpsimd", "d_ao1")
        P.op("gpsimd", lambda e, sm, hh=hh: e.collective_compute(
            "AllGather", ALU.bypass, replica_groups=[[0, 1, 2, 3], [4, 5, 6, 7]],
            ins=[cc_in[128 * hh:128 * (hh + 1), :].opt()],
            outs=[cc_outs_t[hh].ap().opt()]), AG_SEMS[hh])

    rb_hh_waited = set()

    def emit_rb(i, hh):
        if hh not in rb_hh_waited:
            rb_hh_waited.add(hh)
            P.wait("gpsimd", AG_SEMS[hh])
        P.wait("gpsimd", "t", mark[("sc", 8 * i + 7, NP - 1)])
        P.raw("gpsimd", lambda e, sm: e.reg_load(holder["roff"], SOFF[0:1, 0:1]))
        P.raw("gpsimd", lambda e, sm, i=i: e.reg_add(holder["roff"], holder["roff"], 128 * i * S))
        ecd = 4 * i + hh
        P.dma("gpsimd", lambda e, sm, ecd=ecd, hh=hh: e.dma_start(
            out=KT[:, TOK * ecd:TOK * (ecd + 1)],
            in_=bass.AP(cc_outs_t[hh], holder["roff"], [[S, 128], [1, TOK]])), "d_rb")

    rb_sched = {}
    rb_post = []
    for i in range(4):
        for hh in range(4):
            r = max(8 * i + 8, 8 * (hh + 1) + 2)
            if r < n_it:
                rb_sched.setdefault(r, []).append((i, hh))
            else:
                rb_post.append((i, hh))

    pend_den = [None]
    for it in range(n_it):
        h, qt = divmod(it, NQT)
        f_tc, f_oc = divmod(it, NOC)
        fbank = ffb[it % 2]
        pvb = pvb_t[it % 2]

        if it == 8:
            mark[("xqld", 1)] = emit_xq_load(1, 1)

        ffn_dc = 0
        for p in range(NP + 2):
            if p >= 2:
                pp = p - 2
                for half in range(2):
                    kcp = 2 * pp + half
                    if kcp == 0:
                        prev = mark.get(("aoevac", it - 2))
                        if prev is not None:
                            P.wait("tensor", "v", prev)
                        elif it < 2:
                            P.wait("tensor", "v", mark["sweep2_done_v"])
                    P.wait("tensor", "s", mark[("exp", it, pp)])
                    P.op("tensor", lambda e, sm, h=h, kcp=kcp, pvb=pvb, pp=pp, half=half: e.matmul(
                        pvb[:, :],
                        VT[:, 512 * kcp + 128 * h:512 * kcp + 128 * (h + 1)],
                        PBUF[:, 1024 * (pp % 2) + 512 * half:1024 * (pp % 2) + 512 * (half + 1)],
                        start=(kcp == 0), stop=(kcp == NKC - 1), skip_group_check=True),
                        "t" if kcp == NKC - 1 else None)
                    if kcp == NKC - 1:
                        mark[("pvmm", it)] = P.count("t")
                if ffn_dc < NDC:
                    dc = ffn_dc
                    ffn_dc += 1
                    gb = it // 2
                    wslot2 = gb % 2
                    if dc == 0:
                        P.wait("tensor", f"d_wq{wslot2}", wq_marks[gb])
                        P.wait("tensor", "d_xq0" if f_tc == 0 else "d_xq1", mark[("xqld", f_tc)])
                        prevf = mark.get(("recip", it - 2))
                        if prevf is not None:
                            P.wait("tensor", "v", prevf)
                        elif it < 2:
                            P.wait("tensor", "s", mark[("vevac_g", NTT * 4 - 1)])
                    P.op("tensor", lambda e, sm, f_oc=f_oc, f_tc=f_tc, dc=dc, fbank=fbank, wslot2=wslot2: e.matmul(
                        fbank[:, :],
                        WB[:, 4096 * wslot2 + 256 * dc + 128 * (f_oc % 2):4096 * wslot2 + 256 * dc + 128 * (f_oc % 2) + 128],
                        XQR[:, 8192 * f_tc + 512 * dc:8192 * f_tc + 512 * (dc + 1)],
                        start=(dc == 0), stop=(dc == NDC - 1), skip_group_check=True),
                        "t" if dc == NDC - 1 else None)
                    if dc == NDC - 1:
                        mark[("ffnmm", it)] = P.count("t")
            if p < NP:
                scb = scp[p % 2]
                prev = mark.get(("exp", it, p - 2))
                if prev is None:
                    prev = mark.get(("exp", it - 1, NP - 2 + p))
                if prev is not None:
                    P.wait("tensor", "s", prev)
                for half in range(2):
                    kc = 2 * p + half
                    P.op("tensor", lambda e, sm, h=h, qt=qt, kc=kc, scb=scb, half=half: e.matmul(
                        scb[:, 512 * half:512 * (half + 1)],
                        KT[:, S * h + 128 * kc:S * h + 128 * (kc + 1)],
                        QT[:, S * h + 512 * qt:S * h + 512 * (qt + 1)],
                        start=True, stop=True, skip_group_check=True), "t")
                mark[("sc", it, p)] = P.count("t")
                if p == 0 and pend_den[0] is not None:
                    pend_den[0]()
                    pend_den[0] = None
                    if it in (8, 16, 24):
                        emit_ag(it // 8 - 1)
                    for ri, rhh in rb_sched.get(it, []):
                        emit_rb(ri, rhh)
                P.wait("scalar", "t", mark[("sc", it, p)])
                prevv = mark.get(("add", it, p - 2)) if p >= 2 else mark.get(("add", it - 1, NP - 1))
                if prevv is not None:
                    P.wait("scalar", "v", prevv)
                P.op("scalar", lambda e, sm, p=p, scb=scb: e.activation(
                    PBUF[:, 1024 * (p % 2):1024 * (p % 2 + 1)], scb[:, :],
                    AF.Exp, scale=SCL_ATTN), "s")
                mark[("exp", it, p)] = P.count("s")
                P.wait("vector", "s", mark[("exp", it, p)])
                if p == 0:
                    prevd = mark.get(("denmm", it - 1))
                    if prevd is not None:
                        P.wait("vector", "t", prevd)
                    P.op("vector", lambda e, sm, p=p: e.tensor_copy(
                        PSAC[:, :], PBUF[:, 1024 * (p % 2):1024 * (p % 2 + 1)]), "v")
                else:
                    P.drain("vector")
                    P.op("vector", lambda e, sm, p=p: e.tensor_tensor(
                        PSAC[:, :], PSAC[:, :], PBUF[:, 1024 * (p % 2):1024 * (p % 2 + 1)],
                        ALU.add), "v")
                mark[("add", it, p)] = P.count("v")

        P.drain("vector")
        P.op("vector", lambda e, sm: e.tensor_tensor(
            PSAC[:, 0:512], PSAC[:, 0:512], PSAC[:, 512:1024], ALU.add), "v")
        mark[("fold", it)] = P.count("v")
        def emit_den(it=it, fbank=fbank, pvb=pvb, h=h, qt=qt):
            P.wait("tensor", "v", mark[("fold", it)])
            P.wait("tensor", "v", mark[("ffn_deq", it)])
            P.op("tensor", lambda e, sm, fbank=fbank: e.matmul(
                fbank[:, :], ONES[:, :], PSAC[:, 0:512], start=True, stop=True,
                skip_group_check=True), "t")
            mark[("denmm", it)] = P.count("t")
            db, doff = BDEN[it % 2]
            P.wait("vector", "t", mark[("denmm", it)])
            prev = mark.get(("aoevac", it - 2))
            if prev is not None:
                P.wait("vector", "v", prev)
            elif it == 0:
                P.wait("vector", "d_bc3", mark["brws_ready"])
                P.wait("vector", "v", mark[("wqq", NDC - 1)])
            P.op("vector", lambda e, sm, db=db, doff=doff, fbank=fbank: e.reciprocal(
                db[:, doff:doff + 512], fbank[:, :]), "v")
            mark[("recip", it)] = P.count("v")
            P.drain("vector")
            P.wait("vector", "t", mark[("pvmm", it)])
            prev = mark.get(("aodma", it - 2))
            if prev is not None:
                P.wait("vector", "d_ao0" if it % 2 == 0 else "d_ao1", prev)
            P.op("vector", lambda e, sm, it=it, db=db, doff=doff, pvb=pvb: e.tensor_tensor(
                AOB[:, 512 * (it % 2):512 * (it % 2 + 1)],
                pvb[:, :], db[:, doff:doff + 512], ALU.mult), "v")
            mark[("aoevac", it)] = P.count("v")
            P.wait("gpsimd", "v", mark[("aoevac", it)])
            P.dma("gpsimd", lambda e, sm, it=it, h=h, qt=qt: e.dma_start(
                out=cc_in[128 * h:128 * (h + 1), 512 * qt:512 * (qt + 1)],
                in_=AOB[:, 512 * (it % 2):512 * (it % 2 + 1)]), "d_ao0" if it % 2 == 0 else "d_ao1")
            mark[("aodma", it)] = P.count("d_ao0" if it % 2 == 0 else "d_ao1")
        pend_den[0] = emit_den

        t1b, t1off = FFT1
        gb, goff = GST
        P.wait("vector", "t", mark[("ffnmm", it)])
        prevg = mark.get(("gdma", it - 1))
        if prevg is not None:
            P.wait("vector", "d_gs", prevg)
        P.op("vector", lambda e, sm, t1b=t1b, t1off=t1off, fbank=fbank, f_tc=f_tc: e.tensor_tensor(
            t1b[:, t1off:t1off + 512], fbank[:, :],
            BQS[:, 512 * f_tc:512 * (f_tc + 1)], ALU.mult), "v")
        mark[("ffn_deq", it)] = P.count("v")
        P.wait("scalar", "v", mark[("ffn_deq", it)])
        P.op("scalar", lambda e, sm, t1b=t1b, t1off=t1off, gb=gb, goff=goff, f_oc=f_oc: e.activation(
            gb[:, goff:goff + 512], t1b[:, t1off:t1off + 512], AF.Silu,
            bias=BIASC[:, f_oc:f_oc + 1], scale=WSC[:, f_oc:f_oc + 1]), "s")
        mark[("silu", it)] = P.count("s")
        P.wait("gpsimd", "s", mark[("silu", it)])
        P.dma("gpsimd", lambda e, sm, f_oc=f_oc, f_tc=f_tc, gb=gb, goff=goff: e.dma_start(
            out=g_spill[128 * f_oc:128 * (f_oc + 1), 512 * f_tc:512 * (f_tc + 1)],
            in_=gb[:, goff:goff + 512]), "d_gs")
        mark[("gdma", it)] = P.count("d_gs")

        if it % 2 == 1:
            gb_next = it // 2 + 2
            if gb_next < 16:
                P.wait("scalar", "v", mark[("ffn_deq", it)])
                wq_marks[gb_next] = emit_wq_load(gb_next % 8)


    wo_cnt = [0]

    def emit_wo_load(ec, gate_t):
        wslot = ec % 2
        for hf in range(2):
            idx = wo_cnt[0]
            wo_cnt[0] += 1
            xs2 = idx % 2
            wsem = "d_xt0" if xs2 == 0 else "d_xt1"
            prevx = mark.get(("wocastg", idx - 2))
            if prevx is not None:
                P.wait("sync", "s", prevx)
            else:
                P.wait("sync", "s", mark["wq_cast_done"])
                P.wait("sync", "s", mark[("xcast2", NTT * 8 - 1)])
                P.wait("sync", "t", mark["sweep2_done_t"])
            P.dma("sync", lambda e, sm, ec=ec, hf=hf, xs2=xs2: e.dma_start(
                out=XTG[:, 1024 * xs2:1024 * (xs2 + 1)],
                in_=WOT[ec][:, 8 * hf:8 * (hf + 1), :].rearrange("p d c -> p (d c)")), wsem)
            P.wait("scalar", wsem)
            if hf == 0:
                P.wait("scalar", "t", gate_t)
            P.op("scalar", lambda e, sm, wslot=wslot, hf=hf, xs2=xs2: e.copy(
                WA[:, 2048 * wslot + 1024 * hf:2048 * wslot + 1024 * (hf + 1)],
                XTG[:, 1024 * xs2:1024 * (xs2 + 1)]), "s")
            mark[("wocastg", idx)] = P.count("s")
        mark[("wocast", ec)] = P.count("s")

    if pend_den[0] is not None:
        pend_den[0]()
        pend_den[0] = None

    emit_wo_load(0, mark["sweep2_done_t"])
    emit_wo_load(1, mark["sweep2_done_t"])

    emit_ag(3)
    for ri, rhh in rb_post:
        emit_rb(ri, rhh)
    mark["rb_done"] = P.count("d_rb")

    for gi in range(NOC * NTC):
        ec, tc = divmod(gi, NTC)
        bank = pvb_t[gi % 2]
        wslot = ec % 2

        gsem = "d_g0" if gi % 2 == 0 else "d_g1"
        xsem = "d_xe0" if gi % 2 == 0 else "d_xe1"
        prev = mark.get(("ep_done", gi - 2))
        if prev is not None:
            P.wait("scalar", "v", prev)
        else:
            P.wait("scalar", "d_gs", mark[("gdma", n_it - 1)])
        P.dma("scalar", lambda e, sm, ec=ec, tc=tc, gi=gi: e.dma_start(
            out=F32A[:, 512 * (gi % 2):512 * (gi % 2 + 1)],
            in_=g_spill[128 * ec:128 * (ec + 1), 512 * tc:512 * (tc + 1)]), gsem)
        P.dma("scalar", lambda e, sm, ec=ec, tc=tc, gi=gi: e.dma_start(
            out=BRS[:, 512 * (gi % 2):512 * (gi % 2 + 1)],
            in_=XTM[128 * ec:128 * (ec + 1), 512 * tc:512 * (tc + 1)]), xsem)
        mark[("epld", gi)] = (P.count(gsem), P.count(xsem))

        for dc in range(NOC):
            if dc == 0:
                P.wait("tensor", "d_rb", mark["rb_done"])
                P.wait("tensor", "s", mark[("wocast", ec)])
                prevb = mark.get(("ep_done", gi - 2))
                if prevb is not None:
                    P.wait("tensor", "v", prevb)
            P.op("tensor", lambda e, sm, wslot=wslot, dc=dc, tc=tc, bank=bank: e.matmul(
                bank[:, :],
                WA[:, 2048 * wslot + 128 * dc:2048 * wslot + 128 * (dc + 1)],
                KT[:, TOK * dc + 512 * tc:TOK * dc + 512 * (tc + 1)],
                start=(dc == 0), stop=(dc == NOC - 1), skip_group_check=True),
                "t" if dc == NOC - 1 else None)
        mark[("womm", gi)] = P.count("t")
        if tc == NTC - 1 and ec + 2 < NOC:
            emit_wo_load(ec + 2, mark[("womm", gi)])

        P.wait("vector", "t", mark[("womm", gi)])
        P.wait("vector", gsem, mark[("epld", gi)][0])
        P.wait("vector", xsem, mark[("epld", gi)][1])
        prevo = mark.get(("outdma", gi - 2))
        if prevo is not None:
            P.wait("vector", "d_out0" if gi % 2 == 0 else "d_out1", prevo)
        P.op("vector", lambda e, sm, gi=gi, bank=bank: e.scalar_tensor_tensor(
            F32B[:, 512 * (gi % 2):512 * (gi % 2 + 1)],
            bank[:, :], 1.0,
            F32A[:, 512 * (gi % 2):512 * (gi % 2 + 1)], ALU.mult, ALU.mult), "v")
        P.drain("vector")
        P.op("vector", lambda e, sm, gi=gi: e.tensor_tensor(
            F32B[:, 512 * (gi % 2):512 * (gi % 2 + 1)],
            F32B[:, 512 * (gi % 2):512 * (gi % 2 + 1)],
            BRS[:, 512 * (gi % 2):512 * (gi % 2 + 1)], ALU.add), "v")
        mark[("ep_done", gi)] = P.count("v")
        P.wait("gpsimd", "v", mark[("ep_done", gi)])
        P.dma("gpsimd", lambda e, sm, ec=ec, tc=tc, gi=gi: e.dma_start(
            out=OUT[128 * ec:128 * (ec + 1), 512 * tc:512 * (tc + 1)],
            in_=F32B[:, 512 * (gi % 2):512 * (gi % 2 + 1)]), "d_out0" if gi % 2 == 0 else "d_out1")
        mark[("outdma", gi)] = P.count("d_out0" if gi % 2 == 0 else "d_out1")

    P.wait("gpsimd", "d_out0")
    P.wait("gpsimd", "d_out1")

    with ExitStack() as ses:
        sems = {n: ses.enter_context(nc.semaphore(n)) for n in sem_names}
        with nc.Block() as block:
            @block.sync
            def _(sync):
                P.replay("sync", sync, sems)

            @block.tensor
            def _(tensor):
                P.replay("tensor", tensor, sems)

            @block.vector
            def _(vector):
                P.replay("vector", vector, sems)

            @block.scalar
            def _(scalar):
                P.replay("scalar", scalar, sems)

            @block.gpsimd
            def _(gpsimd):
                with allow_ici_all_to_all():
                    P.replay("gpsimd", gpsimd, sems)
    es.close()
    return nc


_NC_CACHE = None


def kernel(L0_attn_x, L0_attn_wq, L0_attn_wk, L0_attn_wv, L0_attn_wo,
           L0_ffn_weight, L0_ffn_bias):
    global _NC_CACHE
    x = np.asarray(L0_attn_x, dtype=np.float32)
    wq = np.asarray(L0_attn_wq, dtype=np.float32)
    wk = np.asarray(L0_attn_wk, dtype=np.float32)
    wv = np.asarray(L0_attn_wv, dtype=np.float32)
    wo = np.asarray(L0_attn_wo, dtype=np.float32)
    fw = np.asarray(L0_ffn_weight, dtype=np.float32)
    fb = np.asarray(L0_ffn_bias, dtype=np.float32)

    C = np.ascontiguousarray
    wot = C(wo.T.reshape(NDC, 128, NOC, 128).transpose(2, 1, 0, 3))
    fwt = C(fw.T)
    fw_c = C(fw)
    in_maps = []
    for c in range(8):
        b, g = divmod(c, 4)
        sl = slice(TOK * g, TOK * (g + 1))
        es0 = slice(E_LOC * g, E_LOC * (g + 1))
        xb = x[b]
        xtr = np.ascontiguousarray(
            xb.T.reshape(NDC, 128, NTT, 512).transpose(2, 0, 1, 3))
        in_maps.append({
            "XT": xtr,
            "XTM": C(xb[sl].T),
            "XM": C(xb[sl]),
            "WQ": C(wq[es0].T),
            "WK": C(wk[es0].T),
            "WV": C(wv[es0].T),
            "WOT": wot,
            "FW": fw_c,
            "FWT": fwt,
            "FB": fb,
            "COFF": np.array([[TOK * g]], dtype=np.uint32),
        })

    if _NC_CACHE is None:
        _NC_CACHE = build_nc()
    nc = _NC_CACHE
    res = run_bass_kernel_spmd(nc, in_maps, core_ids=list(range(8)))

    out = np.empty((B, S, D), dtype=np.float32)
    for c in range(8):
        b, g = divmod(c, 4)
        out[b, TOK * g:TOK * (g + 1), :] = np.asarray(res.results[c]["OUT"]).T
    return out
